# version 12
# baseline (speedup 1.0000x reference)
import numpy as np
import ml_dtypes
import concourse.bass as bass
import concourse.mybir as mybir
from concourse.bass import IndirectOffsetOnAxis
from concourse.bass_utils import run_bass_kernel_spmd

F32 = mybir.dt.float32
BF16 = mybir.dt.bfloat16
I32 = mybir.dt.int32
ALU = mybir.AluOpType
AF = mybir.ActivationFunctionType
AXX = mybir.AxisListType.X

D = 1024
T = 32768
TC = 4096
NT = 32
GT = 256
NB = 8
KVL = 2048
NH = 16
E = 64
K8 = 8
CAP = 512
LCAP = 128
NCORE = 8
EPS = 1e-5
YBROWS = TC * K8
PAY = 65536.0
DUMMY_ASG = 40000.0
SCRATCH = 16384


class _DryI:
    def then_inc(self, sem, n=1):
        return self


_DRYI = _DryI()


class _DryEng:
    def __getattr__(self, name):
        def f(*a, **k):
            return _DRYI
        return f


class Ctx:

    def __init__(self, sems, engines, real_name):
        self.sems = sems
        self.eng = engines
        self.real = real_name
        self.cnt = {k: 0 for k in sems}
        self.marks = {}

    def inc(self, ins, sem, n=1):
        self.cnt[sem] += n
        if ins is not _DRYI and ins is not None:
            ins.then_inc(self.sems[sem], n)
        return self.cnt[sem]

    def wait(self, engname, sem, val=None):
        if val is None:
            val = self.cnt[sem]
        if val <= 0:
            return
        e = self.eng[engname]
        if not isinstance(e, _DryEng):
            e.wait_ge(self.sems[sem], val)

    def mark(self, tag, sem):
        self.marks[tag] = self.cnt[sem]

    def at(self, tag):
        return self.marks.get(tag, 0)


def build_nc():
    nc = bass.Bass("TRN2", target_bir_lowering=False,
                   dynamic_dma_scratch_size=SCRATCH)

    d_x_own = nc.dram_tensor("x_own", [TC, D], F32, kind="ExternalInput").ap()
    d_x_fm = nc.dram_tensor("x_fm", [D, T], F32, kind="ExternalInput").ap()
    d_k = nc.dram_tensor("kv_k", [KVL, D], F32, kind="ExternalInput").ap()
    d_v = nc.dram_tensor("kv_v", [KVL, D], F32, kind="ExternalInput").ap()
    d_wq = nc.dram_tensor("w_q", [D, D], F32, kind="ExternalInput").ap()
    d_wk = nc.dram_tensor("w_k", [D, D], F32, kind="ExternalInput").ap()
    d_wv = nc.dram_tensor("w_v", [D, D], F32, kind="ExternalInput").ap()
    d_wo = nc.dram_tensor("w_o", [D, D], F32, kind="ExternalInput").ap()
    d_bq = nc.dram_tensor("b_q", [1, D], F32, kind="ExternalInput").ap()
    d_bk = nc.dram_tensor("b_k", [1, D], F32, kind="ExternalInput").ap()
    d_bv = nc.dram_tensor("b_v", [1, D], F32, kind="ExternalInput").ap()
    d_bo = nc.dram_tensor("b_o", [1, D], F32, kind="ExternalInput").ap()
    d_gw = nc.dram_tensor("gate_w", [D, E], F32, kind="ExternalInput").ap()
    d_sgw = nc.dram_tensor("sgw", [D, 1], F32, kind="ExternalInput").ap()
    d_sguw = nc.dram_tensor("sguw", [D, 4096], F32, kind="ExternalInput").ap()
    d_sdw = nc.dram_tensor("sdw", [2048, D], F32, kind="ExternalInput").ap()
    d_egu = nc.dram_tensor("egu", [E, D, D], F32, kind="ExternalInput").ap()
    d_edw = nc.dram_tensor("edw", [E, 512, D], F32, kind="ExternalInput").ap()
    d_cm = nc.dram_tensor("cmask", [128, 8], F32, kind="ExternalInput").ap()

    d_out = nc.dram_tensor("out", [TC, D], F32, kind="ExternalOutput").ap()

    d_xn = nc.dram_tensor("xn_scr", [TC, D], BF16, kind="Internal").ap()
    d_xnfm = nc.dram_tensor("xnfm_scr", [D, TC], BF16, kind="Internal").ap()
    d_yb = nc.dram_tensor("yb_scr", [YBROWS, D], BF16, kind="Internal").ap()
    d_s = nc.dram_tensor("s_scr", [TC, D], F32, kind="Internal").ap()
    d_ysh = nc.dram_tensor("ysh_scr", [TC, D], F32, kind="Internal").ap()
    d_qs = nc.dram_tensor("q_scr", [NB * 1024, 512], BF16, kind="Internal").ap()
    d_sg = nc.dram_tensor("sg_scr", [NB, 512], F32, kind="Internal").ap()

    c_id_f = nc.inline_tensor(np.eye(128, dtype=np.float32), name="c_id_f")
    c_id_b = nc.inline_tensor(np.eye(128).astype(ml_dtypes.bfloat16), name="c_id_b")
    c_lt = nc.inline_tensor(np.triu(np.ones((128, 128), np.float32), 1), name="c_lt")
    c_onc_f = nc.inline_tensor(np.ones((128, 1), np.float32), name="c_onc_f")
    c_onc_b = nc.inline_tensor(np.ones((128, 1)).astype(ml_dtypes.bfloat16), name="c_onc_b")
    c_onr_f = nc.inline_tensor(np.ones((1, 128), np.float32), name="c_onr_f")
    c_onr_b = nc.inline_tensor(np.ones((1, 128)).astype(ml_dtypes.bfloat16), name="c_onr_b")
    c_iota = nc.inline_tensor(np.arange(128, dtype=np.float32).reshape(128, 1),
                              name="c_iota")

    class Alloc:
        def __init__(self, base):
            self.cur = base

        def al(self, name, shape, dtype, align=64):
            itm = {F32: 4, BF16: 2, I32: 4}[dtype]
            per_part = int(np.prod(shape[1:])) * itm
            off = (self.cur + align - 1) // align * align
            self.cur = off + per_part
            return nc.alloc_sbuf_tensor_at(name, list(shape), dtype, offset=off)

    BASE = (nc.sbuf_base + 63) // 64 * 64
    TOP = nc.sbuf_top
    P = Alloc(BASE)
    s_id_f = P.al("s_id_f", [128, 128], F32)
    s_id_b = P.al("s_id_b", [128, 128], BF16)
    s_lt = P.al("s_lt", [128, 128], F32)
    s_onc_f = P.al("s_onc_f", [128, 1], F32)
    s_onc_b = P.al("s_onc_b", [128, 1], BF16)
    s_onr_f = P.al("s_onr_f", [1, 128], F32)
    s_onr_b = P.al("s_onr_b", [1, 128], BF16)
    s_iota = P.al("s_iota", [128, 1], F32)
    s_gb = [P.al(f"s_gb{i}", [128, E], BF16) for i in range(8)]
    s_gcol = P.al("s_gcol", [1, E], BF16)
    s_bq = P.al("s_bq", [128, 8], F32)
    s_bk = P.al("s_bk", [128, 8], F32)
    s_bo_b = P.al("s_bo_b", [1, D], BF16)
    s_bv_b = P.al("s_bv_b", [1, D], BF16)
    s_w8 = P.al("s_w8", [128, NT * K8], F32)
    s_sgt = P.al("s_sgt", [128, NT], F32)
    s_zero = P.al("s_zero", [128, D], BF16)
    s_cm = P.al("s_cm", [128, 8], F32)
    s_gof_run = P.al("s_gof_run", [1, E], F32)
    s_cst = P.al("s_cst", [128, 8], F32)
    s_tokid = P.al("s_tokid", [128, E], I32)
    s_asg = P.al("s_asg", [128, E], I32)
    s_vld = P.al("s_vld", [128, E], F32)

    ARENA0 = P.cur

    a = Alloc(ARENA0)
    p_xf = [a.al(f"p_xf{i}", [128, 8 * 512], F32) for i in range(2)]
    p_xb = [a.al(f"p_xb{i}", [128, 8 * 512], BF16) for i in range(2)]
    p_nmu = [a.al(f"p_nmu{i}", [1, 512], BF16) for i in range(2)]
    p_t8 = a.al("p_t8", [128, 4 * K8], F32)
    p_M = a.al("p_M", [128, 4 * E], F32)
    p_cnt = a.al("p_cnt", [1, 4 * E], F32)
    p_ust = a.al("p_ust", [128, NT * E], F32)
    p_gst = a.al("p_gst", [1, NT * E], F32)
    p_xo = [a.al(f"p_xo{i}", [128, D], F32) for i in range(2)]
    p_xm = a.al("p_xm", [128, D], F32)
    p_xn = [a.al(f"p_xn{i}", [128, D], BF16) for i in range(2)]
    p_fmst = a.al("p_fmst", [128, D], BF16)
    p_bn = a.al("p_bn", [128, 12], F32)
    p_mv = a.al("p_mv", [128, 2], F32)
    p_r = a.al("p_r", [128, 1], F32)
    p_lnt = a.al("p_lnt", [128, 1], F32)
    p_t8j = a.al("p_t8j", [128, K8], F32)
    p_Mj = a.al("p_Mj", [128, E], F32)
    p_c1 = a.al("p_c1", [128, E], F32)
    p_sel = a.al("p_sel", [128, E], F32)
    p_oh = a.al("p_oh", [128, E], F32)
    p_Kc = a.al("p_Kc", [128, E], F32)
    p_v2 = a.al("p_v2", [128, E], F32)
    p_tl8 = a.al("p_tl8", [128, 1], F32)
    p_kp8 = a.al("p_kp8", [128, K8], F32)
    p_e8 = a.al("p_e8", [128, K8], F32)
    p_nm8 = a.al("p_nm8", [128, 1], F32)
    p_d8 = a.al("p_d8", [128, 1], F32)
    p_di = a.al("p_di", [128, 1], F32)
    p_vm = a.al("p_vm", [64, TC], F32)
    p_a16 = a.al("p_a16", [64, 128], F32)
    p_der = a.al("p_der", [128, E], F32)
    p_der2 = a.al("p_der2", [128, E], F32)
    P0_top = a.cur

    a = Alloc(ARENA0)
    A_kh = [a.al(f"A_kh{i}", [128, KVL], BF16) for i in range(8)]
    A_vhx = a.al("A_vhx", [128, 16 * NH * 65], BF16)
    s3_base = a.cur
    A_wb = [a.al(f"A_wb{i}", [128, D], BF16) for i in range(8)]
    A_cvt = a.al("A_cvt", [128, D], F32)
    A_kvb = a.al("A_kvb", [128, D], BF16)
    A_kvfm = [a.al(f"A_kvfm{i}", [128, KVL], BF16) for i in range(8)]
    A12_top = a.cur
    a = Alloc(s3_base + 8 * D * 2)
    A_q = [a.al(f"A_q{i}", [128, 512], BF16) for i in range(8)]
    A_P = [a.al(f"A_P{i}", [128, 1024], BF16) for i in range(2)]
    A_ofm = [a.al(f"A_ofm{i}", [128, 512], BF16) for i in range(8)]
    A_otm = [a.al(f"A_otm{i}", [128, D], BF16) for i in range(4)]
    A_stm = a.al("A_stm", [128, D], F32)
    A_drow = a.al("A_drow", [1, 512], F32)
    A_dstack = a.al("A_dstack", [16, 512], F32)
    A_dinv = a.al("A_dinv", [128, NH], F32)
    A_wcvt = a.al("A_wcvt", [128, D], F32)
    A3_top = a.cur

    a = Alloc(ARENA0)
    B_wgu = [a.al(f"B_wgu{i}", [128, 4096], BF16) for i in range(8)]
    B_wd = [a.al(f"B_wd{i}", [128, D], BF16) for i in range(16)]
    B_sgwc = a.al("B_sgwc", [128, 8], BF16)
    B_cvt = a.al("B_cvt", [128, D], F32)
    B_xn = [a.al(f"B_xn{i}", [128, 512], BF16) for i in range(8)]
    B_act = [a.al(f"B_act{i}", [128, 512], BF16) for i in range(16)]
    B_sh = a.al("B_sh", [128, 512], F32)
    B_y = a.al("B_y", [128, D], F32)
    B_sgr = a.al("B_sgr", [1, 512], F32)
    B_top = a.cur

    a = Alloc(ARENA0)
    C_wguf = [a.al(f"C_wguf{i}", [128, 2 * D], F32) for i in range(2)]
    C_wgu = [[a.al(f"C_wgu{p}_{i}", [128, D], BF16) for i in range(8)] for p in range(2)]
    C_wdf = [a.al(f"C_wdf{i}", [128, 2 * D], F32) for i in range(2)]
    C_wd = [[a.al(f"C_wd{p}_{i}", [128, D], BF16) for i in range(4)] for p in range(2)]
    C_eb = [a.al(f"C_eb{p}", [128, D], BF16) for p in range(2)]
    C_ebt = [a.al(f"C_ebt{p}", [128, D], BF16) for p in range(2)]
    C_sh = [a.al(f"C_sh{p}", [128, 512], F32) for p in range(2)]
    C_act = [a.al(f"C_act{p}", [128, 512], BF16) for p in range(2)]
    C_actT = [a.al(f"C_actT{p}", [128, 512], BF16) for p in range(2)]
    C_ebo = [a.al(f"C_ebo{p}", [128, D], BF16) for p in range(2)]
    C_g = [a.al(f"C_g{p}", [128, D], BF16) for p in range(4)]
    C_y = a.al("C_y", [128, D], F32)
    C_x = [a.al(f"C_x{p}", [128, D], F32) for p in range(2)]
    C_sv = [a.al(f"C_sv{p}", [128, D], F32) for p in range(2)]
    C_ysh = [a.al(f"C_ysh{p}", [128, D], F32) for p in range(2)]
    C_ffn = [a.al(f"C_ffn{p}", [128, D], F32) for p in range(2)]
    C_gg = [a.al(f"C_gg{p}", [128, D], F32) for p in range(2)]
    C_out = [a.al(f"C_out{p}", [128, D], F32) for p in range(2)]
    C_top = a.cur
    a = Alloc(ARENA0)
    C_sgs = a.al("C_sgs", [1, TC], F32)

    for nm, top in [("p0", P0_top), ("A12", A12_top), ("A3", A3_top),
                    ("B", B_top), ("C", C_top)]:
        assert top <= TOP, f"{nm} arena exceeds sbuf top: {top} > {TOP}"

    pbA = nc.alloc_psum_tensor("pbA", [128, 1024], F32)
    pbB = nc.alloc_psum_tensor("pbB", [128, 1024], F32)
    pbC = nc.alloc_psum_tensor("pbC", [128, 512], F32)
    pbD = nc.alloc_psum_tensor("pbD", [128, 512], F32)
    pb = [pbA[:, 0:512], pbA[:, 512:1024], pbB[:, 0:512], pbB[:, 512:1024],
          pbC[:], pbD[:]]
    pbh = [nc.alloc_psum_tensor(f"pbh{i}", [128, 1024], BF16) for i in range(2)]

    sems = {n: nc.alloc_semaphore(n)
            for n in ["sx", "so", "pe", "va", "sa", "gd", "gs", "s1",
                      "d0", "d1", "d2", "d3", "g2", "g3", "e0", "e1",
                      "c0", "c1"]}

    CONSTS = [EPS, -0.5, 0.125, -12.0]
    for ci, cv in enumerate(CONSTS):
        nc.const_aps.register(F32, cv, s_cst[:, ci:ci + 1])


    def emit_all(ctx):
        SP = ctx.eng["sp"]; PE = ctx.eng["pe"]; V = ctx.eng["dve"]
        A_ = ctx.eng["act"]; G = ctx.eng["pool"]

        def sp_dma(out, in_, **kw):
            return ctx.inc(SP.dma_start(out=out, in_=in_, **kw), "sx", 16)

        def gp_dma(out, in_, sem="gs", **kw):
            return ctx.inc(G.dma_start(out=out, in_=in_, **kw), sem, 16)

        def sq_dma(sem, out, in_, **kw):
            return ctx.inc(SP.dma_start(out=out, in_=in_, **kw), sem, 16)

        def vop(fn, *deps):
            ctx.wait("dve", "va")
            for s, v in deps:
                ctx.wait("dve", s, v)
            return ctx.inc(fn(V), "va")

        def aop(fn, *deps):
            ctx.wait("act", "sa")
            for s, v in deps:
                ctx.wait("act", s, v)
            return ctx.inc(fn(A_), "sa")

        def pe_w(sem, v=None):
            ctx.wait("pe", sem, v)

        def pe_i(ins):
            return ctx.inc(ins, "pe")

        def bank_w(tag, sem="va"):
            pe_w(sem, ctx.at("bk_" + tag))

        def bank_m(tag, sem="va"):
            ctx.mark("bk_" + tag, sem)

        def barrier():
            for en in ("sp", "pe", "dve", "act", "pool"):
                for s in ("pe", "va", "sa", "sx", "gd", "gs", "so", "s1",
                          "d0", "d1", "d2", "d3", "g2", "g3", "e0", "e1",
                          "c0", "c1"):
                    ctx.wait(en, s)

        def load_wmat(dram, dest, stage):
            for i in range(8):
                ctx.wait("sp", "va", ctx.at("wstage_free"))
                sq_dma("d2", stage[:], dram[i * 128:(i + 1) * 128, :])
                wsx = ctx.cnt["d2"]
                vop(lambda v, i=i: v.tensor_copy(out=dest[i][:], in_=stage[:]),
                    ("d2", wsx))
                ctx.mark("wstage_free", "va")

        sp_dma(s_id_f[:], c_id_f.ap())
        sp_dma(s_id_b[:], c_id_b.ap())
        sp_dma(s_lt[:], c_lt.ap())
        sp_dma(s_onc_f[:], c_onc_f.ap())
        sp_dma(s_onc_b[:], c_onc_b.ap())
        sp_dma(s_onr_f[:], c_onr_f.ap())
        sp_dma(s_onr_b[:], c_onr_b.ap())
        sp_dma(s_iota[:], c_iota.ap())
        sp_dma(s_cm[:], d_cm[:])
        sq_dma("d3", p_xf[0][:, 0:512].rearrange("p (c e) -> p c e", c=8),
               d_gw.rearrange("(c p) e -> p c e", p=128))
        ld0 = ctx.cnt["d3"]
        for i in range(8):
            vop(lambda v, i=i: v.tensor_copy(out=s_gb[i][:],
                                             in_=p_xf[0][:, i * 64:(i + 1) * 64]),
                ("d3", ld0))
        sp_dma(s_bq[:].rearrange("p (c o) -> p c o", o=1),
               d_bq.rearrange("o (c p) -> p c o", p=128),
               allow_slow_non_contiguous=True)
        sp_dma(s_bk[:].rearrange("p (c o) -> p c o", o=1),
               d_bk.rearrange("o (c p) -> p c o", p=128),
               allow_slow_non_contiguous=True)
        sq_dma("g3", p_xf[1][0:1, 0:1024], d_bo[:])
        sq_dma("g3", p_xf[1][0:1, 1024:2048], d_bv[:])
        ld1 = ctx.cnt["g3"]
        vop(lambda v: v.tensor_copy(out=s_bo_b[:], in_=p_xf[1][0:1, 0:1024]), ("g3", ld1))
        vop(lambda v: v.tensor_copy(out=s_bv_b[:], in_=p_xf[1][0:1, 1024:2048]), ("g3", ld1))
        vop(lambda v: v.memset(s_zero[:], 0.0))
        vop(lambda v: v.memset(s_gof_run[:], 0.0))
        vop(lambda v: v.memset(p_ust[:], 0.0))
        vop(lambda v: v.memset(p_gst[:], 0.0))
        for ci, cv in enumerate(CONSTS):
            vop(lambda v, ci=ci, cv=cv: v.memset(s_cst[:, ci:ci + 1], cv))
        z0 = ctx.cnt["va"]
        ctx.wait("sp", "va", z0)
        for rr in range(YBROWS // 128):
            sq_dma("g2", d_yb[rr * 128:(rr + 1) * 128, :], s_zero[:])
        ctx.mark("ybzero", "g2")
        pe_w("va", z0)
        for i in range(8):
            pe_i(PE.matmul(pb[0][0:1, 0:E], s_onc_b[:], s_gb[i][:],
                           start=(i == 0), stop=(i == 7)))
        vop(lambda v: v.tensor_copy(out=s_gcol[:], in_=pb[0][0:1, 0:E]),
            ("pe", ctx.cnt["pe"]))
        barrier()

        for b in range(GT // 4):
            pbu = b % 2
            ctx.wait("sp", "pe", ctx.at(f"xfm_used_{b - 2}"))
            sq_dma(f"d{b % 2}", p_xf[pbu][:].rearrange("p (c t) -> p c t", c=8),
                   d_x_fm.rearrange("(c p) t -> p c t", p=128)
                   [:, :, b * 512:(b + 1) * 512])
            ldb = ctx.cnt[f"d{b % 2}"]
            for i in range(4):
                vop(lambda v, i=i: v.tensor_copy(
                    out=p_xb[pbu][:, i * 512:(i + 1) * 512],
                    in_=p_xf[pbu][:, i * 512:(i + 1) * 512]), (f"d{b % 2}", ldb))
            for i in range(4, 8):
                aop(lambda s, i=i: s.copy(
                    out=p_xb[pbu][:, i * 512:(i + 1) * 512],
                    in_=p_xf[pbu][:, i * 512:(i + 1) * 512]), (f"d{b % 2}", ldb))
            cvt_v = ctx.cnt["va"]; cvt_a = ctx.cnt["sa"]
            pe_w("va", cvt_v); pe_w("sa", cvt_a)
            bank_w("mu")
            for i in range(8):
                pe_i(PE.matmul(pb[5][0:1, 0:512], s_onc_b[:],
                               p_xb[pbu][:, i * 512:(i + 1) * 512],
                               start=(i == 0), stop=(i == 7)))
            vop(lambda v: v.tensor_scalar_mul(p_nmu[pbu][:], pb[5][0:1, 0:512],
                                              -1.0 / D), ("pe", ctx.cnt["pe"]))
            bank_m("mu")
            nmu_v = ctx.cnt["va"]

            ub = pb[0] if b % 2 == 0 else pb[3]
            pe_w("va", nmu_v)
            bank_w(f"u{b % 2}")
            for ti in range(4):
                for i in range(8):
                    pe_i(PE.matmul(
                        ub[0:128, ti * E:(ti + 1) * E],
                        p_xb[pbu][:, i * 512 + ti * 128:i * 512 + (ti + 1) * 128],
                        s_gb[i][:], start=(i == 0), stop=False))
                pe_i(PE.matmul(ub[0:128, ti * E:(ti + 1) * E],
                               p_nmu[pbu][0:1, ti * 128:(ti + 1) * 128],
                               s_gcol[:], start=False, stop=True))
            u_pe = ctx.cnt["pe"]
            for ti in range(4):
                vop(lambda v, ub=ub, ti=ti: v.max(
                    out=p_t8[:, ti * K8:(ti + 1) * K8],
                    in_=ub[0:128, ti * E:(ti + 1) * E]),
                    *([("pe", u_pe)] if ti == 0 else []))
                vop(lambda v, ub=ub, ti=ti: v.tensor_scalar(
                    out=p_M[:, ti * E:(ti + 1) * E], in0=ub[0:128, ti * E:(ti + 1) * E],
                    scalar1=p_t8[:, ti * K8 + 7:ti * K8 + 8],
                    scalar2=None, op0=ALU.is_ge))
                g = 4 * b + ti
                j = g // 8
                r = g % 8
                vop(lambda v, ub=ub, ti=ti, j=j, r=r: v.scalar_tensor_tensor(
                    out=p_ust[:, j * E:(j + 1) * E],
                    in0=ub[0:128, ti * E:(ti + 1) * E],
                    scalar=s_cm[:, r:r + 1], in1=p_ust[:, j * E:(j + 1) * E],
                    op0=ALU.mult, op1=ALU.add))
            m_v = ctx.cnt["va"]
            bank_m(f"u{b % 2}")
            pe_w("va", m_v)
            bank_w("cnt")
            for ti in range(4):
                pe_i(PE.matmul(pb[1][0:1, ti * E:(ti + 1) * E], s_onc_f[:],
                               p_M[:, ti * E:(ti + 1) * E], start=True, stop=True))
            cnt_pe = ctx.cnt["pe"]
            vop(lambda v: v.tensor_copy(out=p_cnt[:], in_=pb[1][0:1, 0:4 * E]),
                ("pe", cnt_pe))
            bank_m("cnt")
            for ti in range(4):
                g = 4 * b + ti
                j = g // 8
                r = g % 8
                vop(lambda v, j=j, r=r: v.scalar_tensor_tensor(
                    out=p_gst[0:1, j * E:(j + 1) * E], in0=s_gof_run[:],
                    scalar=s_cm[0:1, r:r + 1], in1=p_gst[0:1, j * E:(j + 1) * E],
                    op0=ALU.mult, op1=ALU.add))
                vop(lambda v, ti=ti: v.tensor_add(
                    out=s_gof_run[:], in0=s_gof_run[:],
                    in1=p_cnt[0:1, ti * E:(ti + 1) * E]))
            ctx.mark(f"xfm_used_{b}", "pe")
        barrier()

        for j in range(NT):
            pj = j % 2
            ctx.wait("sp", "va", ctx.at(f"oc_xo_{j - 2}"))
            sq_dma(f"d{pj}", p_xo[pj][:], d_x_own[j * 128:(j + 1) * 128, :])
            xo_sx = ctx.cnt[f"d{pj}"]
            vop(lambda v: v.bn_stats(out=p_bn[:, 0:6], in_=p_xo[pj][:, 0:512]),
                (f"d{pj}", xo_sx))
            vop(lambda v: v.bn_stats(out=p_bn[:, 6:12], in_=p_xo[pj][:, 512:1024]))
            vop(lambda v: v.bn_aggr(out=p_mv[:],
                                    in_=p_bn[:].rearrange("p (a b) -> p a b", a=2)),
                ("sa", ctx.at("mv_free")))
            mv_v = ctx.cnt["va"]
            aop(lambda s: s.activation(out=p_lnt[:], in_=p_mv[:, 1:2], func=AF.Ln,
                                       bias=EPS), ("va", mv_v))
            aop(lambda s: s.activation(out=p_r[:], in_=p_lnt[:], func=AF.Exp,
                                       scale=-0.5))
            r_sa = ctx.cnt["sa"]
            ctx.mark("mv_free", "sa")
            vop(lambda v: v.tensor_scalar(out=p_xm[:], in0=p_xo[pj][:],
                                          scalar1=p_mv[:, 0:1], scalar2=None,
                                          op0=ALU.subtract))
            vop(lambda v: v.tensor_scalar(out=p_xn[pj][:], in0=p_xm[:],
                                          scalar1=p_r[:], scalar2=None,
                                          op0=ALU.mult),
                ("sa", r_sa), (f"e{pj}", ctx.at(f"xn_st_{j - 2}")))
            xn_v = ctx.cnt["va"]
            ctx.mark(f"oc_xo_{j}", "va")
            ctx.wait("pool", "va", xn_v)
            gp_dma(d_xn[j * 128:(j + 1) * 128, :], p_xn[pj][:], sem=f"e{pj}")
            ctx.mark(f"xn_st_{j}", f"e{pj}")
            for i in range(8):
                pe_w("va", xn_v)
                bank_w("pbh0")
                pe_i(PE.transpose(pbh[0][0:128, 0:128],
                                  p_xn[pj][:, i * 128:(i + 1) * 128], s_id_b[:]))
                tp_pe = ctx.cnt["pe"]
                vop(lambda v, i=i: v.tensor_copy(
                    out=p_fmst[:, i * 128:(i + 1) * 128],
                    in_=pbh[0][0:128, 0:128]),
                    ("pe", tp_pe), ("c0", ctx.at(f"xnfm_st_{j - 1}")))
                bank_m("pbh0")
            fm_v = ctx.cnt["va"]
            ctx.wait("pool", "va", fm_v)
            gp_dma(d_xnfm.rearrange("(c p) t -> p c t", p=128)
                   [:, :, j * 128:(j + 1) * 128],
                   p_fmst[:].rearrange("p (c t) -> p c t", c=8), sem="c0")
            ctx.mark(f"xnfm_st_{j}", "c0")
            uj = p_ust[:, j * E:(j + 1) * E]
            vop(lambda v, uj=uj: v.max(out=p_t8j[:], in_=uj),
                ("sa", ctx.at("t8j_free")))
            vop(lambda v, uj=uj: v.tensor_scalar(out=p_Mj[:], in0=uj,
                                                 scalar1=p_t8j[:, 7:8], scalar2=None,
                                                 op0=ALU.is_ge))
            mj_v = ctx.cnt["va"]
            pe_w("va", mj_v)
            bank_w("ocp2")
            pe_i(PE.matmul(pb[2][0:128, 0:E], s_lt[:], p_Mj[:], start=True, stop=False))
            pe_i(PE.matmul(pb[2][0:128, 0:E], s_onr_f[:],
                           p_gst[0:1, j * E:(j + 1) * E], start=False, stop=True))
            pos_pe = ctx.cnt["pe"]
            vop(lambda v: v.tensor_scalar(out=p_c1[:], in0=pb[2][0:128, 0:E],
                                          scalar1=CAP - 0.5, scalar2=None,
                                          op0=ALU.is_le), ("pe", pos_pe))
            bank_m("ocp2")
            vop(lambda v: v.tensor_mul(out=p_sel[:], in0=p_c1[:], in1=p_Mj[:]))
            vop(lambda v: v.memset(p_Kc[:], 0.0))
            for k in range(K8):
                vop(lambda v, k=k, uj=uj: v.tensor_scalar(
                    out=p_oh[:], in0=uj, scalar1=p_t8j[:, k:k + 1],
                    scalar2=None, op0=ALU.is_equal))
                vop(lambda v, k=k: v.scalar_tensor_tensor(
                    out=p_v2[:], in0=p_oh[:], scalar=1.0, in1=p_sel[:],
                    op0=ALU.mult, op1=ALU.mult, accum_out=p_kp8[:, k:k + 1]))
                if k > 0:
                    vop(lambda v, k=k: v.scalar_tensor_tensor(
                        out=p_Kc[:], in0=p_oh[:], scalar=float(k), in1=p_Kc[:],
                        op0=ALU.mult, op1=ALU.add))
            vop(lambda v: v.scalar_tensor_tensor(out=p_nm8[:], in0=p_t8j[:, 0:1],
                                                 scalar=-1.0, in1=p_r[:],
                                                 op0=ALU.mult, op1=ALU.mult))
            nm_v = ctx.cnt["va"]
            aop(lambda s: s.activation(out=p_e8[:], in_=p_t8j[:, 0:K8], func=AF.Exp,
                                       bias=p_nm8[:], scale=p_r[:]), ("va", nm_v))
            e8_sa = ctx.cnt["sa"]
            ctx.mark("t8j_free", "sa")
            vop(lambda v: v.reduce_sum(out=p_d8[:], in_=p_e8[:], axis=AXX),
                ("sa", e8_sa))
            vop(lambda v: v.reciprocal(out=p_di[:], in_=p_d8[:]))
            vop(lambda v: v.scalar_tensor_tensor(out=p_e8[:], in0=p_e8[:],
                                                 scalar=p_di[:], in1=p_kp8[:],
                                                 op0=ALU.mult, op1=ALU.mult))
            vop(lambda v, j=j: v.tensor_copy(out=s_w8[:, j * K8:(j + 1) * K8],
                                             in_=p_e8[:]))
            vop(lambda v, j=j: v.tensor_scalar(out=p_tl8[:], in0=s_iota[:],
                                               scalar1=8.0, scalar2=float(j * 1024),
                                               op0=ALU.mult, op1=ALU.add))
            vop(lambda v: v.tensor_scalar(out=p_v2[:], in0=p_Kc[:],
                                          scalar1=p_tl8[:], scalar2=None,
                                          op0=ALU.add))
            vop(lambda v: v.tensor_scalar(out=p_v2[:], in0=p_v2[:], scalar1=-1.0,
                                          scalar2=PAY, op0=ALU.mult, op1=ALU.add))
            vop(lambda v: v.tensor_mul(out=p_v2[:], in0=p_v2[:], in1=p_sel[:]))
            v2_v = ctx.cnt["va"]
            pe_w("va", v2_v)
            bank_w("ocp4")
            pe_i(PE.transpose(pb[4][0:64, 0:128], p_v2[:], s_id_f[:]))
            vt_pe = ctx.cnt["pe"]
            vop(lambda v, j=j: v.tensor_copy(
                out=p_vm[:, j * 128:(j + 1) * 128], in_=pb[4][0:64, 0:128]),
                ("pe", vt_pe))
            bank_m("ocp4")
        barrier()

        for rd in range(16):
            vop(lambda v, rd=rd: v.max(out=p_a16[:, rd * 8:(rd + 1) * 8],
                                       in_=p_vm[:]))
            vop(lambda v, rd=rd: v.match_replace(
                out=p_vm[:], in_to_replace=p_a16[:, rd * 8:(rd + 1) * 8],
                in_values=p_vm[:], imm_value=0.0))
        x16_v = ctx.cnt["va"]
        pe_w("va", x16_v)
        bank_w("ocp4")
        pe_i(PE.transpose(pb[4][0:128, 0:64], p_a16[:], s_id_f[0:64, 0:64]))
        a16_pe = ctx.cnt["pe"]
        vop(lambda v: v.tensor_copy(out=p_der[:], in_=pb[4][0:128, 0:64]),
            ("pe", a16_pe))
        bank_m("ocp4")
        vop(lambda v: v.tensor_scalar(out=s_vld[:, 0:E], in0=p_der[:], scalar1=0.5,
                                      scalar2=None, op0=ALU.is_gt))
        vop(lambda v: v.tensor_scalar(out=p_der2[:], in0=p_der[:], scalar1=-1.0,
                                      scalar2=PAY, op0=ALU.mult, op1=ALU.add))
        vop(lambda v: v.tensor_mul(out=p_der[:], in0=p_der2[:], in1=s_vld[:, 0:E]))
        vop(lambda v: v.tensor_scalar(out=p_oh[:], in0=s_vld[:, 0:E],
                                      scalar1=-DUMMY_ASG, scalar2=DUMMY_ASG,
                                      op0=ALU.mult, op1=ALU.add))
        vop(lambda v: v.tensor_add(out=p_der[:], in0=p_der[:], in1=p_oh[:]))
        vop(lambda v: v.tensor_copy(out=s_asg[:, 0:E], in_=p_der[:]))
        vop(lambda v: v.tensor_mul(out=p_der2[:], in0=p_der2[:], in1=s_vld[:, 0:E]))
        vop(lambda v: v.tensor_copy(out=s_tokid[:, 0:E], in_=p_der2[:]))
        vop(lambda v: v.tensor_scalar(out=s_tokid[:, 0:E], in0=s_tokid[:, 0:E],
                                      scalar1=3, scalar2=None,
                                      op0=ALU.arith_shift_right))
        barrier()

        load_wmat(d_wq, A_wb, A_cvt)
        wq_v = ctx.cnt["va"]
        for b in range(NB):
            ctx.wait("sp", "pe", ctx.at(f"qblk_{b - 1}"))
            ctx.wait("sp", "va", wq_v)
            for i in range(8):
                sp_dma(A_q[i][:], d_xnfm[i * 128:(i + 1) * 128,
                                         b * 512:(b + 1) * 512])
            xnb_sx = ctx.cnt["sx"]
            for oc in range(8):
                bank = pb[oc % 2]
                pe_w("va", wq_v)
                pe_w("sx", xnb_sx)
                bank_w(f"q{oc % 2}")
                for i in range(8):
                    pe_i(PE.matmul(bank[0:128, 0:512],
                                   A_wb[i][:, oc * 128:(oc + 1) * 128],
                                   A_q[i][:], start=(i == 0), stop=(i == 7)))
                q_pe = ctx.cnt["pe"]
                vop(lambda v, oc=oc, bank=bank: v.tensor_scalar(
                    out=A_P[oc % 2][:, 0:512], in0=bank[0:128, 0:512],
                    scalar1=s_bq[:, oc:oc + 1], scalar2=None, op0=ALU.add),
                    ("pe", q_pe), (f"e{oc % 2}", ctx.at(f"q_st_{b * 8 + oc - 2}")))
                qe_v = ctx.cnt["va"]
                bank_m(f"q{oc % 2}")
                ctx.wait("pool", "va", qe_v)
                ctx.inc(G.dma_start(
                    out=d_qs[(b * 8 + oc) * 128:(b * 8 + oc + 1) * 128, :],
                    in_=A_P[oc % 2][:, 0:512]), f"e{oc % 2}", 16)
                ctx.mark(f"q_st_{b * 8 + oc}", f"e{oc % 2}")
            ctx.mark(f"qblk_{b}", "pe")
        barrier()

        load_wmat(d_wk, A_wb, A_cvt)
        wk_v = ctx.cnt["va"]
        for kt in range(16):
            ctx.wait("sp", "va", max(ctx.at("acvt_free"), wk_v))
            sp_dma(A_cvt[:], d_k[kt * 128:(kt + 1) * 128, :])
            k_sx = ctx.cnt["sx"]
            vop(lambda v: v.tensor_copy(out=A_kvb[:], in_=A_cvt[:]),
                ("sx", k_sx), ("pe", ctx.at("kvb_readers")))
            ctx.mark("acvt_free", "va")
            kb_v = ctx.cnt["va"]
            for i in range(8):
                pe_w("va", kb_v)
                bank_w("pbh0")
                pe_i(PE.transpose(pbh[0][0:128, 0:128],
                                  A_kvb[:, i * 128:(i + 1) * 128], s_id_b[:]))
                vop(lambda v, i=i, kt=kt: v.tensor_copy(
                    out=A_kvfm[i][:, kt * 128:(kt + 1) * 128],
                    in_=pbh[0][0:128, 0:128]), ("pe", ctx.cnt["pe"]))
                bank_m("pbh0")
            ctx.mark("kvb_readers", "pe")
        kfm_v = ctx.cnt["va"]
        for oc in range(8):
            for sl in range(4):
                bank = pb[(oc * 4 + sl) % 2]
                pe_w("va", kfm_v)
                bank_w(f"kh{(oc * 4 + sl) % 2}")
                for i in range(8):
                    pe_i(PE.matmul(bank[0:128, 0:512],
                                   A_wb[i][:, oc * 128:(oc + 1) * 128],
                                   A_kvfm[i][:, sl * 512:(sl + 1) * 512],
                                   start=(i == 0), stop=(i == 7)))
                kh_pe = ctx.cnt["pe"]
                vop(lambda v, oc=oc, sl=sl, bank=bank: v.tensor_scalar(
                    out=A_kh[oc][:, sl * 512:(sl + 1) * 512], in0=bank[0:128, 0:512],
                    scalar1=s_bk[:, oc:oc + 1], scalar2=None, op0=ALU.add),
                    ("pe", kh_pe))
                bank_m(f"kh{(oc * 4 + sl) % 2}")
        barrier()
        load_wmat(d_wv, A_wb, A_cvt)
        wv_v = ctx.cnt["va"]
        for kt in range(16):
            ctx.wait("sp", "va", max(ctx.at("acvt_free"), wv_v))
            sp_dma(A_cvt[:], d_v[kt * 128:(kt + 1) * 128, :])
            v_sx = ctx.cnt["sx"]
            vop(lambda v: v.tensor_copy(out=A_kvb[:], in_=A_cvt[:]),
                ("sx", v_sx), ("pe", ctx.at("kvb_readers")))
            ctx.mark("acvt_free", "va")
            vb_v = ctx.cnt["va"]
            for i in range(8):
                pe_w("va", vb_v)
                bank_w("pbh0")
                pe_i(PE.transpose(pbh[0][0:128, 0:128],
                                  A_kvb[:, i * 128:(i + 1) * 128], s_id_b[:]))
                vop(lambda v, i=i, kt=kt: v.tensor_copy(
                    out=A_kvfm[i][:, kt * 128:(kt + 1) * 128],
                    in_=pbh[0][0:128, 0:128]), ("pe", ctx.cnt["pe"]))
                bank_m("pbh0")
            ctx.mark("kvb_readers", "pe")
        vfm_v = ctx.cnt["va"]
        for kt in range(16):
            vop(lambda v, kt=kt: v.memset(
                A_vhx[:, kt * NH * 65 + 64:(kt + 1) * NH * 65:65], 1.0))
        ones_v = ctx.cnt["va"]
        for kt in range(16):
            for ns in range(2):
                bank = pb[(kt * 2 + ns) % 2]
                pe_w("va", max(vfm_v, ones_v, wv_v))
                bank_w(f"vh{(kt * 2 + ns) % 2}")
                for i in range(8):
                    pe_i(PE.matmul(bank[0:128, 0:512],
                                   A_kvfm[i][:, kt * 128:(kt + 1) * 128],
                                   A_wb[i][:, ns * 512:(ns + 1) * 512],
                                   start=(i == 0), stop=False))
                pe_i(PE.matmul(bank[0:128, 0:512], s_onr_b[:],
                               s_bv_b[0:1, ns * 512:(ns + 1) * 512],
                               start=False, stop=True))
                vh_pe = ctx.cnt["pe"]
                for hh in range(8):
                    h = ns * 8 + hh
                    vop(lambda v, kt=kt, h=h, hh=hh, bank=bank: v.tensor_copy(
                        out=A_vhx[:, (kt * NH + h) * 65:(kt * NH + h) * 65 + 64],
                        in_=bank[0:128, hh * 64:(hh + 1) * 64]), ("pe", vh_pe))
                bank_m(f"vh{(kt * 2 + ns) % 2}")
        barrier()
        load_wmat(d_wo, A_wb, A_wcvt)
        wo_v = ctx.cnt["va"]
        for b in range(NB):
            ctx.wait("sp", "pe", ctx.at(f"attn_blk_{b - 1}"))
            for oc in range(8):
                sp_dma(A_q[oc][:], d_qs[(b * 8 + oc) * 128:(b * 8 + oc + 1) * 128, :])
            q_sx = ctx.cnt["sx"]
            for hp in range(NH // 2):
                ha, hb = 2 * hp, 2 * hp + 1
                ch = hp
                if hp == 0:
                    bank_w("op0")
                    bank_w("op1")
                    bank_w("S1v")
                    bank_w("ocp2v")
                for k2 in range(8):
                    kt0, kt1 = 2 * k2, 2 * k2 + 1
                    pe_w("sx", q_sx)
                    bank_w("S0", "sa")
                    pe_i(PE.matmul(pbA[:, 0:512],
                                   A_kh[ch][0:64, kt0 * 128:(kt0 + 1) * 128],
                                   A_q[ch][0:64, 0:512], start=True, stop=True))
                    pe_i(PE.matmul(pbA[:, 512:1024],
                                   A_kh[ch][0:64, kt1 * 128:(kt1 + 1) * 128],
                                   A_q[ch][0:64, 0:512], start=True, stop=True))
                    sA = ctx.cnt["pe"]
                    bank_w("S1", "sa")
                    pe_i(PE.matmul(pbB[:, 0:512],
                                   A_kh[ch][64:128, kt0 * 128:(kt0 + 1) * 128],
                                   A_q[ch][64:128, 0:512], start=True, stop=True))
                    pe_i(PE.matmul(pbB[:, 512:1024],
                                   A_kh[ch][64:128, kt1 * 128:(kt1 + 1) * 128],
                                   A_q[ch][64:128, 0:512], start=True, stop=True))
                    sB = ctx.cnt["pe"]
                    aop(lambda s: s.activation(
                        out=A_P[0][:], in_=pbA[:, 0:1024], func=AF.Exp,
                        bias=-12.0, scale=0.125),
                        ("pe", sA), ("pe", ctx.at("aVa")))
                    bank_m("S0", "sa")
                    expA = ctx.cnt["sa"]
                    aop(lambda s: s.activation(
                        out=A_P[1][:], in_=pbB[:, 0:1024], func=AF.Exp,
                        bias=-12.0, scale=0.125),
                        ("pe", sB), ("pe", ctx.at("aVb")))
                    bank_m("S1", "sa")
                    expB = ctx.cnt["sa"]
                    pe_w("sa", expA)
                    if k2 == 0:
                        bank_w("oaccA")
                    pe_i(PE.matmul(pbC[0:65, 0:512],
                                   A_vhx[:, (kt0 * NH + ha) * 65:(kt0 * NH + ha + 1) * 65],
                                   A_P[0][:, 0:512], start=(k2 == 0), stop=False))
                    pe_i(PE.matmul(pbC[0:65, 0:512],
                                   A_vhx[:, (kt1 * NH + ha) * 65:(kt1 * NH + ha + 1) * 65],
                                   A_P[0][:, 512:1024], start=False, stop=(k2 == 7)))
                    ctx.mark("aVa", "pe")
                    pe_w("sa", expB)
                    if k2 == 0:
                        bank_w("oaccB")
                    pe_i(PE.matmul(pbD[0:65, 0:512],
                                   A_vhx[:, (kt0 * NH + hb) * 65:(kt0 * NH + hb + 1) * 65],
                                   A_P[1][:, 0:512], start=(k2 == 0), stop=False))
                    pe_i(PE.matmul(pbD[0:65, 0:512],
                                   A_vhx[:, (kt1 * NH + hb) * 65:(kt1 * NH + hb + 1) * 65],
                                   A_P[1][:, 512:1024], start=False, stop=(k2 == 7)))
                    ctx.mark("aVb", "pe")
                av_pe = ctx.cnt["pe"]
                vop(lambda v, ch=ch: v.tensor_copy(
                    out=A_ofm[ch][0:64, 0:512], in_=pbC[0:64, 0:512]),
                    ("pe", av_pe))
                vop(lambda v: v.tensor_copy(out=A_drow[:], in_=pbC[64:65, 0:512]),
                    ("gd", ctx.at("drow_free")))
                dr_v = ctx.cnt["va"]
                bank_m("oaccA")
                ctx.wait("pool", "va", dr_v)
                ctx.wait("pool", "pe", ctx.at(f"dstack_rd_{b - 1}"))
                ctx.inc(G.dma_start(out=A_dstack[ha:ha + 1, :], in_=A_drow[:]),
                        "gd", 16)
                ctx.mark("drow_free", "gd")
                vop(lambda v, ch=ch: v.tensor_copy(
                    out=A_ofm[ch][64:128, 0:512], in_=pbD[0:64, 0:512]))
                vop(lambda v: v.tensor_copy(out=A_drow[:], in_=pbD[64:65, 0:512]),
                    ("gd", ctx.at("drow_free")))
                dr_v = ctx.cnt["va"]
                bank_m("oaccB")
                ctx.wait("pool", "va", dr_v)
                ctx.inc(G.dma_start(out=A_dstack[hb:hb + 1, :], in_=A_drow[:]),
                        "gd", 16)
                ctx.mark("drow_free", "gd")
            dstack_gd = ctx.cnt["gd"]
            for tt in range(4):
                pe_w("gd", dstack_gd)
                bank_w("dD")
                bank_w("S1", "sa")
                bank_w("S1v")
                pe_i(PE.transpose(pbB[0:128, 0:16],
                                  A_dstack[0:16, tt * 128:(tt + 1) * 128],
                                  s_id_f[0:16, 0:16]))
                dt_pe = ctx.cnt["pe"]
                vop(lambda v: v.reciprocal(out=A_dinv[:], in_=pbB[0:128, 0:16]),
                    ("pe", dt_pe))
                bank_m("dD")
                bank_m("S1v")
                di_v = ctx.cnt["va"]
                for h in range(NH):
                    ch = h // 2
                    r0 = (h % 2) * 64
                    pe_w("va", di_v)
                    bank_w("pbh0")
                    pe_i(PE.transpose(pbh[0][0:128, 0:64],
                                      A_ofm[ch][r0:r0 + 64, tt * 128:(tt + 1) * 128],
                                      s_id_b[r0:r0 + 64, r0:r0 + 64]))
                    ot_pe = ctx.cnt["pe"]
                    vop(lambda v, tt=tt, h=h: v.tensor_scalar(
                        out=A_otm[tt][:, h * 64:(h + 1) * 64],
                        in0=pbh[0][0:128, 0:64],
                        scalar1=A_dinv[:, h:h + 1], scalar2=None, op0=ALU.mult),
                        ("pe", ot_pe))
                    bank_m("pbh0")
            ctx.mark(f"dstack_rd_{b}", "pe")
            otm_v = ctx.cnt["va"]
            for tt in range(4):
                for i in range(8):
                    pe_w("va", otm_v)
                    bank_w("pbh1")
                    pe_i(PE.transpose(pbh[1][0:128, 0:128],
                                      A_otm[tt][:, i * 128:(i + 1) * 128], s_id_b[:]))
                    vop(lambda v, i=i, tt=tt: v.tensor_copy(
                        out=A_ofm[i][:, tt * 128:(tt + 1) * 128],
                        in_=pbh[1][0:128, 0:128]), ("pe", ctx.cnt["pe"]))
                    bank_m("pbh1")
            onfm_v = ctx.cnt["va"]
            for tt in range(4):
                for ns in range(2):
                    bank = pbA[:, ns * 512:(ns + 1) * 512]
                    pe_w("va", max(onfm_v, wo_v))
                    bank_w(f"op{ns}")
                    if tt == 0 and ns == 0:
                        bank_w("S0", "sa")
                    for i in range(8):
                        pe_i(PE.matmul(bank[0:128, 0:512],
                                       A_ofm[i][:, tt * 128:(tt + 1) * 128],
                                       A_wb[i][:, ns * 512:(ns + 1) * 512],
                                       start=(i == 0), stop=False))
                    pe_i(PE.matmul(bank[0:128, 0:512], s_onr_b[:],
                                   s_bo_b[0:1, ns * 512:(ns + 1) * 512],
                                   start=False, stop=True))
                    o_pe = ctx.cnt["pe"]
                    vop(lambda v, ns=ns, bank=bank: v.tensor_copy(
                        out=A_stm[:, ns * 512:(ns + 1) * 512], in_=bank[0:128, 0:512]),
                        ("pe", o_pe), ("gs", ctx.at("stm_free")))
                    bank_m(f"op{ns}")
                sev = ctx.cnt["va"]
                ctx.wait("pool", "va", sev)
                gp_dma(d_s[(b * 4 + tt) * 128:(b * 4 + tt + 1) * 128, :], A_stm[:])
                ctx.mark("stm_free", "gs")
            ctx.mark(f"attn_blk_{b}", "pe")
        barrier()

        for i in range(8):
            for q4 in range(4):
                ctx.wait("sp", "va", ctx.at("bstage_free"))
                sq_dma("d2", B_cvt[:], d_sguw[i * 128:(i + 1) * 128,
                                              q4 * 1024:(q4 + 1) * 1024])
                wsx = ctx.cnt["d2"]
                vop(lambda v, i=i, q4=q4: v.tensor_copy(
                    out=B_wgu[i][:, q4 * 1024:(q4 + 1) * 1024], in_=B_cvt[:]),
                    ("d2", wsx))
                ctx.mark("bstage_free", "va")
        for i in range(16):
            ctx.wait("sp", "va", ctx.at("bstage_free"))
            sq_dma("d2", B_cvt[:], d_sdw[i * 128:(i + 1) * 128, :])
            wsx = ctx.cnt["d2"]
            vop(lambda v, i=i: v.tensor_copy(out=B_wd[i][:], in_=B_cvt[:]),
                ("d2", wsx))
            ctx.mark("bstage_free", "va")
        ctx.wait("sp", "va", ctx.at("bstage_free"))
        sq_dma("d2", B_cvt[:, 0:8].rearrange("p (c o) -> p c o", o=1),
               d_sgw.rearrange("(c p) o -> p c o", p=128),
               allow_slow_non_contiguous=True)
        sg_sx = ctx.cnt["d2"]
        vop(lambda v: v.tensor_copy(out=B_sgwc[:], in_=B_cvt[:, 0:8]), ("d2", sg_sx))
        bw_v = ctx.cnt["va"]
        for b in range(NB):
            ctx.wait("sp", "pe", ctx.at(f"bblk_{b - 1}"))
            for i in range(8):
                sp_dma(B_xn[i][:], d_xnfm[i * 128:(i + 1) * 128,
                                          b * 512:(b + 1) * 512])
            xnb_sx = ctx.cnt["sx"]
            for jj in range(16):
                bk0 = pb[(jj % 2) * 2]
                bk1 = pb[(jj % 2) * 2 + 1]
                pe_w("va", bw_v)
                pe_w("sx", xnb_sx)
                bank_w(f"bg{jj % 2}", "sa")
                bank_w(f"bu{jj % 2}")
                for i in range(8):
                    pe_i(PE.matmul(bk0[0:128, 0:512],
                                   B_wgu[i][:, jj * 128:(jj + 1) * 128],
                                   B_xn[i][:], start=(i == 0), stop=(i == 7)))
                for i in range(8):
                    pe_i(PE.matmul(bk1[0:128, 0:512],
                                   B_wgu[i][:, 2048 + jj * 128:2048 + (jj + 1) * 128],
                                   B_xn[i][:], start=(i == 0), stop=(i == 7)))
                gu_pe = ctx.cnt["pe"]
                aop(lambda s, bk0=bk0: s.activation(out=B_sh[:], in_=bk0[0:128, 0:512],
                                                    func=AF.Silu),
                    ("pe", gu_pe), ("va", ctx.at("bsh_free")))
                sh_sa = ctx.cnt["sa"]
                bank_m(f"bg{jj % 2}", "sa")
                vop(lambda v, jj=jj, bk1=bk1: v.tensor_mul(
                    out=B_act[jj][:], in0=B_sh[:], in1=bk1[0:128, 0:512]),
                    ("sa", sh_sa), ("pe", gu_pe))
                bank_m(f"bu{jj % 2}")
                ctx.mark("bsh_free", "va")
            act_v = ctx.cnt["va"]
            pe_w("va", act_v)
            bank_w("sgz")
            for i in range(8):
                pe_i(PE.matmul(pb[4][0:1, 0:512], B_sgwc[:, i:i + 1],
                               B_xn[i][:], start=(i == 0), stop=(i == 7)))
            sgz_pe = ctx.cnt["pe"]
            vop(lambda v: v.tensor_copy(out=B_sgr[:], in_=pb[4][0:1, 0:512]),
                ("pe", sgz_pe), ("e0", ctx.at(f"sg_st_{b - 1}")))
            bank_m("sgz")
            sgr_v = ctx.cnt["va"]
            ctx.wait("pool", "va", sgr_v)
            gp_dma(d_sg[b:b + 1, :], B_sgr[:], sem="e0")
            ctx.mark(f"sg_st_{b}", "e0")
            for tt in range(4):
                for ns in range(2):
                    bank = pb[4 + ns]
                    pe_w("va", act_v)
                    if ns == 0:
                        bank_w("sgz")
                    bank_w(f"dn{ns}")
                    for i in range(16):
                        pe_i(PE.matmul(bank[0:128, 0:512],
                                       B_act[i][:, tt * 128:(tt + 1) * 128],
                                       B_wd[i][:, ns * 512:(ns + 1) * 512],
                                       start=(i == 0), stop=(i == 15)))
                    y_pe = ctx.cnt["pe"]
                    vop(lambda v, ns=ns, bank=bank: v.tensor_copy(
                        out=B_y[:, ns * 512:(ns + 1) * 512], in_=bank[0:128, 0:512]),
                        ("pe", y_pe), ("gs", ctx.at("by_free")))
                    bank_m(f"dn{ns}")
                yv = ctx.cnt["va"]
                ctx.wait("pool", "va", yv)
                gp_dma(d_ysh[(b * 4 + tt) * 128:(b * 4 + tt + 1) * 128, :], B_y[:])
                ctx.mark("by_free", "gs")
            ctx.mark(f"bblk_{b}", "pe")
        barrier()

        sp_dma(C_sgs[:], d_sg.rearrange("(o a) b -> o (a b)", o=1))
        sgl_sx = ctx.cnt["sx"]
        aop(lambda s: s.activation(out=C_sgs[:], in_=C_sgs[:], func=AF.Sigmoid),
            ("sx", sgl_sx))
        sgs_sa = ctx.cnt["sa"]
        for j in range(NT):
            pe_w("sa", sgs_sa)
            bank_w("sgt")
            pe_i(PE.transpose(pb[5][0:128, 0:1], C_sgs[0:1, j * 128:(j + 1) * 128],
                              s_id_f[0:1, 0:1]))
            vop(lambda v, j=j: v.tensor_copy(out=s_sgt[:, j:j + 1],
                                             in_=pb[5][0:128, 0:1]),
                ("pe", ctx.cnt["pe"]))
            bank_m("sgt")
        barrier()
        bc_reg = G.to_reg(YBROWS - 1)
        ctx.wait("pool", "g2", ctx.at("ybzero"))
        for e in range(E):
            p = e % 2
            ctx.wait("pool", "va", ctx.at(f"ebmask_{e - 2}"))
            ctx.inc(G.indirect_dma_start(
                out=C_eb[p][:], out_offset=None, in_=d_xn[:, :],
                in_offset=IndirectOffsetOnAxis(ap=s_tokid[:, e:e + 1], axis=0)),
                f"e{e % 2}", 16)
            gd_e = ctx.cnt[f"e{e % 2}"]
            vop(lambda v, p=p, e=e: v.tensor_scalar(
                out=C_eb[p][:], in0=C_eb[p][:], scalar1=s_vld[:, e:e + 1],
                scalar2=None, op0=ALU.mult), (f"e{e % 2}", gd_e))
            ctx.mark(f"ebmask_{e}", "va")
            egu_rows = d_egu.rearrange("e d n -> (e d) n")
            edw_rows = d_edw.rearrange("e d n -> (e d) n")
            for i2 in range(4):
                sp_ = i2 % 2
                ctx.wait("sp", "va", ctx.at(f"wguf{sp_}_va"))
                ctx.wait("sp", "sa", ctx.at(f"wguf{sp_}_sa"))
                sq_dma(f"d{sp_}", C_wguf[sp_][:].rearrange("p (a n) -> p a n", a=2),
                       egu_rows[e * D + i2 * 256:e * D + (i2 + 1) * 256, :]
                       .rearrange("(a p) n -> p a n", p=128))
                wsx = ctx.cnt[f"d{sp_}"]
                vop(lambda v, sp_=sp_, i2=i2, p=p: v.tensor_copy(
                    out=C_wgu[p][2 * i2][:], in_=C_wguf[sp_][:, 0:D]),
                    (f"d{sp_}", wsx))
                aop(lambda s, sp_=sp_, i2=i2, p=p: s.copy(
                    out=C_wgu[p][2 * i2 + 1][:], in_=C_wguf[sp_][:, D:2 * D]),
                    (f"d{sp_}", wsx))
                ctx.mark(f"wguf{sp_}_va", "va")
                ctx.mark(f"wguf{sp_}_sa", "sa")
            for i2 in range(2):
                sp_ = i2 % 2
                ctx.wait("sp", "va", ctx.at(f"wdf{sp_}_va"))
                ctx.wait("sp", "sa", ctx.at(f"wdf{sp_}_sa"))
                sq_dma(f"d{2 + sp_}",
                       C_wdf[sp_][:].rearrange("p (a n) -> p a n", a=2),
                       edw_rows[e * 512 + i2 * 256:e * 512 + (i2 + 1) * 256, :]
                       .rearrange("(a p) n -> p a n", p=128))
                wsx = ctx.cnt[f"d{2 + sp_}"]
                vop(lambda v, sp_=sp_, i2=i2, p=p: v.tensor_copy(
                    out=C_wd[p][2 * i2][:], in_=C_wdf[sp_][:, 0:D]),
                    (f"d{2 + sp_}", wsx))
                aop(lambda s, sp_=sp_, i2=i2, p=p: s.copy(
                    out=C_wd[p][2 * i2 + 1][:], in_=C_wdf[sp_][:, D:2 * D]),
                    (f"d{2 + sp_}", wsx))
                ctx.mark(f"wdf{sp_}_va", "va")
                ctx.mark(f"wdf{sp_}_sa", "sa")
            cvt_v = ctx.cnt["va"]; cvt_a = ctx.cnt["sa"]
            for i in range(8):
                pe_w("va", ctx.at(f"ebmask_{e}"))
                bank_w("pbh0")
                pe_i(PE.transpose(pbh[0][0:128, 0:128],
                                  C_eb[p][:, i * 128:(i + 1) * 128], s_id_b[:]))
                vop(lambda v, i=i: v.tensor_copy(out=C_ebt[0][:, i * 128:(i + 1) * 128],
                                                 in_=pbh[0][0:128, 0:128]),
                    ("pe", ctx.cnt["pe"]))
                bank_m("pbh0")
            ebt_v = ctx.cnt["va"]
            pe_w("va", max(ebt_v, cvt_v))
            pe_w("sa", cvt_a)
            bank_w("cg", "sa")
            bank_w("cu")
            for ns in range(2):
                for i in range(8):
                    pe_i(PE.matmul(pb[ns][0:128, 0:512],
                                   C_ebt[0][:, i * 128:(i + 1) * 128],
                                   C_wgu[p][i][:, ns * 512:(ns + 1) * 512],
                                   start=(i == 0), stop=(i == 7)))
            gu_pe = ctx.cnt["pe"]
            aop(lambda s: s.activation(out=C_sh[0][:], in_=pb[0][0:128, 0:512],
                                       func=AF.Silu),
                ("pe", gu_pe), ("va", ctx.at("csh_free")))
            sh_sa = ctx.cnt["sa"]
            bank_m("cg", "sa")
            vop(lambda v: v.tensor_mul(out=C_act[0][:], in0=C_sh[0][:],
                                       in1=pb[1][0:128, 0:512]),
                ("sa", sh_sa), ("pe", gu_pe))
            bank_m("cu")
            ctx.mark("csh_free", "va")
            cact_v = ctx.cnt["va"]
            for i in range(4):
                pe_w("va", cact_v)
                bank_w("pbh1")
                pe_i(PE.transpose(pbh[1][0:128, 0:128],
                                  C_act[0][:, i * 128:(i + 1) * 128], s_id_b[:]))
                vop(lambda v, i=i: v.tensor_copy(out=C_actT[0][:, i * 128:(i + 1) * 128],
                                                 in_=pbh[1][0:128, 0:128]),
                    ("pe", ctx.cnt["pe"]))
                bank_m("pbh1")
            at_v = ctx.cnt["va"]
            pe_w("va", at_v)
            bank_w("cd0")
            bank_w("cd1")
            for ns in range(2):
                for i in range(4):
                    pe_i(PE.matmul(pb[2 + ns][0:128, 0:512],
                                   C_actT[0][:, i * 128:(i + 1) * 128],
                                   C_wd[p][i][:, ns * 512:(ns + 1) * 512],
                                   start=(i == 0), stop=(i == 3)))
            dn_pe = ctx.cnt["pe"]
            ctx.mark(f"exp_pe_{e}", "pe")
            for ns in range(2):
                vop(lambda v, ns=ns, p=p: v.tensor_copy(
                    out=C_ebo[p][:, ns * 512:(ns + 1) * 512],
                    in_=pb[2 + ns][0:128, 0:512]),
                    ("pe", dn_pe), (f"c{e % 2}", ctx.at(f"ebo_st_{e - 2}")))
                bank_m(f"cd{ns}")
            ebo_v = ctx.cnt["va"]
            ctx.wait("pool", "va", ebo_v)
            ctx.inc(G.indirect_dma_start(
                out=d_yb[:, :],
                out_offset=IndirectOffsetOnAxis(ap=s_asg[:, e:e + 1], axis=0),
                in_=C_ebo[p][:], in_offset=None,
                bounds_check=bc_reg, oob_is_err=False), f"c{e % 2}", 16)
            ctx.mark(f"ebo_st_{e}", f"c{e % 2}")
        barrier()
        yb3 = d_yb.rearrange("(t k) d -> t k d", k=K8)
        for j in range(NT):
            p = j % 2
            ctx.wait("sp", "va", ctx.at(f"comb_{j - 2}"))
            sq_dma(f"d{p}", C_x[p][:], d_x_own[j * 128:(j + 1) * 128, :])
            sq_dma(f"d{p}", C_sv[p][:], d_s[j * 128:(j + 1) * 128, :])
            sq_dma(f"d{p}", C_ysh[p][:], d_ysh[j * 128:(j + 1) * 128, :])
            ld_sx = ctx.cnt[f"d{p}"]
            vop(lambda v: v.memset(C_y[:], 0.0))
            for k in range(K8):
                gsem = ["d2", "d3", "g2", "g3"][k % 4]
                ctx.wait("sp", "va", ctx.at(f"gaccP{k % 4}"))
                sq_dma(gsem, C_g[k % 4][:], yb3[j * 128:(j + 1) * 128, k, :])
                gk_sx = ctx.cnt[gsem]
                vop(lambda v, j=j, k=k: v.scalar_tensor_tensor(
                    out=C_y[:], in0=C_g[k % 4][:],
                    scalar=s_w8[:, j * K8 + k:j * K8 + k + 1], in1=C_y[:],
                    op0=ALU.mult, op1=ALU.add), (gsem, gk_sx))
                ctx.mark(f"gaccP{k % 4}", "va")
            vop(lambda v, j=j, p=p: v.scalar_tensor_tensor(
                out=C_ffn[p][:], in0=C_ysh[p][:], scalar=s_sgt[:, j:j + 1],
                in1=C_y[:], op0=ALU.mult, op1=ALU.add),
                (f"d{p}", ld_sx), ("sa", ctx.at(f"ffnfree{p}")))
            ffn_v = ctx.cnt["va"]
            aop(lambda s, p=p: s.activation(out=C_gg[p][:], in_=C_ffn[p][:],
                                            func=AF.Silu), ("va", ffn_v))
            gg_sa = ctx.cnt["sa"]
            ctx.mark(f"ffnfree{p}", "sa")
            vop(lambda v, p=p: v.tensor_mul(out=C_gg[p][:], in0=C_gg[p][:],
                                            in1=C_sv[p][:]),
                ("sa", gg_sa))
            osem = "so" if p == 0 else "s1"
            vop(lambda v, p=p: v.tensor_add(out=C_out[p][:], in0=C_gg[p][:],
                                            in1=C_x[p][:]),
                (osem, ctx.at(f"out_st_{j - 2}")))
            o_v = ctx.cnt["va"]
            ctx.mark(f"comb_{j}", "va")
            ctx.wait("pool", "va", o_v)
            ctx.inc(G.dma_start(out=d_out[j * 128:(j + 1) * 128, :],
                                in_=C_out[p][:]), osem, 16)
            ctx.mark(f"out_st_{j}", osem)
        for en in ("sp", "pe", "dve", "act", "pool"):
            ctx.wait(en, "so", NT * 8)
            ctx.wait(en, "s1", NT * 8)

    engine_map = {"sp": "sync", "pe": "tensor", "act": "scalar",
                  "dve": "vector", "pool": "gpsimd"}
    counts = []
    with nc.Block() as block:
        for name, attr in engine_map.items():
            def make_body(nm):
                def body(eng):
                    engines = {k: (_DryEng() if k != nm else eng) for k in engine_map}
                    ctx = Ctx(sems, engines, nm)
                    emit_all(ctx)
                    counts.append(dict(ctx.cnt))
                return body
            getattr(block, attr)(make_body(name))
    assert all(c == counts[0] for c in counts[1:]), "pass count divergence"
    return nc


_NC_CACHE = None


def kernel(**inputs):
    global _NC_CACHE
    x = np.ascontiguousarray(np.asarray(inputs["L0_attn_q"], np.float32).reshape(T, D))
    x_fm = np.ascontiguousarray(x.T)
    in_common = dict(
        x_fm=x_fm,
        kv_k=np.ascontiguousarray(np.asarray(inputs["L0_attn_k"], np.float32)[0]),
        kv_v=np.ascontiguousarray(np.asarray(inputs["L0_attn_v"], np.float32)[0]),
        w_q=np.asarray(inputs["L0_attn_w_q"], np.float32),
        w_k=np.asarray(inputs["L0_attn_w_k"], np.float32),
        w_v=np.asarray(inputs["L0_attn_w_v"], np.float32),
        w_o=np.asarray(inputs["L0_attn_w_o"], np.float32),
        b_q=np.asarray(inputs["L0_attn_b_q"], np.float32).reshape(1, D),
        b_k=np.asarray(inputs["L0_attn_b_k"], np.float32).reshape(1, D),
        b_v=np.asarray(inputs["L0_attn_b_v"], np.float32).reshape(1, D),
        b_o=np.asarray(inputs["L0_attn_b_o"], np.float32).reshape(1, D),
        gate_w=np.asarray(inputs["L0_ffn_gate_w"], np.float32),
        sgw=np.asarray(inputs["L0_ffn_shared_gate_w"], np.float32),
        sguw=np.asarray(inputs["L0_ffn_shared_gate_up_w"], np.float32),
        sdw=np.asarray(inputs["L0_ffn_shared_down_w"], np.float32),
        egu=np.asarray(inputs["L0_ffn_expert_gate_up_w"], np.float32),
        edw=np.asarray(inputs["L0_ffn_expert_down_w"], np.float32),
    )
    xt = x.reshape(GT, 128, D)
    if _NC_CACHE is None:
        _NC_CACHE = build_nc()
    nc = _NC_CACHE
    in_maps = []
    for c in range(NCORE):
        im = dict(in_common)
        im["x_own"] = np.ascontiguousarray(xt[c::NCORE].reshape(TC, D))
        cm = np.zeros((128, 8), np.float32)
        cm[:, c] = 1.0
        im["cmask"] = cm
        in_maps.append(im)
    res = run_bass_kernel_spmd(nc, in_maps, core_ids=list(range(NCORE)))
    full = np.empty((GT, 128, D), np.float32)
    for c in range(NCORE):
        full[c::NCORE] = np.asarray(res.results[c]["out"]).reshape(NT, 128, D)
    return full.reshape(8, 4096, D)


# version 13
# speedup vs baseline: 1.0285x; 1.0285x over previous
import numpy as np
import ml_dtypes
import concourse.bass as bass
import concourse.mybir as mybir
from concourse.bass import IndirectOffsetOnAxis
from concourse.bass_utils import run_bass_kernel_spmd

F32 = mybir.dt.float32
BF16 = mybir.dt.bfloat16
I32 = mybir.dt.int32
ALU = mybir.AluOpType
AF = mybir.ActivationFunctionType
AXX = mybir.AxisListType.X

D = 1024
T = 32768
TC = 4096
NT = 32
GT = 256
NB = 8
KVL = 2048
NH = 16
E = 64
K8 = 8
CAP = 512
LCAP = 128
NCORE = 8
EPS = 1e-5
YBROWS = TC * K8
PAY = 65536.0
DUMMY_ASG = 40000.0
SCRATCH = 16384


class _DryI:
    def then_inc(self, sem, n=1):
        return self


_DRYI = _DryI()


class _DryEng:
    def __getattr__(self, name):
        def f(*a, **k):
            return _DRYI
        return f


class Ctx:

    def __init__(self, sems, engines, real_name):
        self.sems = sems
        self.eng = engines
        self.real = real_name
        self.cnt = {k: 0 for k in sems}
        self.marks = {}

    def inc(self, ins, sem, n=1):
        self.cnt[sem] += n
        if ins is not _DRYI and ins is not None:
            ins.then_inc(self.sems[sem], n)
        return self.cnt[sem]

    def wait(self, engname, sem, val=None):
        if val is None:
            val = self.cnt[sem]
        if val <= 0:
            return
        e = self.eng[engname]
        if not isinstance(e, _DryEng):
            e.wait_ge(self.sems[sem], val)

    def mark(self, tag, sem):
        self.marks[tag] = self.cnt[sem]

    def at(self, tag):
        return self.marks.get(tag, 0)


def build_nc():
    nc = bass.Bass("TRN2", target_bir_lowering=False,
                   dynamic_dma_scratch_size=SCRATCH)

    d_x_own = nc.dram_tensor("x_own", [TC, D], F32, kind="ExternalInput").ap()
    d_x_fm = nc.dram_tensor("x_fm", [D, T], F32, kind="ExternalInput").ap()
    d_k = nc.dram_tensor("kv_k", [KVL, D], F32, kind="ExternalInput").ap()
    d_v = nc.dram_tensor("kv_v", [KVL, D], F32, kind="ExternalInput").ap()
    d_wq = nc.dram_tensor("w_q", [D, D], F32, kind="ExternalInput").ap()
    d_wk = nc.dram_tensor("w_k", [D, D], F32, kind="ExternalInput").ap()
    d_wv = nc.dram_tensor("w_v", [D, D], F32, kind="ExternalInput").ap()
    d_wo = nc.dram_tensor("w_o", [D, D], F32, kind="ExternalInput").ap()
    d_bq = nc.dram_tensor("b_q", [1, D], F32, kind="ExternalInput").ap()
    d_bk = nc.dram_tensor("b_k", [1, D], F32, kind="ExternalInput").ap()
    d_bv = nc.dram_tensor("b_v", [1, D], F32, kind="ExternalInput").ap()
    d_bo = nc.dram_tensor("b_o", [1, D], F32, kind="ExternalInput").ap()
    d_gw = nc.dram_tensor("gate_w", [D, E], F32, kind="ExternalInput").ap()
    d_sgw = nc.dram_tensor("sgw", [D, 1], F32, kind="ExternalInput").ap()
    d_sguw = nc.dram_tensor("sguw", [D, 4096], F32, kind="ExternalInput").ap()
    d_sdw = nc.dram_tensor("sdw", [2048, D], F32, kind="ExternalInput").ap()
    d_egu = nc.dram_tensor("egu", [E, D, D], F32, kind="ExternalInput").ap()
    d_edw = nc.dram_tensor("edw", [E, 512, D], F32, kind="ExternalInput").ap()
    d_cm = nc.dram_tensor("cmask", [128, 8], F32, kind="ExternalInput").ap()

    d_out = nc.dram_tensor("out", [TC, D], F32, kind="ExternalOutput").ap()

    d_xn = nc.dram_tensor("xn_scr", [TC, D], BF16, kind="Internal").ap()
    d_xnfm = nc.dram_tensor("xnfm_scr", [D, TC], BF16, kind="Internal").ap()
    d_yb = nc.dram_tensor("yb_scr", [YBROWS, D], BF16, kind="Internal").ap()
    d_s = nc.dram_tensor("s_scr", [TC, D], F32, kind="Internal").ap()
    d_ysh = nc.dram_tensor("ysh_scr", [TC, D], F32, kind="Internal").ap()
    d_qs = nc.dram_tensor("q_scr", [NB * 1024, 512], BF16, kind="Internal").ap()
    d_sg = nc.dram_tensor("sg_scr", [NB, 512], F32, kind="Internal").ap()

    c_id_f = nc.inline_tensor(np.eye(128, dtype=np.float32), name="c_id_f")
    c_id_b = nc.inline_tensor(np.eye(128).astype(ml_dtypes.bfloat16), name="c_id_b")
    c_lt = nc.inline_tensor(np.triu(np.ones((128, 128), np.float32), 1), name="c_lt")
    c_onc_f = nc.inline_tensor(np.ones((128, 1), np.float32), name="c_onc_f")
    c_onc_b = nc.inline_tensor(np.ones((128, 1)).astype(ml_dtypes.bfloat16), name="c_onc_b")
    c_onr_f = nc.inline_tensor(np.ones((1, 128), np.float32), name="c_onr_f")
    c_onr_b = nc.inline_tensor(np.ones((1, 128)).astype(ml_dtypes.bfloat16), name="c_onr_b")
    c_iota = nc.inline_tensor(np.arange(128, dtype=np.float32).reshape(128, 1),
                              name="c_iota")

    class Alloc:
        def __init__(self, base):
            self.cur = base

        def al(self, name, shape, dtype, align=64):
            itm = {F32: 4, BF16: 2, I32: 4}[dtype]
            per_part = int(np.prod(shape[1:])) * itm
            off = (self.cur + align - 1) // align * align
            self.cur = off + per_part
            return nc.alloc_sbuf_tensor_at(name, list(shape), dtype, offset=off)

    BASE = (nc.sbuf_base + 63) // 64 * 64
    TOP = nc.sbuf_top
    P = Alloc(BASE)
    s_id_f = P.al("s_id_f", [128, 128], F32)
    s_id_b = P.al("s_id_b", [128, 128], BF16)
    s_lt = P.al("s_lt", [128, 128], F32)
    s_onc_f = P.al("s_onc_f", [128, 1], F32)
    s_onc_b = P.al("s_onc_b", [128, 1], BF16)
    s_onr_f = P.al("s_onr_f", [1, 128], F32)
    s_onr_b = P.al("s_onr_b", [1, 128], BF16)
    s_iota = P.al("s_iota", [128, 1], F32)
    s_gb = [P.al(f"s_gb{i}", [128, E], BF16) for i in range(8)]
    s_gcol = P.al("s_gcol", [1, E], BF16)
    s_bq = P.al("s_bq", [128, 8], F32)
    s_bk = P.al("s_bk", [128, 8], F32)
    s_bo_b = P.al("s_bo_b", [1, D], BF16)
    s_bv_b = P.al("s_bv_b", [1, D], BF16)
    s_w8 = P.al("s_w8", [128, NT * K8], F32)
    s_sgt = P.al("s_sgt", [128, NT], F32)
    s_zero = P.al("s_zero", [128, D], BF16)
    s_cm = P.al("s_cm", [128, 8], F32)
    s_gof_run = P.al("s_gof_run", [1, E], F32)
    s_cst = P.al("s_cst", [128, 8], F32)
    s_tokid = P.al("s_tokid", [128, E], I32)
    s_asg = P.al("s_asg", [128, E], I32)
    s_vld = P.al("s_vld", [128, E], F32)

    ARENA0 = P.cur

    a = Alloc(ARENA0)
    p_xf = [a.al(f"p_xf{i}", [128, 8 * 512], F32) for i in range(2)]
    p_xb = [a.al(f"p_xb{i}", [128, 8 * 512], BF16) for i in range(2)]
    p_nmu = [a.al(f"p_nmu{i}", [1, 512], BF16) for i in range(2)]
    p_t8 = [a.al(f"p_t8{i}", [128, 4 * K8], F32) for i in range(2)]
    p_M = [a.al(f"p_M{i}", [128, 4 * E], F32) for i in range(2)]
    p_cnt = a.al("p_cnt", [1, 4 * E], F32)
    p_ust = a.al("p_ust", [128, NT * E], F32)
    p_gst = a.al("p_gst", [1, NT * E], F32)
    p_xo = [a.al(f"p_xo{i}", [128, D], F32) for i in range(2)]
    p_xm = a.al("p_xm", [128, D], F32)
    p_xn = [a.al(f"p_xn{i}", [128, D], BF16) for i in range(2)]
    p_fmst = a.al("p_fmst", [128, D], BF16)
    p_bn = a.al("p_bn", [128, 12], F32)
    p_mv = a.al("p_mv", [128, 2], F32)
    p_r = a.al("p_r", [128, 1], F32)
    p_lnt = a.al("p_lnt", [128, 1], F32)
    p_t8j = a.al("p_t8j", [128, K8], F32)
    p_Mj = a.al("p_Mj", [128, E], F32)
    p_c1 = a.al("p_c1", [128, E], F32)
    p_sel = a.al("p_sel", [128, E], F32)
    p_oh = a.al("p_oh", [128, E], F32)
    p_Kc = a.al("p_Kc", [128, E], F32)
    p_v2 = a.al("p_v2", [128, E], F32)
    p_tl8 = a.al("p_tl8", [128, 1], F32)
    p_kp8 = a.al("p_kp8", [128, K8], F32)
    p_e8 = a.al("p_e8", [128, K8], F32)
    p_nm8 = a.al("p_nm8", [128, 1], F32)
    p_d8 = a.al("p_d8", [128, 1], F32)
    p_di = a.al("p_di", [128, 1], F32)
    p_vm = a.al("p_vm", [64, TC], F32)
    p_a16 = a.al("p_a16", [64, 128], F32)
    p_der = a.al("p_der", [128, E], F32)
    p_der2 = a.al("p_der2", [128, E], F32)
    P0_top = a.cur

    a = Alloc(ARENA0)
    A_kh = [a.al(f"A_kh{i}", [128, KVL], BF16) for i in range(8)]
    A_vhx = a.al("A_vhx", [128, 16 * NH * 65], BF16)
    s3_base = a.cur
    A_wb = [a.al(f"A_wb{i}", [128, D], BF16) for i in range(8)]
    A_cvt = a.al("A_cvt", [128, D], F32)
    A_kvb = a.al("A_kvb", [128, D], BF16)
    A_kvfm = [a.al(f"A_kvfm{i}", [128, KVL], BF16) for i in range(8)]
    A12_top = a.cur
    a = Alloc(s3_base + 8 * D * 2)
    A_q = [a.al(f"A_q{i}", [128, 512], BF16) for i in range(8)]
    A_P = [a.al(f"A_P{i}", [128, 1024], BF16) for i in range(2)]
    A_ofm = [a.al(f"A_ofm{i}", [128, 512], BF16) for i in range(8)]
    A_otm = [a.al(f"A_otm{i}", [128, D], BF16) for i in range(4)]
    A_stm = a.al("A_stm", [128, D], F32)
    A_drow = a.al("A_drow", [1, 512], F32)
    A_dstack = a.al("A_dstack", [16, 512], F32)
    A_dinv = a.al("A_dinv", [128, NH], F32)
    A_wcvt = a.al("A_wcvt", [128, D], F32)
    A3_top = a.cur

    a = Alloc(ARENA0)
    B_wgu = [a.al(f"B_wgu{i}", [128, 4096], BF16) for i in range(8)]
    B_wd = [a.al(f"B_wd{i}", [128, D], BF16) for i in range(16)]
    B_sgwc = a.al("B_sgwc", [128, 8], BF16)
    B_cvt = a.al("B_cvt", [128, D], F32)
    B_xn = [a.al(f"B_xn{i}", [128, 512], BF16) for i in range(8)]
    B_act = [a.al(f"B_act{i}", [128, 512], BF16) for i in range(16)]
    B_sh = a.al("B_sh", [128, 512], F32)
    B_y = a.al("B_y", [128, D], F32)
    B_sgr = a.al("B_sgr", [1, 512], F32)
    B_top = a.cur

    a = Alloc(ARENA0)
    C_wguf = [a.al(f"C_wguf{i}", [128, 2 * D], F32) for i in range(2)]
    C_wgu = [[a.al(f"C_wgu{p}_{i}", [128, D], BF16) for i in range(8)] for p in range(2)]
    C_wdf = [a.al(f"C_wdf{i}", [128, 2 * D], F32) for i in range(2)]
    C_wd = [[a.al(f"C_wd{p}_{i}", [128, D], BF16) for i in range(4)] for p in range(2)]
    C_eb = [a.al(f"C_eb{p}", [128, D], BF16) for p in range(2)]
    C_ebt = [a.al(f"C_ebt{p}", [128, D], BF16) for p in range(2)]
    C_sh = [a.al(f"C_sh{p}", [128, 512], F32) for p in range(2)]
    C_act = [a.al(f"C_act{p}", [128, 512], BF16) for p in range(2)]
    C_actT = [a.al(f"C_actT{p}", [128, 512], BF16) for p in range(2)]
    C_ebo = [a.al(f"C_ebo{p}", [128, D], BF16) for p in range(2)]
    C_g = [a.al(f"C_g{p}", [128, D], BF16) for p in range(4)]
    C_y = a.al("C_y", [128, D], F32)
    C_x = [a.al(f"C_x{p}", [128, D], F32) for p in range(2)]
    C_sv = [a.al(f"C_sv{p}", [128, D], F32) for p in range(2)]
    C_ysh = [a.al(f"C_ysh{p}", [128, D], F32) for p in range(2)]
    C_ffn = [a.al(f"C_ffn{p}", [128, D], F32) for p in range(2)]
    C_gg = [a.al(f"C_gg{p}", [128, D], F32) for p in range(2)]
    C_out = [a.al(f"C_out{p}", [128, D], F32) for p in range(2)]
    C_top = a.cur
    a = Alloc(ARENA0)
    C_sgs = a.al("C_sgs", [1, TC], F32)

    for nm, top in [("p0", P0_top), ("A12", A12_top), ("A3", A3_top),
                    ("B", B_top), ("C", C_top)]:
        assert top <= TOP, f"{nm} arena exceeds sbuf top: {top} > {TOP}"

    pbA = nc.alloc_psum_tensor("pbA", [128, 1024], F32)
    pbB = nc.alloc_psum_tensor("pbB", [128, 1024], F32)
    pbC = nc.alloc_psum_tensor("pbC", [128, 512], F32)
    pbD = nc.alloc_psum_tensor("pbD", [128, 512], F32)
    pb = [pbA[:, 0:512], pbA[:, 512:1024], pbB[:, 0:512], pbB[:, 512:1024],
          pbC[:], pbD[:]]
    pbh = [nc.alloc_psum_tensor(f"pbh{i}", [128, 1024], BF16) for i in range(2)]

    sems = {n: nc.alloc_semaphore(n)
            for n in ["sx", "so", "pe", "va", "sa", "gd", "gs", "s1",
                      "d0", "d1", "d2", "d3", "g2", "g3", "e0", "e1",
                      "c0", "c1"]}

    CONSTS = [EPS, -0.5, 0.125, -12.0]
    for ci, cv in enumerate(CONSTS):
        nc.const_aps.register(F32, cv, s_cst[:, ci:ci + 1])


    def emit_all(ctx):
        SP = ctx.eng["sp"]; PE = ctx.eng["pe"]; V = ctx.eng["dve"]
        A_ = ctx.eng["act"]; G = ctx.eng["pool"]

        def sp_dma(out, in_, **kw):
            return ctx.inc(SP.dma_start(out=out, in_=in_, **kw), "sx", 16)

        def gp_dma(out, in_, sem="gs", **kw):
            return ctx.inc(G.dma_start(out=out, in_=in_, **kw), sem, 16)

        def sq_dma(sem, out, in_, **kw):
            return ctx.inc(SP.dma_start(out=out, in_=in_, **kw), sem, 16)

        def vop(fn, *deps):
            ctx.wait("dve", "va")
            for s, v in deps:
                ctx.wait("dve", s, v)
            return ctx.inc(fn(V), "va")

        def aop(fn, *deps):
            ctx.wait("act", "sa")
            for s, v in deps:
                ctx.wait("act", s, v)
            return ctx.inc(fn(A_), "sa")

        def pe_w(sem, v=None):
            ctx.wait("pe", sem, v)

        def pe_i(ins):
            return ctx.inc(ins, "pe")

        def bank_w(tag, sem="va"):
            pe_w(sem, ctx.at("bk_" + tag))

        def bank_m(tag, sem="va"):
            ctx.mark("bk_" + tag, sem)

        def barrier():
            for en in ("sp", "pe", "dve", "act", "pool"):
                for s in ("pe", "va", "sa", "sx", "gd", "gs", "so", "s1",
                          "d0", "d1", "d2", "d3", "g2", "g3", "e0", "e1",
                          "c0", "c1"):
                    ctx.wait(en, s)

        def load_wmat(dram, dest, stage):
            for i in range(8):
                ctx.wait("sp", "va", ctx.at("wstage_free"))
                sq_dma("d2", stage[:], dram[i * 128:(i + 1) * 128, :])
                wsx = ctx.cnt["d2"]
                vop(lambda v, i=i: v.tensor_copy(out=dest[i][:], in_=stage[:]),
                    ("d2", wsx))
                ctx.mark("wstage_free", "va")

        sp_dma(s_id_f[:], c_id_f.ap())
        sp_dma(s_id_b[:], c_id_b.ap())
        sp_dma(s_lt[:], c_lt.ap())
        sp_dma(s_onc_f[:], c_onc_f.ap())
        sp_dma(s_onc_b[:], c_onc_b.ap())
        sp_dma(s_onr_f[:], c_onr_f.ap())
        sp_dma(s_onr_b[:], c_onr_b.ap())
        sp_dma(s_iota[:], c_iota.ap())
        sp_dma(s_cm[:], d_cm[:])
        sq_dma("d3", p_xf[0][:, 0:512].rearrange("p (c e) -> p c e", c=8),
               d_gw.rearrange("(c p) e -> p c e", p=128))
        ld0 = ctx.cnt["d3"]
        for i in range(8):
            vop(lambda v, i=i: v.tensor_copy(out=s_gb[i][:],
                                             in_=p_xf[0][:, i * 64:(i + 1) * 64]),
                ("d3", ld0))
        sp_dma(s_bq[:].rearrange("p (c o) -> p c o", o=1),
               d_bq.rearrange("o (c p) -> p c o", p=128),
               allow_slow_non_contiguous=True)
        sp_dma(s_bk[:].rearrange("p (c o) -> p c o", o=1),
               d_bk.rearrange("o (c p) -> p c o", p=128),
               allow_slow_non_contiguous=True)
        sq_dma("g3", p_xf[1][0:1, 0:1024], d_bo[:])
        sq_dma("g3", p_xf[1][0:1, 1024:2048], d_bv[:])
        ld1 = ctx.cnt["g3"]
        vop(lambda v: v.tensor_copy(out=s_bo_b[:], in_=p_xf[1][0:1, 0:1024]), ("g3", ld1))
        vop(lambda v: v.tensor_copy(out=s_bv_b[:], in_=p_xf[1][0:1, 1024:2048]), ("g3", ld1))
        vop(lambda v: v.memset(s_zero[:], 0.0))
        vop(lambda v: v.memset(s_gof_run[:], 0.0))
        vop(lambda v: v.memset(p_ust[:], 0.0))
        vop(lambda v: v.memset(p_gst[:], 0.0))
        for ci, cv in enumerate(CONSTS):
            vop(lambda v, ci=ci, cv=cv: v.memset(s_cst[:, ci:ci + 1], cv))
        z0 = ctx.cnt["va"]
        ctx.wait("sp", "va", z0)
        for rr in range(YBROWS // 128):
            sq_dma("g2", d_yb[rr * 128:(rr + 1) * 128, :], s_zero[:])
        ctx.mark("ybzero", "g2")
        pe_w("va", z0)
        for i in range(8):
            pe_i(PE.matmul(pb[0][0:1, 0:E], s_onc_b[:], s_gb[i][:],
                           start=(i == 0), stop=(i == 7)))
        vop(lambda v: v.tensor_copy(out=s_gcol[:], in_=pb[0][0:1, 0:E]),
            ("pe", ctx.cnt["pe"]))
        barrier()

        NB0 = GT // 4
        ldb_d = {}; cvtv_d = {}; cvta_d = {}; mupe_d = {}; nmuv_d = {}
        upe_d = {}; mv_d = {}; cntpe_d = {}; cntfree_d = {-1: 0}

        def L0a(b):
            pbu = b % 2
            if b - 2 >= 0:
                ctx.wait("sp", "pe", upe_d[b - 2])
                ctx.wait("sp", "va", cvtv_d[b - 2])
                ctx.wait("sp", "sa", cvta_d[b - 2])
            sq_dma(f"d{pbu}", p_xf[pbu][:].rearrange("p (c t) -> p c t", c=8),
                   d_x_fm.rearrange("(c p) t -> p c t", p=128)
                   [:, :, b * 512:(b + 1) * 512])
            ldb_d[b] = ctx.cnt[f"d{pbu}"]

        def C0a(b):
            pbu = b % 2
            for i in range(4):
                vop(lambda v, i=i: v.tensor_copy(
                    out=p_xb[pbu][:, i * 512:(i + 1) * 512],
                    in_=p_xf[pbu][:, i * 512:(i + 1) * 512]),
                    (f"d{pbu}", ldb_d[b]))
            for i in range(4, 8):
                aop(lambda s, i=i: s.copy(
                    out=p_xb[pbu][:, i * 512:(i + 1) * 512],
                    in_=p_xf[pbu][:, i * 512:(i + 1) * 512]),
                    (f"d{pbu}", ldb_d[b]))
            cvtv_d[b] = ctx.cnt["va"]; cvta_d[b] = ctx.cnt["sa"]

        def MU0a(b):
            pbu = b % 2
            pe_w("va", cvtv_d[b]); pe_w("sa", cvta_d[b])
            if b - 1 >= 0:
                pe_w("va", nmuv_d[b - 1])
            for i in range(8):
                pe_i(PE.matmul(pb[5][0:1, 0:512], s_onc_b[:],
                               p_xb[pbu][:, i * 512:(i + 1) * 512],
                               start=(i == 0), stop=(i == 7)))
            mupe_d[b] = ctx.cnt["pe"]

        def NMU0a(b):
            pbu = b % 2
            vop(lambda v: v.tensor_scalar_mul(p_nmu[pbu][:], pb[5][0:1, 0:512],
                                              -1.0 / D), ("pe", mupe_d[b]))
            nmuv_d[b] = ctx.cnt["va"]

        def U0a(b):
            pbu = b % 2
            ub = pb[0] if b % 2 == 0 else pb[3]
            pe_w("va", nmuv_d[b])
            if b - 2 >= 0:
                pe_w("va", mv_d[b - 2])
            for ti in range(4):
                for i in range(8):
                    pe_i(PE.matmul(
                        ub[0:128, ti * E:(ti + 1) * E],
                        p_xb[pbu][:, i * 512 + ti * 128:i * 512 + (ti + 1) * 128],
                        s_gb[i][:], start=(i == 0), stop=False))
                pe_i(PE.matmul(ub[0:128, ti * E:(ti + 1) * E],
                               p_nmu[pbu][0:1, ti * 128:(ti + 1) * 128],
                               s_gcol[:], start=False, stop=True))
            upe_d[b] = ctx.cnt["pe"]

        def D10a(b):
            ub = pb[0] if b % 2 == 0 else pb[3]
            pu = b % 2
            for ti in range(4):
                vop(lambda v, ub=ub, ti=ti, pu=pu: v.max(
                    out=p_t8[pu][:, ti * K8:(ti + 1) * K8],
                    in_=ub[0:128, ti * E:(ti + 1) * E]),
                    *([("pe", upe_d[b])] if ti == 0 else []))
                vop(lambda v, ub=ub, ti=ti, pu=pu: v.tensor_scalar(
                    out=p_M[pu][:, ti * E:(ti + 1) * E],
                    in0=ub[0:128, ti * E:(ti + 1) * E],
                    scalar1=p_t8[pu][:, ti * K8 + 7:ti * K8 + 8],
                    scalar2=None, op0=ALU.is_ge))
                g = 4 * b + ti
                j = g // 8
                r = g % 8
                vop(lambda v, ub=ub, ti=ti, j=j, r=r: v.scalar_tensor_tensor(
                    out=p_ust[:, j * E:(j + 1) * E],
                    in0=ub[0:128, ti * E:(ti + 1) * E],
                    scalar=s_cm[:, r:r + 1], in1=p_ust[:, j * E:(j + 1) * E],
                    op0=ALU.mult, op1=ALU.add))
            mv_d[b] = ctx.cnt["va"]

        def K0a(b):
            pu = b % 2
            pe_w("va", mv_d[b])
            pe_w("va", cntfree_d[b - 1])
            for ti in range(4):
                pe_i(PE.matmul(pb[1][0:1, ti * E:(ti + 1) * E], s_onc_f[:],
                               p_M[pu][:, ti * E:(ti + 1) * E],
                               start=True, stop=True))
            cntpe_d[b] = ctx.cnt["pe"]

        def D20a(b):
            vop(lambda v: v.tensor_copy(out=p_cnt[:], in_=pb[1][0:1, 0:4 * E]),
                ("pe", cntpe_d[b]))
            cntfree_d[b] = ctx.cnt["va"]
            for ti in range(4):
                g = 4 * b + ti
                j = g // 8
                r = g % 8
                vop(lambda v, j=j, r=r: v.scalar_tensor_tensor(
                    out=p_gst[0:1, j * E:(j + 1) * E], in0=s_gof_run[:],
                    scalar=s_cm[0:1, r:r + 1], in1=p_gst[0:1, j * E:(j + 1) * E],
                    op0=ALU.mult, op1=ALU.add))
                vop(lambda v, ti=ti: v.tensor_add(
                    out=s_gof_run[:], in0=s_gof_run[:],
                    in1=p_cnt[0:1, ti * E:(ti + 1) * E]))

        L0a(0)
        for i in range(NB0 + 1):
            if i + 1 < NB0:
                L0a(i + 1)
            if i < NB0:
                C0a(i)
                MU0a(i)
            if i - 1 >= 0:
                D10a(i - 1)
            if i < NB0:
                NMU0a(i)
                U0a(i)
            if i - 1 >= 0:
                K0a(i - 1)
                D20a(i - 1)
        barrier()

        for j in range(NT):
            pj = j % 2
            ctx.wait("sp", "va", ctx.at(f"oc_xo_{j - 2}"))
            sq_dma(f"d{pj}", p_xo[pj][:], d_x_own[j * 128:(j + 1) * 128, :])
            xo_sx = ctx.cnt[f"d{pj}"]
            vop(lambda v: v.bn_stats(out=p_bn[:, 0:6], in_=p_xo[pj][:, 0:512]),
                (f"d{pj}", xo_sx))
            vop(lambda v: v.bn_stats(out=p_bn[:, 6:12], in_=p_xo[pj][:, 512:1024]))
            vop(lambda v: v.bn_aggr(out=p_mv[:],
                                    in_=p_bn[:].rearrange("p (a b) -> p a b", a=2)),
                ("sa", ctx.at("mv_free")))
            mv_v = ctx.cnt["va"]
            aop(lambda s: s.activation(out=p_lnt[:], in_=p_mv[:, 1:2], func=AF.Ln,
                                       bias=EPS), ("va", mv_v))
            aop(lambda s: s.activation(out=p_r[:], in_=p_lnt[:], func=AF.Exp,
                                       scale=-0.5))
            r_sa = ctx.cnt["sa"]
            ctx.mark("mv_free", "sa")
            vop(lambda v: v.tensor_scalar(out=p_xm[:], in0=p_xo[pj][:],
                                          scalar1=p_mv[:, 0:1], scalar2=None,
                                          op0=ALU.subtract))
            vop(lambda v: v.tensor_scalar(out=p_xn[pj][:], in0=p_xm[:],
                                          scalar1=p_r[:], scalar2=None,
                                          op0=ALU.mult),
                ("sa", r_sa), (f"e{pj}", ctx.at(f"xn_st_{j - 2}")))
            xn_v = ctx.cnt["va"]
            ctx.mark(f"oc_xo_{j}", "va")
            ctx.wait("pool", "va", xn_v)
            gp_dma(d_xn[j * 128:(j + 1) * 128, :], p_xn[pj][:], sem=f"e{pj}")
            ctx.mark(f"xn_st_{j}", f"e{pj}")
            for i in range(8):
                pe_w("va", xn_v)
                bank_w("pbh0")
                pe_i(PE.transpose(pbh[0][0:128, 0:128],
                                  p_xn[pj][:, i * 128:(i + 1) * 128], s_id_b[:]))
                tp_pe = ctx.cnt["pe"]
                vop(lambda v, i=i: v.tensor_copy(
                    out=p_fmst[:, i * 128:(i + 1) * 128],
                    in_=pbh[0][0:128, 0:128]),
                    ("pe", tp_pe), ("c0", ctx.at(f"xnfm_st_{j - 1}")))
                bank_m("pbh0")
            fm_v = ctx.cnt["va"]
            ctx.wait("pool", "va", fm_v)
            gp_dma(d_xnfm.rearrange("(c p) t -> p c t", p=128)
                   [:, :, j * 128:(j + 1) * 128],
                   p_fmst[:].rearrange("p (c t) -> p c t", c=8), sem="c0")
            ctx.mark(f"xnfm_st_{j}", "c0")
            uj = p_ust[:, j * E:(j + 1) * E]
            vop(lambda v, uj=uj: v.max(out=p_t8j[:], in_=uj),
                ("sa", ctx.at("t8j_free")))
            vop(lambda v, uj=uj: v.tensor_scalar(out=p_Mj[:], in0=uj,
                                                 scalar1=p_t8j[:, 7:8], scalar2=None,
                                                 op0=ALU.is_ge))
            mj_v = ctx.cnt["va"]
            pe_w("va", mj_v)
            bank_w("ocp2")
            pe_i(PE.matmul(pb[2][0:128, 0:E], s_lt[:], p_Mj[:], start=True, stop=False))
            pe_i(PE.matmul(pb[2][0:128, 0:E], s_onr_f[:],
                           p_gst[0:1, j * E:(j + 1) * E], start=False, stop=True))
            pos_pe = ctx.cnt["pe"]
            vop(lambda v: v.tensor_scalar(out=p_c1[:], in0=pb[2][0:128, 0:E],
                                          scalar1=CAP - 0.5, scalar2=None,
                                          op0=ALU.is_le), ("pe", pos_pe))
            bank_m("ocp2")
            vop(lambda v: v.tensor_mul(out=p_sel[:], in0=p_c1[:], in1=p_Mj[:]))
            vop(lambda v: v.memset(p_Kc[:], 0.0))
            for k in range(K8):
                vop(lambda v, k=k, uj=uj: v.tensor_scalar(
                    out=p_oh[:], in0=uj, scalar1=p_t8j[:, k:k + 1],
                    scalar2=None, op0=ALU.is_equal))
                vop(lambda v, k=k: v.scalar_tensor_tensor(
                    out=p_v2[:], in0=p_oh[:], scalar=1.0, in1=p_sel[:],
                    op0=ALU.mult, op1=ALU.mult, accum_out=p_kp8[:, k:k + 1]))
                if k > 0:
                    vop(lambda v, k=k: v.scalar_tensor_tensor(
                        out=p_Kc[:], in0=p_oh[:], scalar=float(k), in1=p_Kc[:],
                        op0=ALU.mult, op1=ALU.add))
            vop(lambda v: v.scalar_tensor_tensor(out=p_nm8[:], in0=p_t8j[:, 0:1],
                                                 scalar=-1.0, in1=p_r[:],
                                                 op0=ALU.mult, op1=ALU.mult))
            nm_v = ctx.cnt["va"]
            aop(lambda s: s.activation(out=p_e8[:], in_=p_t8j[:, 0:K8], func=AF.Exp,
                                       bias=p_nm8[:], scale=p_r[:]), ("va", nm_v))
            e8_sa = ctx.cnt["sa"]
            ctx.mark("t8j_free", "sa")
            vop(lambda v: v.reduce_sum(out=p_d8[:], in_=p_e8[:], axis=AXX),
                ("sa", e8_sa))
            vop(lambda v: v.reciprocal(out=p_di[:], in_=p_d8[:]))
            vop(lambda v: v.scalar_tensor_tensor(out=p_e8[:], in0=p_e8[:],
                                                 scalar=p_di[:], in1=p_kp8[:],
                                                 op0=ALU.mult, op1=ALU.mult))
            vop(lambda v, j=j: v.tensor_copy(out=s_w8[:, j * K8:(j + 1) * K8],
                                             in_=p_e8[:]))
            vop(lambda v, j=j: v.tensor_scalar(out=p_tl8[:], in0=s_iota[:],
                                               scalar1=8.0, scalar2=float(j * 1024),
                                               op0=ALU.mult, op1=ALU.add))
            vop(lambda v: v.tensor_scalar(out=p_v2[:], in0=p_Kc[:],
                                          scalar1=p_tl8[:], scalar2=None,
                                          op0=ALU.add))
            vop(lambda v: v.tensor_scalar(out=p_v2[:], in0=p_v2[:], scalar1=-1.0,
                                          scalar2=PAY, op0=ALU.mult, op1=ALU.add))
            vop(lambda v: v.tensor_mul(out=p_v2[:], in0=p_v2[:], in1=p_sel[:]))
            v2_v = ctx.cnt["va"]
            pe_w("va", v2_v)
            bank_w("ocp4")
            pe_i(PE.transpose(pb[4][0:64, 0:128], p_v2[:], s_id_f[:]))
            vt_pe = ctx.cnt["pe"]
            vop(lambda v, j=j: v.tensor_copy(
                out=p_vm[:, j * 128:(j + 1) * 128], in_=pb[4][0:64, 0:128]),
                ("pe", vt_pe))
            bank_m("ocp4")
        barrier()

        for rd in range(16):
            vop(lambda v, rd=rd: v.max(out=p_a16[:, rd * 8:(rd + 1) * 8],
                                       in_=p_vm[:]))
            vop(lambda v, rd=rd: v.match_replace(
                out=p_vm[:], in_to_replace=p_a16[:, rd * 8:(rd + 1) * 8],
                in_values=p_vm[:], imm_value=0.0))
        x16_v = ctx.cnt["va"]
        pe_w("va", x16_v)
        bank_w("ocp4")
        pe_i(PE.transpose(pb[4][0:128, 0:64], p_a16[:], s_id_f[0:64, 0:64]))
        a16_pe = ctx.cnt["pe"]
        vop(lambda v: v.tensor_copy(out=p_der[:], in_=pb[4][0:128, 0:64]),
            ("pe", a16_pe))
        bank_m("ocp4")
        vop(lambda v: v.tensor_scalar(out=s_vld[:, 0:E], in0=p_der[:], scalar1=0.5,
                                      scalar2=None, op0=ALU.is_gt))
        vop(lambda v: v.tensor_scalar(out=p_der2[:], in0=p_der[:], scalar1=-1.0,
                                      scalar2=PAY, op0=ALU.mult, op1=ALU.add))
        vop(lambda v: v.tensor_mul(out=p_der[:], in0=p_der2[:], in1=s_vld[:, 0:E]))
        vop(lambda v: v.tensor_scalar(out=p_oh[:], in0=s_vld[:, 0:E],
                                      scalar1=-DUMMY_ASG, scalar2=DUMMY_ASG,
                                      op0=ALU.mult, op1=ALU.add))
        vop(lambda v: v.tensor_add(out=p_der[:], in0=p_der[:], in1=p_oh[:]))
        vop(lambda v: v.tensor_copy(out=s_asg[:, 0:E], in_=p_der[:]))
        vop(lambda v: v.tensor_mul(out=p_der2[:], in0=p_der2[:], in1=s_vld[:, 0:E]))
        vop(lambda v: v.tensor_copy(out=s_tokid[:, 0:E], in_=p_der2[:]))
        vop(lambda v: v.tensor_scalar(out=s_tokid[:, 0:E], in0=s_tokid[:, 0:E],
                                      scalar1=3, scalar2=None,
                                      op0=ALU.arith_shift_right))
        barrier()

        load_wmat(d_wq, A_wb, A_cvt)
        wq_v = ctx.cnt["va"]
        for b in range(NB):
            ctx.wait("sp", "pe", ctx.at(f"qblk_{b - 1}"))
            ctx.wait("sp", "va", wq_v)
            for i in range(8):
                sp_dma(A_q[i][:], d_xnfm[i * 128:(i + 1) * 128,
                                         b * 512:(b + 1) * 512])
            xnb_sx = ctx.cnt["sx"]
            for oc in range(8):
                bank = pb[oc % 2]
                pe_w("va", wq_v)
                pe_w("sx", xnb_sx)
                bank_w(f"q{oc % 2}")
                for i in range(8):
                    pe_i(PE.matmul(bank[0:128, 0:512],
                                   A_wb[i][:, oc * 128:(oc + 1) * 128],
                                   A_q[i][:], start=(i == 0), stop=(i == 7)))
                q_pe = ctx.cnt["pe"]
                vop(lambda v, oc=oc, bank=bank: v.tensor_scalar(
                    out=A_P[oc % 2][:, 0:512], in0=bank[0:128, 0:512],
                    scalar1=s_bq[:, oc:oc + 1], scalar2=None, op0=ALU.add),
                    ("pe", q_pe), (f"e{oc % 2}", ctx.at(f"q_st_{b * 8 + oc - 2}")))
                qe_v = ctx.cnt["va"]
                bank_m(f"q{oc % 2}")
                ctx.wait("pool", "va", qe_v)
                ctx.inc(G.dma_start(
                    out=d_qs[(b * 8 + oc) * 128:(b * 8 + oc + 1) * 128, :],
                    in_=A_P[oc % 2][:, 0:512]), f"e{oc % 2}", 16)
                ctx.mark(f"q_st_{b * 8 + oc}", f"e{oc % 2}")
            ctx.mark(f"qblk_{b}", "pe")
        barrier()

        load_wmat(d_wk, A_wb, A_cvt)
        wk_v = ctx.cnt["va"]
        for kt in range(16):
            ctx.wait("sp", "va", max(ctx.at("acvt_free"), wk_v))
            sp_dma(A_cvt[:], d_k[kt * 128:(kt + 1) * 128, :])
            k_sx = ctx.cnt["sx"]
            vop(lambda v: v.tensor_copy(out=A_kvb[:], in_=A_cvt[:]),
                ("sx", k_sx), ("pe", ctx.at("kvb_readers")))
            ctx.mark("acvt_free", "va")
            kb_v = ctx.cnt["va"]
            for i in range(8):
                pe_w("va", kb_v)
                bank_w("pbh0")
                pe_i(PE.transpose(pbh[0][0:128, 0:128],
                                  A_kvb[:, i * 128:(i + 1) * 128], s_id_b[:]))
                vop(lambda v, i=i, kt=kt: v.tensor_copy(
                    out=A_kvfm[i][:, kt * 128:(kt + 1) * 128],
                    in_=pbh[0][0:128, 0:128]), ("pe", ctx.cnt["pe"]))
                bank_m("pbh0")
            ctx.mark("kvb_readers", "pe")
        kfm_v = ctx.cnt["va"]
        for oc in range(8):
            for sl in range(4):
                bank = pb[(oc * 4 + sl) % 2]
                pe_w("va", kfm_v)
                bank_w(f"kh{(oc * 4 + sl) % 2}")
                for i in range(8):
                    pe_i(PE.matmul(bank[0:128, 0:512],
                                   A_wb[i][:, oc * 128:(oc + 1) * 128],
                                   A_kvfm[i][:, sl * 512:(sl + 1) * 512],
                                   start=(i == 0), stop=(i == 7)))
                kh_pe = ctx.cnt["pe"]
                vop(lambda v, oc=oc, sl=sl, bank=bank: v.tensor_scalar(
                    out=A_kh[oc][:, sl * 512:(sl + 1) * 512], in0=bank[0:128, 0:512],
                    scalar1=s_bk[:, oc:oc + 1], scalar2=None, op0=ALU.add),
                    ("pe", kh_pe))
                bank_m(f"kh{(oc * 4 + sl) % 2}")
        barrier()
        load_wmat(d_wv, A_wb, A_cvt)
        wv_v = ctx.cnt["va"]
        for kt in range(16):
            ctx.wait("sp", "va", max(ctx.at("acvt_free"), wv_v))
            sp_dma(A_cvt[:], d_v[kt * 128:(kt + 1) * 128, :])
            v_sx = ctx.cnt["sx"]
            vop(lambda v: v.tensor_copy(out=A_kvb[:], in_=A_cvt[:]),
                ("sx", v_sx), ("pe", ctx.at("kvb_readers")))
            ctx.mark("acvt_free", "va")
            vb_v = ctx.cnt["va"]
            for i in range(8):
                pe_w("va", vb_v)
                bank_w("pbh0")
                pe_i(PE.transpose(pbh[0][0:128, 0:128],
                                  A_kvb[:, i * 128:(i + 1) * 128], s_id_b[:]))
                vop(lambda v, i=i, kt=kt: v.tensor_copy(
                    out=A_kvfm[i][:, kt * 128:(kt + 1) * 128],
                    in_=pbh[0][0:128, 0:128]), ("pe", ctx.cnt["pe"]))
                bank_m("pbh0")
            ctx.mark("kvb_readers", "pe")
        vfm_v = ctx.cnt["va"]
        for kt in range(16):
            vop(lambda v, kt=kt: v.memset(
                A_vhx[:, kt * NH * 65 + 64:(kt + 1) * NH * 65:65], 1.0))
        ones_v = ctx.cnt["va"]
        for kt in range(16):
            for ns in range(2):
                bank = pb[(kt * 2 + ns) % 2]
                pe_w("va", max(vfm_v, ones_v, wv_v))
                bank_w(f"vh{(kt * 2 + ns) % 2}")
                for i in range(8):
                    pe_i(PE.matmul(bank[0:128, 0:512],
                                   A_kvfm[i][:, kt * 128:(kt + 1) * 128],
                                   A_wb[i][:, ns * 512:(ns + 1) * 512],
                                   start=(i == 0), stop=False))
                pe_i(PE.matmul(bank[0:128, 0:512], s_onr_b[:],
                               s_bv_b[0:1, ns * 512:(ns + 1) * 512],
                               start=False, stop=True))
                vh_pe = ctx.cnt["pe"]
                for hh in range(8):
                    h = ns * 8 + hh
                    vop(lambda v, kt=kt, h=h, hh=hh, bank=bank: v.tensor_copy(
                        out=A_vhx[:, (kt * NH + h) * 65:(kt * NH + h) * 65 + 64],
                        in_=bank[0:128, hh * 64:(hh + 1) * 64]), ("pe", vh_pe))
                bank_m(f"vh{(kt * 2 + ns) % 2}")
        barrier()
        load_wmat(d_wo, A_wb, A_wcvt)
        wo_v = ctx.cnt["va"]
        for b in range(NB):
            ctx.wait("sp", "pe", ctx.at(f"attn_blk_{b - 1}"))
            for oc in range(8):
                sp_dma(A_q[oc][:], d_qs[(b * 8 + oc) * 128:(b * 8 + oc + 1) * 128, :])
            q_sx = ctx.cnt["sx"]
            for hp in range(NH // 2):
                ha, hb = 2 * hp, 2 * hp + 1
                ch = hp
                if hp == 0:
                    bank_w("op0")
                    bank_w("op1")
                    bank_w("S1v")
                    bank_w("ocp2v")
                for k2 in range(8):
                    kt0, kt1 = 2 * k2, 2 * k2 + 1
                    pe_w("sx", q_sx)
                    bank_w("S0", "sa")
                    pe_i(PE.matmul(pbA[:, 0:512],
                                   A_kh[ch][0:64, kt0 * 128:(kt0 + 1) * 128],
                                   A_q[ch][0:64, 0:512], start=True, stop=True))
                    pe_i(PE.matmul(pbA[:, 512:1024],
                                   A_kh[ch][0:64, kt1 * 128:(kt1 + 1) * 128],
                                   A_q[ch][0:64, 0:512], start=True, stop=True))
                    sA = ctx.cnt["pe"]
                    bank_w("S1", "sa")
                    pe_i(PE.matmul(pbB[:, 0:512],
                                   A_kh[ch][64:128, kt0 * 128:(kt0 + 1) * 128],
                                   A_q[ch][64:128, 0:512], start=True, stop=True))
                    pe_i(PE.matmul(pbB[:, 512:1024],
                                   A_kh[ch][64:128, kt1 * 128:(kt1 + 1) * 128],
                                   A_q[ch][64:128, 0:512], start=True, stop=True))
                    sB = ctx.cnt["pe"]
                    aop(lambda s: s.activation(
                        out=A_P[0][:], in_=pbA[:, 0:1024], func=AF.Exp,
                        bias=-12.0, scale=0.125),
                        ("pe", sA), ("pe", ctx.at("aVa")))
                    bank_m("S0", "sa")
                    expA = ctx.cnt["sa"]
                    aop(lambda s: s.activation(
                        out=A_P[1][:], in_=pbB[:, 0:1024], func=AF.Exp,
                        bias=-12.0, scale=0.125),
                        ("pe", sB), ("pe", ctx.at("aVb")))
                    bank_m("S1", "sa")
                    expB = ctx.cnt["sa"]
                    pe_w("sa", expA)
                    if k2 == 0:
                        bank_w("oaccA")
                    pe_i(PE.matmul(pbC[0:65, 0:512],
                                   A_vhx[:, (kt0 * NH + ha) * 65:(kt0 * NH + ha + 1) * 65],
                                   A_P[0][:, 0:512], start=(k2 == 0), stop=False))
                    pe_i(PE.matmul(pbC[0:65, 0:512],
                                   A_vhx[:, (kt1 * NH + ha) * 65:(kt1 * NH + ha + 1) * 65],
                                   A_P[0][:, 512:1024], start=False, stop=(k2 == 7)))
                    ctx.mark("aVa", "pe")
                    pe_w("sa", expB)
                    if k2 == 0:
                        bank_w("oaccB")
                    pe_i(PE.matmul(pbD[0:65, 0:512],
                                   A_vhx[:, (kt0 * NH + hb) * 65:(kt0 * NH + hb + 1) * 65],
                                   A_P[1][:, 0:512], start=(k2 == 0), stop=False))
                    pe_i(PE.matmul(pbD[0:65, 0:512],
                                   A_vhx[:, (kt1 * NH + hb) * 65:(kt1 * NH + hb + 1) * 65],
                                   A_P[1][:, 512:1024], start=False, stop=(k2 == 7)))
                    ctx.mark("aVb", "pe")
                av_pe = ctx.cnt["pe"]
                vop(lambda v, ch=ch: v.tensor_copy(
                    out=A_ofm[ch][0:64, 0:512], in_=pbC[0:64, 0:512]),
                    ("pe", av_pe))
                vop(lambda v: v.tensor_copy(out=A_drow[:], in_=pbC[64:65, 0:512]),
                    ("gd", ctx.at("drow_free")))
                dr_v = ctx.cnt["va"]
                bank_m("oaccA")
                ctx.wait("pool", "va", dr_v)
                ctx.wait("pool", "pe", ctx.at(f"dstack_rd_{b - 1}"))
                ctx.inc(G.dma_start(out=A_dstack[ha:ha + 1, :], in_=A_drow[:]),
                        "gd", 16)
                ctx.mark("drow_free", "gd")
                vop(lambda v, ch=ch: v.tensor_copy(
                    out=A_ofm[ch][64:128, 0:512], in_=pbD[0:64, 0:512]))
                vop(lambda v: v.tensor_copy(out=A_drow[:], in_=pbD[64:65, 0:512]),
                    ("gd", ctx.at("drow_free")))
                dr_v = ctx.cnt["va"]
                bank_m("oaccB")
                ctx.wait("pool", "va", dr_v)
                ctx.inc(G.dma_start(out=A_dstack[hb:hb + 1, :], in_=A_drow[:]),
                        "gd", 16)
                ctx.mark("drow_free", "gd")
            dstack_gd = ctx.cnt["gd"]
            for tt in range(4):
                pe_w("gd", dstack_gd)
                bank_w("dD")
                bank_w("S1", "sa")
                bank_w("S1v")
                pe_i(PE.transpose(pbB[0:128, 0:16],
                                  A_dstack[0:16, tt * 128:(tt + 1) * 128],
                                  s_id_f[0:16, 0:16]))
                dt_pe = ctx.cnt["pe"]
                vop(lambda v: v.reciprocal(out=A_dinv[:], in_=pbB[0:128, 0:16]),
                    ("pe", dt_pe))
                bank_m("dD")
                bank_m("S1v")
                di_v = ctx.cnt["va"]
                for h in range(NH):
                    ch = h // 2
                    r0 = (h % 2) * 64
                    pe_w("va", di_v)
                    bank_w("pbh0")
                    pe_i(PE.transpose(pbh[0][0:128, 0:64],
                                      A_ofm[ch][r0:r0 + 64, tt * 128:(tt + 1) * 128],
                                      s_id_b[r0:r0 + 64, r0:r0 + 64]))
                    ot_pe = ctx.cnt["pe"]
                    vop(lambda v, tt=tt, h=h: v.tensor_scalar(
                        out=A_otm[tt][:, h * 64:(h + 1) * 64],
                        in0=pbh[0][0:128, 0:64],
                        scalar1=A_dinv[:, h:h + 1], scalar2=None, op0=ALU.mult),
                        ("pe", ot_pe))
                    bank_m("pbh0")
            ctx.mark(f"dstack_rd_{b}", "pe")
            otm_v = ctx.cnt["va"]
            for tt in range(4):
                for i in range(8):
                    pe_w("va", otm_v)
                    bank_w("pbh1")
                    pe_i(PE.transpose(pbh[1][0:128, 0:128],
                                      A_otm[tt][:, i * 128:(i + 1) * 128], s_id_b[:]))
                    vop(lambda v, i=i, tt=tt: v.tensor_copy(
                        out=A_ofm[i][:, tt * 128:(tt + 1) * 128],
                        in_=pbh[1][0:128, 0:128]), ("pe", ctx.cnt["pe"]))
                    bank_m("pbh1")
            onfm_v = ctx.cnt["va"]
            for tt in range(4):
                for ns in range(2):
                    bank = pbA[:, ns * 512:(ns + 1) * 512]
                    pe_w("va", max(onfm_v, wo_v))
                    bank_w(f"op{ns}")
                    if tt == 0 and ns == 0:
                        bank_w("S0", "sa")
                    for i in range(8):
                        pe_i(PE.matmul(bank[0:128, 0:512],
                                       A_ofm[i][:, tt * 128:(tt + 1) * 128],
                                       A_wb[i][:, ns * 512:(ns + 1) * 512],
                                       start=(i == 0), stop=False))
                    pe_i(PE.matmul(bank[0:128, 0:512], s_onr_b[:],
                                   s_bo_b[0:1, ns * 512:(ns + 1) * 512],
                                   start=False, stop=True))
                    o_pe = ctx.cnt["pe"]
                    vop(lambda v, ns=ns, bank=bank: v.tensor_copy(
                        out=A_stm[:, ns * 512:(ns + 1) * 512], in_=bank[0:128, 0:512]),
                        ("pe", o_pe), ("gs", ctx.at("stm_free")))
                    bank_m(f"op{ns}")
                sev = ctx.cnt["va"]
                ctx.wait("pool", "va", sev)
                gp_dma(d_s[(b * 4 + tt) * 128:(b * 4 + tt + 1) * 128, :], A_stm[:])
                ctx.mark("stm_free", "gs")
            ctx.mark(f"attn_blk_{b}", "pe")
        barrier()

        for i in range(8):
            for q4 in range(4):
                ctx.wait("sp", "va", ctx.at("bstage_free"))
                sq_dma("d2", B_cvt[:], d_sguw[i * 128:(i + 1) * 128,
                                              q4 * 1024:(q4 + 1) * 1024])
                wsx = ctx.cnt["d2"]
                vop(lambda v, i=i, q4=q4: v.tensor_copy(
                    out=B_wgu[i][:, q4 * 1024:(q4 + 1) * 1024], in_=B_cvt[:]),
                    ("d2", wsx))
                ctx.mark("bstage_free", "va")
        for i in range(16):
            ctx.wait("sp", "va", ctx.at("bstage_free"))
            sq_dma("d2", B_cvt[:], d_sdw[i * 128:(i + 1) * 128, :])
            wsx = ctx.cnt["d2"]
            vop(lambda v, i=i: v.tensor_copy(out=B_wd[i][:], in_=B_cvt[:]),
                ("d2", wsx))
            ctx.mark("bstage_free", "va")
        ctx.wait("sp", "va", ctx.at("bstage_free"))
        sq_dma("d2", B_cvt[:, 0:8].rearrange("p (c o) -> p c o", o=1),
               d_sgw.rearrange("(c p) o -> p c o", p=128),
               allow_slow_non_contiguous=True)
        sg_sx = ctx.cnt["d2"]
        vop(lambda v: v.tensor_copy(out=B_sgwc[:], in_=B_cvt[:, 0:8]), ("d2", sg_sx))
        bw_v = ctx.cnt["va"]
        for b in range(NB):
            ctx.wait("sp", "pe", ctx.at(f"bblk_{b - 1}"))
            for i in range(8):
                sp_dma(B_xn[i][:], d_xnfm[i * 128:(i + 1) * 128,
                                          b * 512:(b + 1) * 512])
            xnb_sx = ctx.cnt["sx"]
            for jj in range(16):
                bk0 = pb[(jj % 2) * 2]
                bk1 = pb[(jj % 2) * 2 + 1]
                pe_w("va", bw_v)
                pe_w("sx", xnb_sx)
                bank_w(f"bg{jj % 2}", "sa")
                bank_w(f"bu{jj % 2}")
                for i in range(8):
                    pe_i(PE.matmul(bk0[0:128, 0:512],
                                   B_wgu[i][:, jj * 128:(jj + 1) * 128],
                                   B_xn[i][:], start=(i == 0), stop=(i == 7)))
                for i in range(8):
                    pe_i(PE.matmul(bk1[0:128, 0:512],
                                   B_wgu[i][:, 2048 + jj * 128:2048 + (jj + 1) * 128],
                                   B_xn[i][:], start=(i == 0), stop=(i == 7)))
                gu_pe = ctx.cnt["pe"]
                aop(lambda s, bk0=bk0: s.activation(out=B_sh[:], in_=bk0[0:128, 0:512],
                                                    func=AF.Silu),
                    ("pe", gu_pe), ("va", ctx.at("bsh_free")))
                sh_sa = ctx.cnt["sa"]
                bank_m(f"bg{jj % 2}", "sa")
                vop(lambda v, jj=jj, bk1=bk1: v.tensor_mul(
                    out=B_act[jj][:], in0=B_sh[:], in1=bk1[0:128, 0:512]),
                    ("sa", sh_sa), ("pe", gu_pe))
                bank_m(f"bu{jj % 2}")
                ctx.mark("bsh_free", "va")
            act_v = ctx.cnt["va"]
            pe_w("va", act_v)
            bank_w("sgz")
            for i in range(8):
                pe_i(PE.matmul(pb[4][0:1, 0:512], B_sgwc[:, i:i + 1],
                               B_xn[i][:], start=(i == 0), stop=(i == 7)))
            sgz_pe = ctx.cnt["pe"]
            vop(lambda v: v.tensor_copy(out=B_sgr[:], in_=pb[4][0:1, 0:512]),
                ("pe", sgz_pe), ("e0", ctx.at(f"sg_st_{b - 1}")))
            bank_m("sgz")
            sgr_v = ctx.cnt["va"]
            ctx.wait("pool", "va", sgr_v)
            gp_dma(d_sg[b:b + 1, :], B_sgr[:], sem="e0")
            ctx.mark(f"sg_st_{b}", "e0")
            for tt in range(4):
                for ns in range(2):
                    bank = pb[4 + ns]
                    pe_w("va", act_v)
                    if ns == 0:
                        bank_w("sgz")
                    bank_w(f"dn{ns}")
                    for i in range(16):
                        pe_i(PE.matmul(bank[0:128, 0:512],
                                       B_act[i][:, tt * 128:(tt + 1) * 128],
                                       B_wd[i][:, ns * 512:(ns + 1) * 512],
                                       start=(i == 0), stop=(i == 15)))
                    y_pe = ctx.cnt["pe"]
                    vop(lambda v, ns=ns, bank=bank: v.tensor_copy(
                        out=B_y[:, ns * 512:(ns + 1) * 512], in_=bank[0:128, 0:512]),
                        ("pe", y_pe), ("gs", ctx.at("by_free")))
                    bank_m(f"dn{ns}")
                yv = ctx.cnt["va"]
                ctx.wait("pool", "va", yv)
                gp_dma(d_ysh[(b * 4 + tt) * 128:(b * 4 + tt + 1) * 128, :], B_y[:])
                ctx.mark("by_free", "gs")
            ctx.mark(f"bblk_{b}", "pe")
        barrier()

        sp_dma(C_sgs[:], d_sg.rearrange("(o a) b -> o (a b)", o=1))
        sgl_sx = ctx.cnt["sx"]
        aop(lambda s: s.activation(out=C_sgs[:], in_=C_sgs[:], func=AF.Sigmoid),
            ("sx", sgl_sx))
        sgs_sa = ctx.cnt["sa"]
        for j in range(NT):
            pe_w("sa", sgs_sa)
            bank_w("sgt")
            pe_i(PE.transpose(pb[5][0:128, 0:1], C_sgs[0:1, j * 128:(j + 1) * 128],
                              s_id_f[0:1, 0:1]))
            vop(lambda v, j=j: v.tensor_copy(out=s_sgt[:, j:j + 1],
                                             in_=pb[5][0:128, 0:1]),
                ("pe", ctx.cnt["pe"]))
            bank_m("sgt")
        barrier()
        bc_reg = G.to_reg(YBROWS - 1)
        ctx.wait("pool", "g2", ctx.at("ybzero"))
        for e in range(E):
            p = e % 2
            ctx.wait("pool", "va", ctx.at(f"ebmask_{e - 2}"))
            ctx.inc(G.indirect_dma_start(
                out=C_eb[p][:], out_offset=None, in_=d_xn[:, :],
                in_offset=IndirectOffsetOnAxis(ap=s_tokid[:, e:e + 1], axis=0)),
                f"e{e % 2}", 16)
            gd_e = ctx.cnt[f"e{e % 2}"]
            vop(lambda v, p=p, e=e: v.tensor_scalar(
                out=C_eb[p][:], in0=C_eb[p][:], scalar1=s_vld[:, e:e + 1],
                scalar2=None, op0=ALU.mult), (f"e{e % 2}", gd_e))
            ctx.mark(f"ebmask_{e}", "va")
            egu_rows = d_egu.rearrange("e d n -> (e d) n")
            edw_rows = d_edw.rearrange("e d n -> (e d) n")
            for i2 in range(4):
                sp_ = i2 % 2
                ctx.wait("sp", "va", ctx.at(f"wguf{sp_}_va"))
                ctx.wait("sp", "sa", ctx.at(f"wguf{sp_}_sa"))
                sq_dma(f"d{sp_}", C_wguf[sp_][:].rearrange("p (a n) -> p a n", a=2),
                       egu_rows[e * D + i2 * 256:e * D + (i2 + 1) * 256, :]
                       .rearrange("(a p) n -> p a n", p=128))
                wsx = ctx.cnt[f"d{sp_}"]
                vop(lambda v, sp_=sp_, i2=i2, p=p: v.tensor_copy(
                    out=C_wgu[p][2 * i2][:], in_=C_wguf[sp_][:, 0:D]),
                    (f"d{sp_}", wsx))
                aop(lambda s, sp_=sp_, i2=i2, p=p: s.copy(
                    out=C_wgu[p][2 * i2 + 1][:], in_=C_wguf[sp_][:, D:2 * D]),
                    (f"d{sp_}", wsx))
                ctx.mark(f"wguf{sp_}_va", "va")
                ctx.mark(f"wguf{sp_}_sa", "sa")
            for i2 in range(2):
                sp_ = i2 % 2
                ctx.wait("sp", "va", ctx.at(f"wdf{sp_}_va"))
                ctx.wait("sp", "sa", ctx.at(f"wdf{sp_}_sa"))
                sq_dma(f"d{2 + sp_}",
                       C_wdf[sp_][:].rearrange("p (a n) -> p a n", a=2),
                       edw_rows[e * 512 + i2 * 256:e * 512 + (i2 + 1) * 256, :]
                       .rearrange("(a p) n -> p a n", p=128))
                wsx = ctx.cnt[f"d{2 + sp_}"]
                vop(lambda v, sp_=sp_, i2=i2, p=p: v.tensor_copy(
                    out=C_wd[p][2 * i2][:], in_=C_wdf[sp_][:, 0:D]),
                    (f"d{2 + sp_}", wsx))
                aop(lambda s, sp_=sp_, i2=i2, p=p: s.copy(
                    out=C_wd[p][2 * i2 + 1][:], in_=C_wdf[sp_][:, D:2 * D]),
                    (f"d{2 + sp_}", wsx))
                ctx.mark(f"wdf{sp_}_va", "va")
                ctx.mark(f"wdf{sp_}_sa", "sa")
            cvt_v = ctx.cnt["va"]; cvt_a = ctx.cnt["sa"]
            for i in range(8):
                pe_w("va", ctx.at(f"ebmask_{e}"))
                bank_w("pbh0")
                pe_i(PE.transpose(pbh[0][0:128, 0:128],
                                  C_eb[p][:, i * 128:(i + 1) * 128], s_id_b[:]))
                vop(lambda v, i=i: v.tensor_copy(out=C_ebt[0][:, i * 128:(i + 1) * 128],
                                                 in_=pbh[0][0:128, 0:128]),
                    ("pe", ctx.cnt["pe"]))
                bank_m("pbh0")
            ebt_v = ctx.cnt["va"]
            pe_w("va", max(ebt_v, cvt_v))
            pe_w("sa", cvt_a)
            bank_w("cg", "sa")
            bank_w("cu")
            for ns in range(2):
                for i in range(8):
                    pe_i(PE.matmul(pb[ns][0:128, 0:512],
                                   C_ebt[0][:, i * 128:(i + 1) * 128],
                                   C_wgu[p][i][:, ns * 512:(ns + 1) * 512],
                                   start=(i == 0), stop=(i == 7)))
            gu_pe = ctx.cnt["pe"]
            aop(lambda s: s.activation(out=C_sh[0][:], in_=pb[0][0:128, 0:512],
                                       func=AF.Silu),
                ("pe", gu_pe), ("va", ctx.at("csh_free")))
            sh_sa = ctx.cnt["sa"]
            bank_m("cg", "sa")
            vop(lambda v: v.tensor_mul(out=C_act[0][:], in0=C_sh[0][:],
                                       in1=pb[1][0:128, 0:512]),
                ("sa", sh_sa), ("pe", gu_pe))
            bank_m("cu")
            ctx.mark("csh_free", "va")
            cact_v = ctx.cnt["va"]
            for i in range(4):
                pe_w("va", cact_v)
                bank_w("pbh1")
                pe_i(PE.transpose(pbh[1][0:128, 0:128],
                                  C_act[0][:, i * 128:(i + 1) * 128], s_id_b[:]))
                vop(lambda v, i=i: v.tensor_copy(out=C_actT[0][:, i * 128:(i + 1) * 128],
                                                 in_=pbh[1][0:128, 0:128]),
                    ("pe", ctx.cnt["pe"]))
                bank_m("pbh1")
            at_v = ctx.cnt["va"]
            pe_w("va", at_v)
            bank_w("cd0")
            bank_w("cd1")
            for ns in range(2):
                for i in range(4):
                    pe_i(PE.matmul(pb[2 + ns][0:128, 0:512],
                                   C_actT[0][:, i * 128:(i + 1) * 128],
                                   C_wd[p][i][:, ns * 512:(ns + 1) * 512],
                                   start=(i == 0), stop=(i == 3)))
            dn_pe = ctx.cnt["pe"]
            ctx.mark(f"exp_pe_{e}", "pe")
            for ns in range(2):
                vop(lambda v, ns=ns, p=p: v.tensor_copy(
                    out=C_ebo[p][:, ns * 512:(ns + 1) * 512],
                    in_=pb[2 + ns][0:128, 0:512]),
                    ("pe", dn_pe), (f"c{e % 2}", ctx.at(f"ebo_st_{e - 2}")))
                bank_m(f"cd{ns}")
            ebo_v = ctx.cnt["va"]
            ctx.wait("pool", "va", ebo_v)
            ctx.inc(G.indirect_dma_start(
                out=d_yb[:, :],
                out_offset=IndirectOffsetOnAxis(ap=s_asg[:, e:e + 1], axis=0),
                in_=C_ebo[p][:], in_offset=None,
                bounds_check=bc_reg, oob_is_err=False), f"c{e % 2}", 16)
            ctx.mark(f"ebo_st_{e}", f"c{e % 2}")
        barrier()
        yb3 = d_yb.rearrange("(t k) d -> t k d", k=K8)
        for j in range(NT):
            p = j % 2
            ctx.wait("sp", "va", ctx.at(f"comb_{j - 2}"))
            sq_dma(f"d{p}", C_x[p][:], d_x_own[j * 128:(j + 1) * 128, :])
            sq_dma(f"d{p}", C_sv[p][:], d_s[j * 128:(j + 1) * 128, :])
            sq_dma(f"d{p}", C_ysh[p][:], d_ysh[j * 128:(j + 1) * 128, :])
            ld_sx = ctx.cnt[f"d{p}"]
            vop(lambda v: v.memset(C_y[:], 0.0))
            for k in range(K8):
                gsem = ["d2", "d3", "g2", "g3"][k % 4]
                ctx.wait("sp", "va", ctx.at(f"gaccP{k % 4}"))
                sq_dma(gsem, C_g[k % 4][:], yb3[j * 128:(j + 1) * 128, k, :])
                gk_sx = ctx.cnt[gsem]
                vop(lambda v, j=j, k=k: v.scalar_tensor_tensor(
                    out=C_y[:], in0=C_g[k % 4][:],
                    scalar=s_w8[:, j * K8 + k:j * K8 + k + 1], in1=C_y[:],
                    op0=ALU.mult, op1=ALU.add), (gsem, gk_sx))
                ctx.mark(f"gaccP{k % 4}", "va")
            vop(lambda v, j=j, p=p: v.scalar_tensor_tensor(
                out=C_ffn[p][:], in0=C_ysh[p][:], scalar=s_sgt[:, j:j + 1],
                in1=C_y[:], op0=ALU.mult, op1=ALU.add),
                (f"d{p}", ld_sx), ("sa", ctx.at(f"ffnfree{p}")))
            ffn_v = ctx.cnt["va"]
            aop(lambda s, p=p: s.activation(out=C_gg[p][:], in_=C_ffn[p][:],
                                            func=AF.Silu), ("va", ffn_v))
            gg_sa = ctx.cnt["sa"]
            ctx.mark(f"ffnfree{p}", "sa")
            vop(lambda v, p=p: v.tensor_mul(out=C_gg[p][:], in0=C_gg[p][:],
                                            in1=C_sv[p][:]),
                ("sa", gg_sa))
            osem = "so" if p == 0 else "s1"
            vop(lambda v, p=p: v.tensor_add(out=C_out[p][:], in0=C_gg[p][:],
                                            in1=C_x[p][:]),
                (osem, ctx.at(f"out_st_{j - 2}")))
            o_v = ctx.cnt["va"]
            ctx.mark(f"comb_{j}", "va")
            ctx.wait("pool", "va", o_v)
            ctx.inc(G.dma_start(out=d_out[j * 128:(j + 1) * 128, :],
                                in_=C_out[p][:]), osem, 16)
            ctx.mark(f"out_st_{j}", osem)
        for en in ("sp", "pe", "dve", "act", "pool"):
            ctx.wait(en, "so", NT * 8)
            ctx.wait(en, "s1", NT * 8)

    engine_map = {"sp": "sync", "pe": "tensor", "act": "scalar",
                  "dve": "vector", "pool": "gpsimd"}
    counts = []
    with nc.Block() as block:
        for name, attr in engine_map.items():
            def make_body(nm):
                def body(eng):
                    engines = {k: (_DryEng() if k != nm else eng) for k in engine_map}
                    ctx = Ctx(sems, engines, nm)
                    emit_all(ctx)
                    counts.append(dict(ctx.cnt))
                return body
            getattr(block, attr)(make_body(name))
    assert all(c == counts[0] for c in counts[1:]), "pass count divergence"
    return nc


_NC_CACHE = None


def kernel(**inputs):
    global _NC_CACHE
    x = np.ascontiguousarray(np.asarray(inputs["L0_attn_q"], np.float32).reshape(T, D))
    x_fm = np.ascontiguousarray(x.T)
    in_common = dict(
        x_fm=x_fm,
        kv_k=np.ascontiguousarray(np.asarray(inputs["L0_attn_k"], np.float32)[0]),
        kv_v=np.ascontiguousarray(np.asarray(inputs["L0_attn_v"], np.float32)[0]),
        w_q=np.asarray(inputs["L0_attn_w_q"], np.float32),
        w_k=np.asarray(inputs["L0_attn_w_k"], np.float32),
        w_v=np.asarray(inputs["L0_attn_w_v"], np.float32),
        w_o=np.asarray(inputs["L0_attn_w_o"], np.float32),
        b_q=np.asarray(inputs["L0_attn_b_q"], np.float32).reshape(1, D),
        b_k=np.asarray(inputs["L0_attn_b_k"], np.float32).reshape(1, D),
        b_v=np.asarray(inputs["L0_attn_b_v"], np.float32).reshape(1, D),
        b_o=np.asarray(inputs["L0_attn_b_o"], np.float32).reshape(1, D),
        gate_w=np.asarray(inputs["L0_ffn_gate_w"], np.float32),
        sgw=np.asarray(inputs["L0_ffn_shared_gate_w"], np.float32),
        sguw=np.asarray(inputs["L0_ffn_shared_gate_up_w"], np.float32),
        sdw=np.asarray(inputs["L0_ffn_shared_down_w"], np.float32),
        egu=np.asarray(inputs["L0_ffn_expert_gate_up_w"], np.float32),
        edw=np.asarray(inputs["L0_ffn_expert_down_w"], np.float32),
    )
    xt = x.reshape(GT, 128, D)
    if _NC_CACHE is None:
        _NC_CACHE = build_nc()
    nc = _NC_CACHE
    in_maps = []
    for c in range(NCORE):
        im = dict(in_common)
        im["x_own"] = np.ascontiguousarray(xt[c::NCORE].reshape(TC, D))
        cm = np.zeros((128, 8), np.float32)
        cm[:, c] = 1.0
        im["cmask"] = cm
        in_maps.append(im)
    res = run_bass_kernel_spmd(nc, in_maps, core_ids=list(range(NCORE)))
    full = np.empty((GT, 128, D), np.float32)
    for c in range(NCORE):
        full[c::NCORE] = np.asarray(res.results[c]["out"]).reshape(NT, 128, D)
    return full.reshape(8, 4096, D)


# version 18
# speedup vs baseline: 1.0955x; 1.0651x over previous
import numpy as np
import ml_dtypes
import concourse.bass as bass
import concourse.mybir as mybir
from concourse.bass import IndirectOffsetOnAxis
from concourse.bass_utils import run_bass_kernel_spmd

F32 = mybir.dt.float32
BF16 = mybir.dt.bfloat16
I32 = mybir.dt.int32
ALU = mybir.AluOpType
AF = mybir.ActivationFunctionType
AXX = mybir.AxisListType.X

D = 1024
T = 32768
TC = 4096
NT = 32
GT = 256
NB = 8
KVL = 2048
NH = 16
E = 64
K8 = 8
CAP = 512
LCAP = 128
NCORE = 8
EPS = 1e-5
YBROWS = TC * K8
PAY = 65536.0
DUMMY_ASG = 40000.0
SCRATCH = 16384


class _DryI:
    def then_inc(self, sem, n=1):
        return self


_DRYI = _DryI()


class _DryEng:
    def __getattr__(self, name):
        def f(*a, **k):
            return _DRYI
        return f


class Ctx:

    def __init__(self, sems, engines, real_name):
        self.sems = sems
        self.eng = engines
        self.real = real_name
        self.cnt = {k: 0 for k in sems}
        self.marks = {}

    def inc(self, ins, sem, n=1):
        self.cnt[sem] += n
        if ins is not _DRYI and ins is not None:
            ins.then_inc(self.sems[sem], n)
        return self.cnt[sem]

    def wait(self, engname, sem, val=None):
        if val is None:
            val = self.cnt[sem]
        if val <= 0:
            return
        e = self.eng[engname]
        if not isinstance(e, _DryEng):
            e.wait_ge(self.sems[sem], val)

    def mark(self, tag, sem):
        self.marks[tag] = self.cnt[sem]

    def at(self, tag):
        return self.marks.get(tag, 0)


def build_nc():
    nc = bass.Bass("TRN2", target_bir_lowering=False,
                   dynamic_dma_scratch_size=SCRATCH)

    d_x_own = nc.dram_tensor("x_own", [TC, D], F32, kind="ExternalInput").ap()
    d_x_fm = nc.dram_tensor("x_fm", [(T // 512) * 128, 8 * 512], F32,
                            kind="ExternalInput").ap()
    d_k = nc.dram_tensor("kv_k", [KVL, D], F32, kind="ExternalInput").ap()
    d_v = nc.dram_tensor("kv_v", [KVL, D], F32, kind="ExternalInput").ap()
    d_wq = nc.dram_tensor("w_q", [D, D], F32, kind="ExternalInput").ap()
    d_wk = nc.dram_tensor("w_k", [D, D], F32, kind="ExternalInput").ap()
    d_wv = nc.dram_tensor("w_v", [D, D], F32, kind="ExternalInput").ap()
    d_wo = nc.dram_tensor("w_o", [D, D], F32, kind="ExternalInput").ap()
    d_bq = nc.dram_tensor("b_q", [1, D], F32, kind="ExternalInput").ap()
    d_bk = nc.dram_tensor("b_k", [1, D], F32, kind="ExternalInput").ap()
    d_bv = nc.dram_tensor("b_v", [1, D], F32, kind="ExternalInput").ap()
    d_bo = nc.dram_tensor("b_o", [1, D], F32, kind="ExternalInput").ap()
    d_gw = nc.dram_tensor("gate_w", [D, E], F32, kind="ExternalInput").ap()
    d_sgw = nc.dram_tensor("sgw", [D, 1], F32, kind="ExternalInput").ap()
    d_sguw = nc.dram_tensor("sguw", [D, 4096], F32, kind="ExternalInput").ap()
    d_sdw = nc.dram_tensor("sdw", [2048, D], F32, kind="ExternalInput").ap()
    d_egu = nc.dram_tensor("egu", [E, D, D], F32, kind="ExternalInput").ap()
    d_edw = nc.dram_tensor("edw", [E, 512, D], F32, kind="ExternalInput").ap()
    d_cm = nc.dram_tensor("cmask", [128, 8], F32, kind="ExternalInput").ap()

    d_out = nc.dram_tensor("out", [TC, D], F32, kind="ExternalOutput").ap()

    d_xn = nc.dram_tensor("xn_scr", [TC, D], BF16, kind="Internal").ap()
    d_xnfm = nc.dram_tensor("xnfm_scr", [D, TC], BF16, kind="Internal").ap()
    d_yb = nc.dram_tensor("yb_scr", [YBROWS, D], BF16, kind="Internal").ap()
    d_s = nc.dram_tensor("s_scr", [TC, D], F32, kind="Internal").ap()
    d_ysh = nc.dram_tensor("ysh_scr", [TC, D], F32, kind="Internal").ap()
    d_qs = nc.dram_tensor("q_scr", [NB * 1024, 512], BF16, kind="Internal").ap()
    d_sg = nc.dram_tensor("sg_scr", [NB, 512], F32, kind="Internal").ap()

    c_id_f = nc.inline_tensor(np.eye(128, dtype=np.float32), name="c_id_f")
    c_id_b = nc.inline_tensor(np.eye(128).astype(ml_dtypes.bfloat16), name="c_id_b")
    c_lt = nc.inline_tensor(np.triu(np.ones((128, 128), np.float32), 1), name="c_lt")
    c_onc_f = nc.inline_tensor(np.ones((128, 1), np.float32), name="c_onc_f")
    c_onc_b = nc.inline_tensor(np.ones((128, 1)).astype(ml_dtypes.bfloat16), name="c_onc_b")
    c_onr_f = nc.inline_tensor(np.ones((1, 128), np.float32), name="c_onr_f")
    c_onr_b = nc.inline_tensor(np.ones((1, 128)).astype(ml_dtypes.bfloat16), name="c_onr_b")
    c_iota = nc.inline_tensor(np.arange(128, dtype=np.float32).reshape(128, 1),
                              name="c_iota")

    class Alloc:
        def __init__(self, base):
            self.cur = base

        def al(self, name, shape, dtype, align=64):
            itm = {F32: 4, BF16: 2, I32: 4}[dtype]
            per_part = int(np.prod(shape[1:])) * itm
            off = (self.cur + align - 1) // align * align
            self.cur = off + per_part
            return nc.alloc_sbuf_tensor_at(name, list(shape), dtype, offset=off)

    BASE = (nc.sbuf_base + 63) // 64 * 64
    TOP = nc.sbuf_top
    P = Alloc(BASE)
    s_id_f = P.al("s_id_f", [128, 128], F32)
    s_id_b = P.al("s_id_b", [128, 128], BF16)
    s_lt = P.al("s_lt", [128, 128], F32)
    s_onc_f = P.al("s_onc_f", [128, 1], F32)
    s_onc_b = P.al("s_onc_b", [128, 1], BF16)
    s_onr_f = P.al("s_onr_f", [1, 128], F32)
    s_onr_b = P.al("s_onr_b", [1, 128], BF16)
    s_iota = P.al("s_iota", [128, 1], F32)
    s_gb = [P.al(f"s_gb{i}", [128, E], BF16) for i in range(8)]
    s_gcol = P.al("s_gcol", [1, E], BF16)
    s_bq = P.al("s_bq", [128, 8], F32)
    s_bk = P.al("s_bk", [128, 8], F32)
    s_bo_b = P.al("s_bo_b", [1, D], BF16)
    s_bv_b = P.al("s_bv_b", [1, D], BF16)
    s_w8 = P.al("s_w8", [128, NT * K8], F32)
    s_sgt = P.al("s_sgt", [128, NT], F32)
    s_zero = P.al("s_zero", [128, D], BF16)
    s_cm = P.al("s_cm", [128, 8], F32)
    s_gof_run = P.al("s_gof_run", [1, E], F32)
    s_cst = P.al("s_cst", [128, 8], F32)
    s_tokid = P.al("s_tokid", [128, E], I32)
    s_asg = P.al("s_asg", [128, E], I32)
    s_vld = P.al("s_vld", [128, E], F32)

    ARENA0 = P.cur

    a = Alloc(ARENA0)
    p_xf = [a.al(f"p_xf{i}", [128, 8 * 512], F32) for i in range(2)]
    p_xb = [a.al(f"p_xb{i}", [128, 8 * 512], BF16) for i in range(2)]
    p_nmu = [a.al(f"p_nmu{i}", [1, 512], BF16) for i in range(2)]
    p_t8 = [a.al(f"p_t8{i}", [128, 4 * K8], F32) for i in range(2)]
    p_M = [a.al(f"p_M{i}", [128, 4 * E], F32) for i in range(2)]
    p_cnt = a.al("p_cnt", [1, 4 * E], F32)
    p_ust = a.al("p_ust", [128, NT * E], F32)
    p_gst = a.al("p_gst", [1, NT * E], F32)
    p_xo = [a.al(f"p_xo{i}", [128, D], F32) for i in range(2)]
    p_xm = a.al("p_xm", [128, D], F32)
    p_xn = [a.al(f"p_xn{i}", [128, D], BF16) for i in range(2)]
    p_fmst = a.al("p_fmst", [128, D], BF16)
    p_bn = a.al("p_bn", [128, 12], F32)
    p_mv = a.al("p_mv", [128, 2], F32)
    p_r = a.al("p_r", [128, 1], F32)
    p_lnt = a.al("p_lnt", [128, 1], F32)
    p_t8j = a.al("p_t8j", [128, K8], F32)
    p_Mj = a.al("p_Mj", [128, E], F32)
    p_c1 = a.al("p_c1", [128, E], F32)
    p_sel = a.al("p_sel", [128, E], F32)
    p_oh = a.al("p_oh", [128, E], F32)
    p_Kc = a.al("p_Kc", [128, E], F32)
    p_v2 = a.al("p_v2", [128, E], F32)
    p_tl8 = a.al("p_tl8", [128, 1], F32)
    p_kp8 = a.al("p_kp8", [128, K8], F32)
    p_e8 = a.al("p_e8", [128, K8], F32)
    p_nm8 = a.al("p_nm8", [128, 1], F32)
    p_d8 = a.al("p_d8", [128, 1], F32)
    p_di = a.al("p_di", [128, 1], F32)
    p_vm = a.al("p_vm", [64, TC], F32)
    p_a16 = a.al("p_a16", [64, 128], F32)
    p_der = a.al("p_der", [128, E], F32)
    p_der2 = a.al("p_der2", [128, E], F32)
    P0_top = a.cur

    a = Alloc(ARENA0)
    A_kh = [a.al(f"A_kh{i}", [128, KVL], BF16) for i in range(8)]
    A_vhx = a.al("A_vhx", [128, 16 * NH * 65], BF16)
    s3_base = a.cur
    A_wb = [a.al(f"A_wb{i}", [128, D], BF16) for i in range(8)]
    A_cvt = a.al("A_cvt", [128, D], F32)
    A_kvb = a.al("A_kvb", [128, D], BF16)
    A_kvfm = [a.al(f"A_kvfm{i}", [128, KVL], BF16) for i in range(8)]
    A12_top = a.cur
    a = Alloc(s3_base + 8 * D * 2)
    A_q = [a.al(f"A_q{i}", [128, 512], BF16) for i in range(8)]
    A_P = [a.al(f"A_P{i}", [128, 1024], BF16) for i in range(2)]
    A_ofm = [a.al(f"A_ofm{i}", [128, 512], BF16) for i in range(8)]
    A_otm = [a.al(f"A_otm{i}", [128, D], BF16) for i in range(4)]
    A_stm = a.al("A_stm", [128, D], F32)
    A_drow = a.al("A_drow", [1, 512], F32)
    A_dstack = a.al("A_dstack", [16, 512], F32)
    A_dinv = a.al("A_dinv", [128, NH], F32)
    A_wcvt = a.al("A_wcvt", [128, D], F32)
    A3_top = a.cur

    a = Alloc(ARENA0)
    B_wgu = [a.al(f"B_wgu{i}", [128, 4096], BF16) for i in range(8)]
    B_wd = [a.al(f"B_wd{i}", [128, D], BF16) for i in range(16)]
    B_sgwc = a.al("B_sgwc", [128, 8], BF16)
    B_cvt = a.al("B_cvt", [128, D], F32)
    B_xn = [a.al(f"B_xn{i}", [128, 512], BF16) for i in range(8)]
    B_act = [a.al(f"B_act{i}", [128, 512], BF16) for i in range(16)]
    B_sh = a.al("B_sh", [128, 512], F32)
    B_y = a.al("B_y", [128, D], F32)
    B_sgr = a.al("B_sgr", [1, 512], F32)
    B_top = a.cur

    a = Alloc(ARENA0)
    C_wguf = [a.al(f"C_wguf{i}", [128, 2 * D], F32) for i in range(4)]
    C_wgu = [[a.al(f"C_wgu{p}_{i}", [128, D], BF16) for i in range(8)] for p in range(3)]
    C_wdf = [a.al(f"C_wdf{i}", [128, 2 * D], F32) for i in range(2)]
    C_wd = [[a.al(f"C_wd{p}_{i}", [128, D], BF16) for i in range(4)] for p in range(2)]
    C_eb = [a.al(f"C_eb{p}", [128, D], BF16) for p in range(2)]
    C_ebt = [a.al(f"C_ebt{p}", [128, D], BF16) for p in range(2)]
    C_sh = [a.al(f"C_sh{p}", [128, 512], F32) for p in range(2)]
    C_act = [a.al(f"C_act{p}", [128, 512], BF16) for p in range(2)]
    C_actT = [a.al(f"C_actT{p}", [128, 512], BF16) for p in range(2)]
    C_ebo = [a.al(f"C_ebo{p}", [128, D], BF16) for p in range(2)]
    C_g = [a.al(f"C_g{p}", [128, D], BF16) for p in range(4)]
    C_y = a.al("C_y", [128, D], F32)
    C_x = [a.al(f"C_x{p}", [128, D], F32) for p in range(2)]
    C_sv = [a.al(f"C_sv{p}", [128, D], F32) for p in range(2)]
    C_ysh = [a.al(f"C_ysh{p}", [128, D], F32) for p in range(2)]
    C_ffn = [a.al(f"C_ffn{p}", [128, D], F32) for p in range(2)]
    C_gg = [a.al(f"C_gg{p}", [128, D], F32) for p in range(2)]
    C_out = [a.al(f"C_out{p}", [128, D], F32) for p in range(2)]
    C_top = a.cur
    a = Alloc(ARENA0)
    C_sgs = a.al("C_sgs", [1, TC], F32)

    for nm, top in [("p0", P0_top), ("A12", A12_top), ("A3", A3_top),
                    ("B", B_top), ("C", C_top)]:
        assert top <= TOP, f"{nm} arena exceeds sbuf top: {top} > {TOP}"

    pbA = nc.alloc_psum_tensor("pbA", [128, 1024], F32)
    pbB = nc.alloc_psum_tensor("pbB", [128, 1024], F32)
    pbC = nc.alloc_psum_tensor("pbC", [128, 512], F32)
    pbD = nc.alloc_psum_tensor("pbD", [128, 512], F32)
    pb = [pbA[:, 0:512], pbA[:, 512:1024], pbB[:, 0:512], pbB[:, 512:1024],
          pbC[:], pbD[:]]
    pbh = [nc.alloc_psum_tensor(f"pbh{i}", [128, 1024], BF16) for i in range(2)]

    sems = {n: nc.alloc_semaphore(n)
            for n in ["sx", "so", "pe", "va", "sa", "gd", "gs", "s1",
                      "d0", "d1", "d2", "d3", "g2", "g3", "e0", "e1",
                      "c0", "c1", "c2", "c3"]}

    CONSTS = [EPS, -0.5, 0.125, -12.0]
    for ci, cv in enumerate(CONSTS):
        nc.const_aps.register(F32, cv, s_cst[:, ci:ci + 1])


    def emit_all(ctx):
        SP = ctx.eng["sp"]; PE = ctx.eng["pe"]; V = ctx.eng["dve"]
        A_ = ctx.eng["act"]; G = ctx.eng["pool"]

        def sp_dma(out, in_, **kw):
            return ctx.inc(SP.dma_start(out=out, in_=in_, **kw), "sx", 16)

        def gp_dma(out, in_, sem="gs", **kw):
            return ctx.inc(G.dma_start(out=out, in_=in_, **kw), sem, 16)

        def sq_dma(sem, out, in_, **kw):
            return ctx.inc(SP.dma_start(out=out, in_=in_, **kw), sem, 16)

        def vop(fn, *deps):
            ctx.wait("dve", "va")
            for s, v in deps:
                ctx.wait("dve", s, v)
            return ctx.inc(fn(V), "va")

        def aop(fn, *deps):
            ctx.wait("act", "sa")
            for s, v in deps:
                ctx.wait("act", s, v)
            return ctx.inc(fn(A_), "sa")

        def pe_w(sem, v=None):
            ctx.wait("pe", sem, v)

        def pe_i(ins):
            return ctx.inc(ins, "pe")

        def bank_w(tag, sem="va"):
            pe_w(sem, ctx.at("bk_" + tag))

        def bank_m(tag, sem="va"):
            ctx.mark("bk_" + tag, sem)

        def barrier():
            for en in ("sp", "pe", "dve", "act", "pool"):
                for s in ("pe", "va", "sa", "sx", "gd", "gs", "so", "s1",
                          "d0", "d1", "d2", "d3", "g2", "g3", "e0", "e1",
                          "c0", "c1", "c2", "c3"):
                    ctx.wait(en, s)

        def load_wmat(dram, dest, stage):
            for i in range(8):
                ctx.wait("sp", "va", ctx.at("wstage_free"))
                sq_dma("d2", stage[:], dram[i * 128:(i + 1) * 128, :])
                wsx = ctx.cnt["d2"]
                vop(lambda v, i=i: v.tensor_copy(out=dest[i][:], in_=stage[:]),
                    ("d2", wsx))
                ctx.mark("wstage_free", "va")

        sp_dma(s_id_f[:], c_id_f.ap())
        sp_dma(s_id_b[:], c_id_b.ap())
        sp_dma(s_lt[:], c_lt.ap())
        sp_dma(s_onc_f[:], c_onc_f.ap())
        sp_dma(s_onc_b[:], c_onc_b.ap())
        sp_dma(s_onr_f[:], c_onr_f.ap())
        sp_dma(s_onr_b[:], c_onr_b.ap())
        sp_dma(s_iota[:], c_iota.ap())
        sp_dma(s_cm[:], d_cm[:])
        sq_dma("d3", p_xf[0][:, 0:512].rearrange("p (c e) -> p c e", c=8),
               d_gw.rearrange("(c p) e -> p c e", p=128))
        ld0 = ctx.cnt["d3"]
        for i in range(8):
            vop(lambda v, i=i: v.tensor_copy(out=s_gb[i][:],
                                             in_=p_xf[0][:, i * 64:(i + 1) * 64]),
                ("d3", ld0))
        sp_dma(s_bq[:].rearrange("p (c o) -> p c o", o=1),
               d_bq.rearrange("o (c p) -> p c o", p=128),
               allow_slow_non_contiguous=True)
        sp_dma(s_bk[:].rearrange("p (c o) -> p c o", o=1),
               d_bk.rearrange("o (c p) -> p c o", p=128),
               allow_slow_non_contiguous=True)
        sq_dma("g3", p_xf[1][0:1, 0:1024], d_bo[:])
        sq_dma("g3", p_xf[1][0:1, 1024:2048], d_bv[:])
        ld1 = ctx.cnt["g3"]
        vop(lambda v: v.tensor_copy(out=s_bo_b[:], in_=p_xf[1][0:1, 0:1024]), ("g3", ld1))
        vop(lambda v: v.tensor_copy(out=s_bv_b[:], in_=p_xf[1][0:1, 1024:2048]), ("g3", ld1))
        vop(lambda v: v.memset(s_zero[:], 0.0))
        vop(lambda v: v.memset(s_gof_run[:], 0.0))
        vop(lambda v: v.memset(p_ust[:], 0.0))
        vop(lambda v: v.memset(p_gst[:], 0.0))
        for ci, cv in enumerate(CONSTS):
            vop(lambda v, ci=ci, cv=cv: v.memset(s_cst[:, ci:ci + 1], cv))
        z0 = ctx.cnt["va"]
        ctx.wait("sp", "va", z0)
        ctx.wait("pool", "va", z0)
        for rr in range(YBROWS // 128):
            gp_dma(d_yb[rr * 128:(rr + 1) * 128, :], s_zero[:], sem="g2")
        ctx.mark("ybzero", "g2")
        pe_w("va", z0)
        for i in range(8):
            pe_i(PE.matmul(pb[0][0:1, 0:E], s_onc_b[:], s_gb[i][:],
                           start=(i == 0), stop=(i == 7)))
        vop(lambda v: v.tensor_copy(out=s_gcol[:], in_=pb[0][0:1, 0:E]),
            ("pe", ctx.cnt["pe"]))
        barrier()

        NB0 = GT // 4
        ldb_d = {}; cvtv_d = {}; cvta_d = {}; mupe_d = {}; nmuv_d = {}
        upe_d = {}; mv_d = {}; cntpe_d = {}; cntfree_d = {-1: 0}

        def L0a(b):
            pbu = b % 2
            if b - 2 >= 0:
                ctx.wait("sp", "pe", upe_d[b - 2])
                ctx.wait("sp", "va", cvtv_d[b - 2])
                ctx.wait("sp", "sa", cvta_d[b - 2])
            sq_dma(f"d{pbu}", p_xf[pbu][:],
                   d_x_fm[b * 128:(b + 1) * 128, :])
            ldb_d[b] = ctx.cnt[f"d{pbu}"]

        def C0a(b):
            pbu = b % 2
            for i in range(4):
                vop(lambda v, i=i: v.tensor_copy(
                    out=p_xb[pbu][:, i * 512:(i + 1) * 512],
                    in_=p_xf[pbu][:, i * 512:(i + 1) * 512]),
                    (f"d{pbu}", ldb_d[b]))
            for i in range(4, 8):
                aop(lambda s, i=i: s.copy(
                    out=p_xb[pbu][:, i * 512:(i + 1) * 512],
                    in_=p_xf[pbu][:, i * 512:(i + 1) * 512]),
                    (f"d{pbu}", ldb_d[b]))
            cvtv_d[b] = ctx.cnt["va"]; cvta_d[b] = ctx.cnt["sa"]

        def MU0a(b):
            pbu = b % 2
            pe_w("va", cvtv_d[b]); pe_w("sa", cvta_d[b])
            if b - 1 >= 0:
                pe_w("va", nmuv_d[b - 1])
            for i in range(8):
                pe_i(PE.matmul(pb[5][0:1, 0:512], s_onc_b[:],
                               p_xb[pbu][:, i * 512:(i + 1) * 512],
                               start=(i == 0), stop=(i == 7)))
            mupe_d[b] = ctx.cnt["pe"]

        def NMU0a(b):
            pbu = b % 2
            vop(lambda v: v.tensor_scalar_mul(p_nmu[pbu][:], pb[5][0:1, 0:512],
                                              -1.0 / D), ("pe", mupe_d[b]))
            nmuv_d[b] = ctx.cnt["va"]

        def U0a(b):
            pbu = b % 2
            ub = pb[0] if b % 2 == 0 else pb[3]
            pe_w("va", nmuv_d[b])
            if b - 2 >= 0:
                pe_w("va", mv_d[b - 2])
            for ti in range(4):
                for i in range(8):
                    pe_i(PE.matmul(
                        ub[0:128, ti * E:(ti + 1) * E],
                        p_xb[pbu][:, i * 512 + ti * 128:i * 512 + (ti + 1) * 128],
                        s_gb[i][:], start=(i == 0), stop=False))
                pe_i(PE.matmul(ub[0:128, ti * E:(ti + 1) * E],
                               p_nmu[pbu][0:1, ti * 128:(ti + 1) * 128],
                               s_gcol[:], start=False, stop=True))
            upe_d[b] = ctx.cnt["pe"]

        def D10a(b):
            ub = pb[0] if b % 2 == 0 else pb[3]
            pu = b % 2
            for ti in range(4):
                vop(lambda v, ub=ub, ti=ti, pu=pu: v.max(
                    out=p_t8[pu][:, ti * K8:(ti + 1) * K8],
                    in_=ub[0:128, ti * E:(ti + 1) * E]),
                    *([("pe", upe_d[b])] if ti == 0 else []))
                vop(lambda v, ub=ub, ti=ti, pu=pu: v.tensor_scalar(
                    out=p_M[pu][:, ti * E:(ti + 1) * E],
                    in0=ub[0:128, ti * E:(ti + 1) * E],
                    scalar1=p_t8[pu][:, ti * K8 + 7:ti * K8 + 8],
                    scalar2=None, op0=ALU.is_ge))
                g = 4 * b + ti
                j = g // 8
                r = g % 8
                vop(lambda v, ub=ub, ti=ti, j=j, r=r: v.scalar_tensor_tensor(
                    out=p_ust[:, j * E:(j + 1) * E],
                    in0=ub[0:128, ti * E:(ti + 1) * E],
                    scalar=s_cm[:, r:r + 1], in1=p_ust[:, j * E:(j + 1) * E],
                    op0=ALU.mult, op1=ALU.add))
            mv_d[b] = ctx.cnt["va"]

        def K0a(b):
            pu = b % 2
            pe_w("va", mv_d[b])
            pe_w("va", cntfree_d[b - 1])
            for ti in range(4):
                pe_i(PE.matmul(pb[1][0:1, ti * E:(ti + 1) * E], s_onc_f[:],
                               p_M[pu][:, ti * E:(ti + 1) * E],
                               start=True, stop=True))
            cntpe_d[b] = ctx.cnt["pe"]

        def D20a(b):
            vop(lambda v: v.tensor_copy(out=p_cnt[:], in_=pb[1][0:1, 0:4 * E]),
                ("pe", cntpe_d[b]))
            cntfree_d[b] = ctx.cnt["va"]
            for ti in range(4):
                g = 4 * b + ti
                j = g // 8
                r = g % 8
                vop(lambda v, j=j, r=r: v.scalar_tensor_tensor(
                    out=p_gst[0:1, j * E:(j + 1) * E], in0=s_gof_run[:],
                    scalar=s_cm[0:1, r:r + 1], in1=p_gst[0:1, j * E:(j + 1) * E],
                    op0=ALU.mult, op1=ALU.add))
                vop(lambda v, ti=ti: v.tensor_add(
                    out=s_gof_run[:], in0=s_gof_run[:],
                    in1=p_cnt[0:1, ti * E:(ti + 1) * E]))

        L0a(0)
        for i in range(NB0 + 1):
            if i + 1 < NB0:
                L0a(i + 1)
            if i < NB0:
                C0a(i)
                MU0a(i)
            if i - 1 >= 0:
                D10a(i - 1)
            if i < NB0:
                NMU0a(i)
                U0a(i)
            if i - 1 >= 0:
                K0a(i - 1)
                D20a(i - 1)
        barrier()

        for j in range(NT):
            pj = j % 2
            ctx.wait("sp", "va", ctx.at(f"oc_xo_{j - 2}"))
            sq_dma(f"d{pj}", p_xo[pj][:], d_x_own[j * 128:(j + 1) * 128, :])
            xo_sx = ctx.cnt[f"d{pj}"]
            vop(lambda v: v.bn_stats(out=p_bn[:, 0:6], in_=p_xo[pj][:, 0:512]),
                (f"d{pj}", xo_sx))
            vop(lambda v: v.bn_stats(out=p_bn[:, 6:12], in_=p_xo[pj][:, 512:1024]))
            vop(lambda v: v.bn_aggr(out=p_mv[:],
                                    in_=p_bn[:].rearrange("p (a b) -> p a b", a=2)),
                ("sa", ctx.at("mv_free")))
            mv_v = ctx.cnt["va"]
            aop(lambda s: s.activation(out=p_lnt[:], in_=p_mv[:, 1:2], func=AF.Ln,
                                       bias=EPS), ("va", mv_v))
            aop(lambda s: s.activation(out=p_r[:], in_=p_lnt[:], func=AF.Exp,
                                       scale=-0.5))
            r_sa = ctx.cnt["sa"]
            ctx.mark("mv_free", "sa")
            vop(lambda v: v.tensor_scalar(out=p_xm[:], in0=p_xo[pj][:],
                                          scalar1=p_mv[:, 0:1], scalar2=None,
                                          op0=ALU.subtract))
            vop(lambda v: v.tensor_scalar(out=p_xn[pj][:], in0=p_xm[:],
                                          scalar1=p_r[:], scalar2=None,
                                          op0=ALU.mult),
                ("sa", r_sa), (f"e{pj}", ctx.at(f"xn_st_{j - 2}")))
            xn_v = ctx.cnt["va"]
            ctx.mark(f"oc_xo_{j}", "va")
            ctx.wait("pool", "va", xn_v)
            gp_dma(d_xn[j * 128:(j + 1) * 128, :], p_xn[pj][:], sem=f"e{pj}")
            ctx.mark(f"xn_st_{j}", f"e{pj}")
            for i in range(8):
                pe_w("va", xn_v)
                bank_w("pbh0")
                pe_i(PE.transpose(pbh[0][0:128, 0:128],
                                  p_xn[pj][:, i * 128:(i + 1) * 128], s_id_b[:]))
                tp_pe = ctx.cnt["pe"]
                vop(lambda v, i=i: v.tensor_copy(
                    out=p_fmst[:, i * 128:(i + 1) * 128],
                    in_=pbh[0][0:128, 0:128]),
                    ("pe", tp_pe), ("c0", ctx.at(f"xnfm_st_{j - 1}")))
                bank_m("pbh0")
            fm_v = ctx.cnt["va"]
            ctx.wait("pool", "va", fm_v)
            gp_dma(d_xnfm.rearrange("(c p) t -> p c t", p=128)
                   [:, :, j * 128:(j + 1) * 128],
                   p_fmst[:].rearrange("p (c t) -> p c t", c=8), sem="c0")
            ctx.mark(f"xnfm_st_{j}", "c0")
            uj = p_ust[:, j * E:(j + 1) * E]
            vop(lambda v, uj=uj: v.max(out=p_t8j[:], in_=uj),
                ("sa", ctx.at("t8j_free")))
            vop(lambda v, uj=uj: v.tensor_scalar(out=p_Mj[:], in0=uj,
                                                 scalar1=p_t8j[:, 7:8], scalar2=None,
                                                 op0=ALU.is_ge))
            mj_v = ctx.cnt["va"]
            pe_w("va", mj_v)
            bank_w("ocp2")
            pe_i(PE.matmul(pb[2][0:128, 0:E], s_lt[:], p_Mj[:], start=True, stop=False))
            pe_i(PE.matmul(pb[2][0:128, 0:E], s_onr_f[:],
                           p_gst[0:1, j * E:(j + 1) * E], start=False, stop=True))
            pos_pe = ctx.cnt["pe"]
            vop(lambda v: v.tensor_scalar(out=p_c1[:], in0=pb[2][0:128, 0:E],
                                          scalar1=CAP - 0.5, scalar2=None,
                                          op0=ALU.is_le), ("pe", pos_pe))
            bank_m("ocp2")
            vop(lambda v: v.tensor_mul(out=p_sel[:], in0=p_c1[:], in1=p_Mj[:]))
            vop(lambda v: v.memset(p_Kc[:], 0.0))
            for k in range(K8):
                vop(lambda v, k=k, uj=uj: v.tensor_scalar(
                    out=p_oh[:], in0=uj, scalar1=p_t8j[:, k:k + 1],
                    scalar2=None, op0=ALU.is_equal))
                vop(lambda v, k=k: v.scalar_tensor_tensor(
                    out=p_v2[:], in0=p_oh[:], scalar=1.0, in1=p_sel[:],
                    op0=ALU.mult, op1=ALU.mult, accum_out=p_kp8[:, k:k + 1]))
                if k > 0:
                    vop(lambda v, k=k: v.scalar_tensor_tensor(
                        out=p_Kc[:], in0=p_oh[:], scalar=float(k), in1=p_Kc[:],
                        op0=ALU.mult, op1=ALU.add))
            vop(lambda v: v.scalar_tensor_tensor(out=p_nm8[:], in0=p_t8j[:, 0:1],
                                                 scalar=-1.0, in1=p_r[:],
                                                 op0=ALU.mult, op1=ALU.mult))
            nm_v = ctx.cnt["va"]
            aop(lambda s: s.activation(out=p_e8[:], in_=p_t8j[:, 0:K8], func=AF.Exp,
                                       bias=p_nm8[:], scale=p_r[:]), ("va", nm_v))
            e8_sa = ctx.cnt["sa"]
            ctx.mark("t8j_free", "sa")
            vop(lambda v: v.reduce_sum(out=p_d8[:], in_=p_e8[:], axis=AXX),
                ("sa", e8_sa))
            vop(lambda v: v.reciprocal(out=p_di[:], in_=p_d8[:]))
            vop(lambda v: v.scalar_tensor_tensor(out=p_e8[:], in0=p_e8[:],
                                                 scalar=p_di[:], in1=p_kp8[:],
                                                 op0=ALU.mult, op1=ALU.mult))
            vop(lambda v, j=j: v.tensor_copy(out=s_w8[:, j * K8:(j + 1) * K8],
                                             in_=p_e8[:]))
            vop(lambda v, j=j: v.tensor_scalar(out=p_tl8[:], in0=s_iota[:],
                                               scalar1=8.0, scalar2=float(j * 1024),
                                               op0=ALU.mult, op1=ALU.add))
            vop(lambda v: v.tensor_scalar(out=p_v2[:], in0=p_Kc[:],
                                          scalar1=p_tl8[:], scalar2=None,
                                          op0=ALU.add))
            vop(lambda v: v.tensor_scalar(out=p_v2[:], in0=p_v2[:], scalar1=-1.0,
                                          scalar2=PAY, op0=ALU.mult, op1=ALU.add))
            vop(lambda v: v.tensor_mul(out=p_v2[:], in0=p_v2[:], in1=p_sel[:]))
            v2_v = ctx.cnt["va"]
            pe_w("va", v2_v)
            bank_w("ocp4")
            pe_i(PE.transpose(pb[4][0:64, 0:128], p_v2[:], s_id_f[:]))
            vt_pe = ctx.cnt["pe"]
            vop(lambda v, j=j: v.tensor_copy(
                out=p_vm[:, j * 128:(j + 1) * 128], in_=pb[4][0:64, 0:128]),
                ("pe", vt_pe))
            bank_m("ocp4")
        barrier()

        for rd in range(16):
            vop(lambda v, rd=rd: v.max(out=p_a16[:, rd * 8:(rd + 1) * 8],
                                       in_=p_vm[:]))
            vop(lambda v, rd=rd: v.match_replace(
                out=p_vm[:], in_to_replace=p_a16[:, rd * 8:(rd + 1) * 8],
                in_values=p_vm[:], imm_value=0.0))
        x16_v = ctx.cnt["va"]
        pe_w("va", x16_v)
        bank_w("ocp4")
        pe_i(PE.transpose(pb[4][0:128, 0:64], p_a16[:], s_id_f[0:64, 0:64]))
        a16_pe = ctx.cnt["pe"]
        vop(lambda v: v.tensor_copy(out=p_der[:], in_=pb[4][0:128, 0:64]),
            ("pe", a16_pe))
        bank_m("ocp4")
        vop(lambda v: v.tensor_scalar(out=s_vld[:, 0:E], in0=p_der[:], scalar1=0.5,
                                      scalar2=None, op0=ALU.is_gt))
        vop(lambda v: v.tensor_scalar(out=p_der2[:], in0=p_der[:], scalar1=-1.0,
                                      scalar2=PAY, op0=ALU.mult, op1=ALU.add))
        vop(lambda v: v.tensor_mul(out=p_der[:], in0=p_der2[:], in1=s_vld[:, 0:E]))
        vop(lambda v: v.tensor_scalar(out=p_oh[:], in0=s_vld[:, 0:E],
                                      scalar1=-DUMMY_ASG, scalar2=DUMMY_ASG,
                                      op0=ALU.mult, op1=ALU.add))
        vop(lambda v: v.tensor_add(out=p_der[:], in0=p_der[:], in1=p_oh[:]))
        vop(lambda v: v.tensor_copy(out=s_asg[:, 0:E], in_=p_der[:]))
        vop(lambda v: v.tensor_mul(out=p_der2[:], in0=p_der2[:], in1=s_vld[:, 0:E]))
        vop(lambda v: v.tensor_copy(out=s_tokid[:, 0:E], in_=p_der2[:]))
        vop(lambda v: v.tensor_scalar(out=s_tokid[:, 0:E], in0=s_tokid[:, 0:E],
                                      scalar1=3, scalar2=None,
                                      op0=ALU.arith_shift_right))
        barrier()

        load_wmat(d_wq, A_wb, A_cvt)
        wq_v = ctx.cnt["va"]
        for b in range(NB):
            ctx.wait("sp", "pe", ctx.at(f"qblk_{b - 1}"))
            ctx.wait("sp", "va", wq_v)
            for i in range(8):
                sp_dma(A_q[i][:], d_xnfm[i * 128:(i + 1) * 128,
                                         b * 512:(b + 1) * 512])
            xnb_sx = ctx.cnt["sx"]
            for oc in range(8):
                bank = pb[oc % 2]
                pe_w("va", wq_v)
                pe_w("sx", xnb_sx)
                bank_w(f"q{oc % 2}")
                for i in range(8):
                    pe_i(PE.matmul(bank[0:128, 0:512],
                                   A_wb[i][:, oc * 128:(oc + 1) * 128],
                                   A_q[i][:], start=(i == 0), stop=(i == 7)))
                q_pe = ctx.cnt["pe"]
                vop(lambda v, oc=oc, bank=bank: v.tensor_scalar(
                    out=A_P[oc % 2][:, 0:512], in0=bank[0:128, 0:512],
                    scalar1=s_bq[:, oc:oc + 1], scalar2=None, op0=ALU.add),
                    ("pe", q_pe), (f"e{oc % 2}", ctx.at(f"q_st_{b * 8 + oc - 2}")))
                qe_v = ctx.cnt["va"]
                bank_m(f"q{oc % 2}")
                ctx.wait("pool", "va", qe_v)
                ctx.inc(G.dma_start(
                    out=d_qs[(b * 8 + oc) * 128:(b * 8 + oc + 1) * 128, :],
                    in_=A_P[oc % 2][:, 0:512]), f"e{oc % 2}", 16)
                ctx.mark(f"q_st_{b * 8 + oc}", f"e{oc % 2}")
            ctx.mark(f"qblk_{b}", "pe")
        barrier()

        load_wmat(d_wk, A_wb, A_cvt)
        wk_v = ctx.cnt["va"]
        for kt in range(16):
            ctx.wait("sp", "va", max(ctx.at("acvt_free"), wk_v))
            sp_dma(A_cvt[:], d_k[kt * 128:(kt + 1) * 128, :])
            k_sx = ctx.cnt["sx"]
            vop(lambda v: v.tensor_copy(out=A_kvb[:], in_=A_cvt[:]),
                ("sx", k_sx), ("pe", ctx.at("kvb_readers")))
            ctx.mark("acvt_free", "va")
            kb_v = ctx.cnt["va"]
            for i in range(8):
                pe_w("va", kb_v)
                bank_w("pbh0")
                pe_i(PE.transpose(pbh[0][0:128, 0:128],
                                  A_kvb[:, i * 128:(i + 1) * 128], s_id_b[:]))
                vop(lambda v, i=i, kt=kt: v.tensor_copy(
                    out=A_kvfm[i][:, kt * 128:(kt + 1) * 128],
                    in_=pbh[0][0:128, 0:128]), ("pe", ctx.cnt["pe"]))
                bank_m("pbh0")
            ctx.mark("kvb_readers", "pe")
        kfm_v = ctx.cnt["va"]
        for oc in range(8):
            for sl in range(4):
                bank = pb[(oc * 4 + sl) % 2]
                pe_w("va", kfm_v)
                bank_w(f"kh{(oc * 4 + sl) % 2}")
                for i in range(8):
                    pe_i(PE.matmul(bank[0:128, 0:512],
                                   A_wb[i][:, oc * 128:(oc + 1) * 128],
                                   A_kvfm[i][:, sl * 512:(sl + 1) * 512],
                                   start=(i == 0), stop=(i == 7)))
                kh_pe = ctx.cnt["pe"]
                vop(lambda v, oc=oc, sl=sl, bank=bank: v.tensor_scalar(
                    out=A_kh[oc][:, sl * 512:(sl + 1) * 512], in0=bank[0:128, 0:512],
                    scalar1=s_bk[:, oc:oc + 1], scalar2=None, op0=ALU.add),
                    ("pe", kh_pe))
                bank_m(f"kh{(oc * 4 + sl) % 2}")
        barrier()
        load_wmat(d_wv, A_wb, A_cvt)
        wv_v = ctx.cnt["va"]
        for kt in range(16):
            ctx.wait("sp", "va", max(ctx.at("acvt_free"), wv_v))
            sp_dma(A_cvt[:], d_v[kt * 128:(kt + 1) * 128, :])
            v_sx = ctx.cnt["sx"]
            vop(lambda v: v.tensor_copy(out=A_kvb[:], in_=A_cvt[:]),
                ("sx", v_sx), ("pe", ctx.at("kvb_readers")))
            ctx.mark("acvt_free", "va")
            vb_v = ctx.cnt["va"]
            for i in range(8):
                pe_w("va", vb_v)
                bank_w("pbh0")
                pe_i(PE.transpose(pbh[0][0:128, 0:128],
                                  A_kvb[:, i * 128:(i + 1) * 128], s_id_b[:]))
                vop(lambda v, i=i, kt=kt: v.tensor_copy(
                    out=A_kvfm[i][:, kt * 128:(kt + 1) * 128],
                    in_=pbh[0][0:128, 0:128]), ("pe", ctx.cnt["pe"]))
                bank_m("pbh0")
            ctx.mark("kvb_readers", "pe")
        vfm_v = ctx.cnt["va"]
        for kt in range(16):
            vop(lambda v, kt=kt: v.memset(
                A_vhx[:, kt * NH * 65 + 64:(kt + 1) * NH * 65:65], 1.0))
        ones_v = ctx.cnt["va"]
        for kt in range(16):
            for ns in range(2):
                bank = pb[(kt * 2 + ns) % 2]
                pe_w("va", max(vfm_v, ones_v, wv_v))
                bank_w(f"vh{(kt * 2 + ns) % 2}")
                for i in range(8):
                    pe_i(PE.matmul(bank[0:128, 0:512],
                                   A_kvfm[i][:, kt * 128:(kt + 1) * 128],
                                   A_wb[i][:, ns * 512:(ns + 1) * 512],
                                   start=(i == 0), stop=False))
                pe_i(PE.matmul(bank[0:128, 0:512], s_onr_b[:],
                               s_bv_b[0:1, ns * 512:(ns + 1) * 512],
                               start=False, stop=True))
                vh_pe = ctx.cnt["pe"]
                for hh in range(8):
                    h = ns * 8 + hh
                    vop(lambda v, kt=kt, h=h, hh=hh, bank=bank: v.tensor_copy(
                        out=A_vhx[:, (kt * NH + h) * 65:(kt * NH + h) * 65 + 64],
                        in_=bank[0:128, hh * 64:(hh + 1) * 64]), ("pe", vh_pe))
                bank_m(f"vh{(kt * 2 + ns) % 2}")
        barrier()
        load_wmat(d_wo, A_wb, A_wcvt)
        wo_v = ctx.cnt["va"]
        for b in range(NB):
            ctx.wait("sp", "pe", ctx.at(f"attn_blk_{b - 1}"))
            for oc in range(8):
                sp_dma(A_q[oc][:], d_qs[(b * 8 + oc) * 128:(b * 8 + oc + 1) * 128, :])
            q_sx = ctx.cnt["sx"]
            for hp in range(NH // 2):
                ha, hb = 2 * hp, 2 * hp + 1
                ch = hp
                if hp == 0:
                    bank_w("op0")
                    bank_w("op1")
                    bank_w("S1v")
                    bank_w("ocp2v")
                for k2 in range(8):
                    kt0, kt1 = 2 * k2, 2 * k2 + 1
                    pe_w("sx", q_sx)
                    bank_w("S0", "sa")
                    pe_i(PE.matmul(pbA[:, 0:512],
                                   A_kh[ch][0:64, kt0 * 128:(kt0 + 1) * 128],
                                   A_q[ch][0:64, 0:512], start=True, stop=True))
                    pe_i(PE.matmul(pbA[:, 512:1024],
                                   A_kh[ch][0:64, kt1 * 128:(kt1 + 1) * 128],
                                   A_q[ch][0:64, 0:512], start=True, stop=True))
                    sA = ctx.cnt["pe"]
                    bank_w("S1", "sa")
                    pe_i(PE.matmul(pbB[:, 0:512],
                                   A_kh[ch][64:128, kt0 * 128:(kt0 + 1) * 128],
                                   A_q[ch][64:128, 0:512], start=True, stop=True))
                    pe_i(PE.matmul(pbB[:, 512:1024],
                                   A_kh[ch][64:128, kt1 * 128:(kt1 + 1) * 128],
                                   A_q[ch][64:128, 0:512], start=True, stop=True))
                    sB = ctx.cnt["pe"]
                    aop(lambda s: s.activation(
                        out=A_P[0][:], in_=pbA[:, 0:1024], func=AF.Exp,
                        bias=-12.0, scale=0.125),
                        ("pe", sA), ("pe", ctx.at("aVa")))
                    bank_m("S0", "sa")
                    expA = ctx.cnt["sa"]
                    aop(lambda s: s.activation(
                        out=A_P[1][:], in_=pbB[:, 0:1024], func=AF.Exp,
                        bias=-12.0, scale=0.125),
                        ("pe", sB), ("pe", ctx.at("aVb")))
                    bank_m("S1", "sa")
                    expB = ctx.cnt["sa"]
                    pe_w("sa", expA)
                    if k2 == 0:
                        bank_w("oaccA")
                    pe_i(PE.matmul(pbC[0:65, 0:512],
                                   A_vhx[:, (kt0 * NH + ha) * 65:(kt0 * NH + ha + 1) * 65],
                                   A_P[0][:, 0:512], start=(k2 == 0), stop=False))
                    pe_i(PE.matmul(pbC[0:65, 0:512],
                                   A_vhx[:, (kt1 * NH + ha) * 65:(kt1 * NH + ha + 1) * 65],
                                   A_P[0][:, 512:1024], start=False, stop=(k2 == 7)))
                    ctx.mark("aVa", "pe")
                    pe_w("sa", expB)
                    if k2 == 0:
                        bank_w("oaccB")
                    pe_i(PE.matmul(pbD[0:65, 0:512],
                                   A_vhx[:, (kt0 * NH + hb) * 65:(kt0 * NH + hb + 1) * 65],
                                   A_P[1][:, 0:512], start=(k2 == 0), stop=False))
                    pe_i(PE.matmul(pbD[0:65, 0:512],
                                   A_vhx[:, (kt1 * NH + hb) * 65:(kt1 * NH + hb + 1) * 65],
                                   A_P[1][:, 512:1024], start=False, stop=(k2 == 7)))
                    ctx.mark("aVb", "pe")
                av_pe = ctx.cnt["pe"]
                vop(lambda v, ch=ch: v.tensor_copy(
                    out=A_ofm[ch][0:64, 0:512], in_=pbC[0:64, 0:512]),
                    ("pe", av_pe))
                vop(lambda v: v.tensor_copy(out=A_drow[:], in_=pbC[64:65, 0:512]),
                    ("gd", ctx.at("drow_free")))
                dr_v = ctx.cnt["va"]
                bank_m("oaccA")
                ctx.wait("pool", "va", dr_v)
                ctx.wait("pool", "pe", ctx.at(f"dstack_rd_{b - 1}"))
                ctx.inc(G.dma_start(out=A_dstack[ha:ha + 1, :], in_=A_drow[:]),
                        "gd", 16)
                ctx.mark("drow_free", "gd")
                vop(lambda v, ch=ch: v.tensor_copy(
                    out=A_ofm[ch][64:128, 0:512], in_=pbD[0:64, 0:512]))
                vop(lambda v: v.tensor_copy(out=A_drow[:], in_=pbD[64:65, 0:512]),
                    ("gd", ctx.at("drow_free")))
                dr_v = ctx.cnt["va"]
                bank_m("oaccB")
                ctx.wait("pool", "va", dr_v)
                ctx.inc(G.dma_start(out=A_dstack[hb:hb + 1, :], in_=A_drow[:]),
                        "gd", 16)
                ctx.mark("drow_free", "gd")
            dstack_gd = ctx.cnt["gd"]
            for tt in range(4):
                pe_w("gd", dstack_gd)
                bank_w("dD")
                bank_w("S1", "sa")
                bank_w("S1v")
                pe_i(PE.transpose(pbB[0:128, 0:16],
                                  A_dstack[0:16, tt * 128:(tt + 1) * 128],
                                  s_id_f[0:16, 0:16]))
                dt_pe = ctx.cnt["pe"]
                vop(lambda v: v.reciprocal(out=A_dinv[:], in_=pbB[0:128, 0:16]),
                    ("pe", dt_pe))
                bank_m("dD")
                bank_m("S1v")
                di_v = ctx.cnt["va"]
                for h in range(NH):
                    ch = h // 2
                    r0 = (h % 2) * 64
                    pe_w("va", di_v)
                    bank_w("pbh0")
                    pe_i(PE.transpose(pbh[0][0:128, 0:64],
                                      A_ofm[ch][r0:r0 + 64, tt * 128:(tt + 1) * 128],
                                      s_id_b[r0:r0 + 64, r0:r0 + 64]))
                    ot_pe = ctx.cnt["pe"]
                    vop(lambda v, tt=tt, h=h: v.tensor_scalar(
                        out=A_otm[tt][:, h * 64:(h + 1) * 64],
                        in0=pbh[0][0:128, 0:64],
                        scalar1=A_dinv[:, h:h + 1], scalar2=None, op0=ALU.mult),
                        ("pe", ot_pe))
                    bank_m("pbh0")
            ctx.mark(f"dstack_rd_{b}", "pe")
            otm_v = ctx.cnt["va"]
            for tt in range(4):
                for i in range(8):
                    pe_w("va", otm_v)
                    bank_w("pbh1")
                    pe_i(PE.transpose(pbh[1][0:128, 0:128],
                                      A_otm[tt][:, i * 128:(i + 1) * 128], s_id_b[:]))
                    vop(lambda v, i=i, tt=tt: v.tensor_copy(
                        out=A_ofm[i][:, tt * 128:(tt + 1) * 128],
                        in_=pbh[1][0:128, 0:128]), ("pe", ctx.cnt["pe"]))
                    bank_m("pbh1")
            onfm_v = ctx.cnt["va"]
            for tt in range(4):
                for ns in range(2):
                    bank = pbA[:, ns * 512:(ns + 1) * 512]
                    pe_w("va", max(onfm_v, wo_v))
                    bank_w(f"op{ns}")
                    if tt == 0 and ns == 0:
                        bank_w("S0", "sa")
                    for i in range(8):
                        pe_i(PE.matmul(bank[0:128, 0:512],
                                       A_ofm[i][:, tt * 128:(tt + 1) * 128],
                                       A_wb[i][:, ns * 512:(ns + 1) * 512],
                                       start=(i == 0), stop=False))
                    pe_i(PE.matmul(bank[0:128, 0:512], s_onr_b[:],
                                   s_bo_b[0:1, ns * 512:(ns + 1) * 512],
                                   start=False, stop=True))
                    o_pe = ctx.cnt["pe"]
                    vop(lambda v, ns=ns, bank=bank: v.tensor_copy(
                        out=A_stm[:, ns * 512:(ns + 1) * 512], in_=bank[0:128, 0:512]),
                        ("pe", o_pe), ("gs", ctx.at("stm_free")))
                    bank_m(f"op{ns}")
                sev = ctx.cnt["va"]
                ctx.wait("pool", "va", sev)
                gp_dma(d_s[(b * 4 + tt) * 128:(b * 4 + tt + 1) * 128, :], A_stm[:])
                ctx.mark("stm_free", "gs")
            ctx.mark(f"attn_blk_{b}", "pe")
        barrier()

        for i in range(8):
            for q4 in range(4):
                ctx.wait("sp", "va", ctx.at("bstage_free"))
                sq_dma("d2", B_cvt[:], d_sguw[i * 128:(i + 1) * 128,
                                              q4 * 1024:(q4 + 1) * 1024])
                wsx = ctx.cnt["d2"]
                vop(lambda v, i=i, q4=q4: v.tensor_copy(
                    out=B_wgu[i][:, q4 * 1024:(q4 + 1) * 1024], in_=B_cvt[:]),
                    ("d2", wsx))
                ctx.mark("bstage_free", "va")
        for i in range(16):
            ctx.wait("sp", "va", ctx.at("bstage_free"))
            sq_dma("d2", B_cvt[:], d_sdw[i * 128:(i + 1) * 128, :])
            wsx = ctx.cnt["d2"]
            vop(lambda v, i=i: v.tensor_copy(out=B_wd[i][:], in_=B_cvt[:]),
                ("d2", wsx))
            ctx.mark("bstage_free", "va")
        ctx.wait("sp", "va", ctx.at("bstage_free"))
        sq_dma("d2", B_cvt[:, 0:8].rearrange("p (c o) -> p c o", o=1),
               d_sgw.rearrange("(c p) o -> p c o", p=128),
               allow_slow_non_contiguous=True)
        sg_sx = ctx.cnt["d2"]
        vop(lambda v: v.tensor_copy(out=B_sgwc[:], in_=B_cvt[:, 0:8]), ("d2", sg_sx))
        bw_v = ctx.cnt["va"]
        for b in range(NB):
            ctx.wait("sp", "pe", ctx.at(f"bblk_{b - 1}"))
            for i in range(8):
                sp_dma(B_xn[i][:], d_xnfm[i * 128:(i + 1) * 128,
                                          b * 512:(b + 1) * 512])
            xnb_sx = ctx.cnt["sx"]
            for jj in range(16):
                bk0 = pb[(jj % 2) * 2]
                bk1 = pb[(jj % 2) * 2 + 1]
                pe_w("va", bw_v)
                pe_w("sx", xnb_sx)
                bank_w(f"bg{jj % 2}", "sa")
                bank_w(f"bu{jj % 2}")
                for i in range(8):
                    pe_i(PE.matmul(bk0[0:128, 0:512],
                                   B_wgu[i][:, jj * 128:(jj + 1) * 128],
                                   B_xn[i][:], start=(i == 0), stop=(i == 7)))
                for i in range(8):
                    pe_i(PE.matmul(bk1[0:128, 0:512],
                                   B_wgu[i][:, 2048 + jj * 128:2048 + (jj + 1) * 128],
                                   B_xn[i][:], start=(i == 0), stop=(i == 7)))
                gu_pe = ctx.cnt["pe"]
                aop(lambda s, bk0=bk0: s.activation(out=B_sh[:], in_=bk0[0:128, 0:512],
                                                    func=AF.Silu),
                    ("pe", gu_pe), ("va", ctx.at("bsh_free")))
                sh_sa = ctx.cnt["sa"]
                bank_m(f"bg{jj % 2}", "sa")
                vop(lambda v, jj=jj, bk1=bk1: v.tensor_mul(
                    out=B_act[jj][:], in0=B_sh[:], in1=bk1[0:128, 0:512]),
                    ("sa", sh_sa), ("pe", gu_pe))
                bank_m(f"bu{jj % 2}")
                ctx.mark("bsh_free", "va")
            act_v = ctx.cnt["va"]
            pe_w("va", act_v)
            bank_w("sgz")
            for i in range(8):
                pe_i(PE.matmul(pb[4][0:1, 0:512], B_sgwc[:, i:i + 1],
                               B_xn[i][:], start=(i == 0), stop=(i == 7)))
            sgz_pe = ctx.cnt["pe"]
            vop(lambda v: v.tensor_copy(out=B_sgr[:], in_=pb[4][0:1, 0:512]),
                ("pe", sgz_pe), ("e0", ctx.at(f"sg_st_{b - 1}")))
            bank_m("sgz")
            sgr_v = ctx.cnt["va"]
            ctx.wait("pool", "va", sgr_v)
            gp_dma(d_sg[b:b + 1, :], B_sgr[:], sem="e0")
            ctx.mark(f"sg_st_{b}", "e0")
            for tt in range(4):
                for ns in range(2):
                    bank = pb[4 + ns]
                    pe_w("va", act_v)
                    if ns == 0:
                        bank_w("sgz")
                    bank_w(f"dn{ns}")
                    for i in range(16):
                        pe_i(PE.matmul(bank[0:128, 0:512],
                                       B_act[i][:, tt * 128:(tt + 1) * 128],
                                       B_wd[i][:, ns * 512:(ns + 1) * 512],
                                       start=(i == 0), stop=(i == 15)))
                    y_pe = ctx.cnt["pe"]
                    vop(lambda v, ns=ns, bank=bank: v.tensor_copy(
                        out=B_y[:, ns * 512:(ns + 1) * 512], in_=bank[0:128, 0:512]),
                        ("pe", y_pe), ("gs", ctx.at("by_free")))
                    bank_m(f"dn{ns}")
                yv = ctx.cnt["va"]
                ctx.wait("pool", "va", yv)
                gp_dma(d_ysh[(b * 4 + tt) * 128:(b * 4 + tt + 1) * 128, :], B_y[:])
                ctx.mark("by_free", "gs")
            ctx.mark(f"bblk_{b}", "pe")
        barrier()

        sp_dma(C_sgs[:], d_sg.rearrange("(o a) b -> o (a b)", o=1))
        sgl_sx = ctx.cnt["sx"]
        aop(lambda s: s.activation(out=C_sgs[:], in_=C_sgs[:], func=AF.Sigmoid),
            ("sx", sgl_sx))
        sgs_sa = ctx.cnt["sa"]
        for j in range(NT):
            pe_w("sa", sgs_sa)
            bank_w("sgt")
            pe_i(PE.transpose(pb[5][0:128, 0:1], C_sgs[0:1, j * 128:(j + 1) * 128],
                              s_id_f[0:1, 0:1]))
            vop(lambda v, j=j: v.tensor_copy(out=s_sgt[:, j:j + 1],
                                             in_=pb[5][0:128, 0:1]),
                ("pe", ctx.cnt["pe"]))
            bank_m("sgt")
        barrier()
        bc_reg = G.to_reg(YBROWS - 1)
        ctx.wait("pool", "g2", ctx.at("ybzero"))
        gu_pe_of = {}

        def stage_A(e):
            p = e % 2
            wp = e % 3
            ctx.wait("pool", "va", ctx.at(f"ebmask_{e - 2}"))
            ctx.wait("pool", "pe", ctx.at(f"ebtpe_{e - 2}"))
            ctx.inc(G.indirect_dma_start(
                out=C_eb[p][:], out_offset=None, in_=d_xn[:, :],
                in_offset=IndirectOffsetOnAxis(ap=s_tokid[:, e:e + 1], axis=0)),
                f"e{p}", 16)
            gd_e = ctx.cnt[f"e{p}"]
            vop(lambda v, p=p, e=e: v.tensor_scalar(
                out=C_eb[p][:], in0=C_eb[p][:], scalar1=s_vld[:, e:e + 1],
                scalar2=None, op0=ALU.mult), (f"e{p}", gd_e))
            ctx.mark(f"ebmask_{e}", "va")
            egu_rows = d_egu.rearrange("e d n -> (e d) n")
            edw_rows = d_edw.rearrange("e d n -> (e d) n")
            for i2 in range(4):
                ctx.wait("sp", "va", ctx.at(f"wguf{i2}_va"))
                ctx.wait("sp", "sa", ctx.at(f"wguf{i2}_sa"))
                sq_dma(f"d{i2}", C_wguf[i2][:].rearrange("p (a n) -> p a n", a=2),
                       egu_rows[e * D + i2 * 256:e * D + (i2 + 1) * 256, :]
                       .rearrange("(a p) n -> p a n", p=128))
                wsx = ctx.cnt[f"d{i2}"]
                vop(lambda v, i2=i2, wp=wp: v.tensor_copy(
                    out=C_wgu[wp][2 * i2][:], in_=C_wguf[i2][:, 0:D]),
                    (f"d{i2}", wsx), ("pe", ctx.at(f"exp_gu_{e - 3}")))
                aop(lambda s, i2=i2, wp=wp: s.copy(
                    out=C_wgu[wp][2 * i2 + 1][:], in_=C_wguf[i2][:, D:2 * D]),
                    (f"d{i2}", wsx), ("pe", ctx.at(f"exp_gu_{e - 3}")))
                ctx.mark(f"wguf{i2}_va", "va")
                ctx.mark(f"wguf{i2}_sa", "sa")
            for i2 in range(2):
                sp_ = i2 % 2
                ctx.wait("pool", "va", ctx.at(f"wdf{sp_}_va"))
                ctx.wait("pool", "sa", ctx.at(f"wdf{sp_}_sa"))
                ctx.inc(G.dma_start(
                    out=C_wdf[sp_][:].rearrange("p (a n) -> p a n", a=2),
                    in_=edw_rows[e * 512 + i2 * 256:e * 512 + (i2 + 1) * 256, :]
                    .rearrange("(a p) n -> p a n", p=128)), f"c{2 + sp_}", 16)
                wgd = ctx.cnt[f"c{2 + sp_}"]
                vop(lambda v, sp_=sp_, i2=i2, p=p: v.tensor_copy(
                    out=C_wd[p][2 * i2][:], in_=C_wdf[sp_][:, 0:D]),
                    (f"c{2 + sp_}", wgd), ("pe", ctx.at(f"exp_dn_{e - 2}")))
                aop(lambda s, sp_=sp_, i2=i2, p=p: s.copy(
                    out=C_wd[p][2 * i2 + 1][:], in_=C_wdf[sp_][:, D:2 * D]),
                    (f"c{2 + sp_}", wgd), ("pe", ctx.at(f"exp_dn_{e - 2}")))
                ctx.mark(f"wdf{sp_}_va", "va")
                ctx.mark(f"wdf{sp_}_sa", "sa")
            cvt_v = ctx.cnt["va"]; cvt_a = ctx.cnt["sa"]
            for i in range(8):
                pe_w("va", ctx.at(f"ebmask_{e}"))
                pe_w("va", ctx.at(f"guused_{e - 2}"))
                bank_w("pbh0")
                pe_i(PE.transpose(pbh[0][0:128, 0:128],
                                  C_eb[p][:, i * 128:(i + 1) * 128], s_id_b[:]))
                vop(lambda v, i=i, p=p: v.tensor_copy(
                    out=C_ebt[p][:, i * 128:(i + 1) * 128],
                    in_=pbh[0][0:128, 0:128]),
                    ("pe", ctx.cnt["pe"]))
                bank_m("pbh0")
            ebt_v = ctx.cnt["va"]
            ctx.mark(f"ebtpe_{e}", "pe")
            gub = pbA if p == 0 else pbB
            pe_w("va", max(ebt_v, cvt_v))
            pe_w("sa", cvt_a)
            bank_w(f"cg{p}", "sa")
            bank_w(f"cu{p}")
            for ns in range(2):
                for i in range(8):
                    pe_i(PE.matmul(gub[:, ns * 512:(ns + 1) * 512],
                                   C_ebt[p][:, i * 128:(i + 1) * 128],
                                   C_wgu[wp][i][:, ns * 512:(ns + 1) * 512],
                                   start=(i == 0), stop=(i == 7)))
            gu_pe_of[e] = ctx.cnt["pe"]
            ctx.mark(f"exp_gu_{e}", "pe")

        def stage_B(e):
            p = e % 2
            wp = e % 3
            gub = pbA if p == 0 else pbB
            aop(lambda s, gub=gub, p=p: s.activation(
                out=C_sh[p][:], in_=gub[0:128, 0:512], func=AF.Silu),
                ("pe", gu_pe_of[e]), ("va", ctx.at(f"cshfree{p}")))
            sh_sa = ctx.cnt["sa"]
            bank_m(f"cg{p}", "sa")
            vop(lambda v, gub=gub, p=p: v.tensor_mul(
                out=C_act[p][:], in0=C_sh[p][:], in1=gub[0:128, 512:1024]),
                ("sa", sh_sa))
            bank_m(f"cu{p}")
            ctx.mark(f"cshfree{p}", "va")
            cact_v = ctx.cnt["va"]
            ctx.mark(f"guused_{e}", "va")
            for i in range(4):
                pe_w("va", cact_v)
                bank_w("pbh1")
                pe_i(PE.transpose(pbh[1][0:128, 0:128],
                                  C_act[p][:, i * 128:(i + 1) * 128], s_id_b[:]))
                vop(lambda v, i=i, p=p: v.tensor_copy(
                    out=C_actT[p][:, i * 128:(i + 1) * 128],
                    in_=pbh[1][0:128, 0:128]),
                    ("pe", ctx.cnt["pe"]))
                bank_m("pbh1")
            at_v = ctx.cnt["va"]
            pe_w("va", at_v)
            bank_w("cd0")
            bank_w("cd1")
            for ns in range(2):
                for i in range(4):
                    pe_i(PE.matmul(pb[4 + ns][0:128, 0:512],
                                   C_actT[p][:, i * 128:(i + 1) * 128],
                                   C_wd[p][i][:, ns * 512:(ns + 1) * 512],
                                   start=(i == 0), stop=(i == 3)))
            dn_pe = ctx.cnt["pe"]
            ctx.mark(f"exp_dn_{e}", "pe")
            for ns in range(2):
                vop(lambda v, ns=ns, p=p: v.tensor_copy(
                    out=C_ebo[p][:, ns * 512:(ns + 1) * 512],
                    in_=pb[4 + ns][0:128, 0:512]),
                    ("pe", dn_pe), (f"c{p}", ctx.at(f"ebo_st_{e - 2}")))
                bank_m(f"cd{ns}")
            ebo_v = ctx.cnt["va"]
            ctx.wait("pool", "va", ebo_v)
            ctx.inc(G.indirect_dma_start(
                out=d_yb[:, :],
                out_offset=IndirectOffsetOnAxis(ap=s_asg[:, e:e + 1], axis=0),
                in_=C_ebo[p][:], in_offset=None,
                bounds_check=bc_reg, oob_is_err=False), f"c{p}", 16)
            ctx.mark(f"ebo_st_{e}", f"c{p}")

        for e in range(E + 1):
            if e < E:
                stage_A(e)
            if e > 0:
                stage_B(e - 1)
        barrier()
        yb3 = d_yb.rearrange("(t k) d -> t k d", k=K8)
        for j in range(NT):
            p = j % 2
            ctx.wait("sp", "va", ctx.at(f"comb_{j - 2}"))
            sq_dma(f"d{p}", C_x[p][:], d_x_own[j * 128:(j + 1) * 128, :])
            sq_dma(f"d{p}", C_sv[p][:], d_s[j * 128:(j + 1) * 128, :])
            sq_dma(f"d{p}", C_ysh[p][:], d_ysh[j * 128:(j + 1) * 128, :])
            ld_sx = ctx.cnt[f"d{p}"]
            vop(lambda v: v.memset(C_y[:], 0.0))
            for k in range(K8):
                gsem = ["d2", "d3", "g3", "sx"][k % 4]
                ctx.wait("sp", "va", ctx.at(f"gaccP{k % 4}"))
                sq_dma(gsem, C_g[k % 4][:], yb3[j * 128:(j + 1) * 128, k, :])
                gk_sx = ctx.cnt[gsem]
                vop(lambda v, j=j, k=k: v.scalar_tensor_tensor(
                    out=C_y[:], in0=C_g[k % 4][:],
                    scalar=s_w8[:, j * K8 + k:j * K8 + k + 1], in1=C_y[:],
                    op0=ALU.mult, op1=ALU.add), (gsem, gk_sx))
                ctx.mark(f"gaccP{k % 4}", "va")
            vop(lambda v, j=j, p=p: v.scalar_tensor_tensor(
                out=C_ffn[p][:], in0=C_ysh[p][:], scalar=s_sgt[:, j:j + 1],
                in1=C_y[:], op0=ALU.mult, op1=ALU.add),
                (f"d{p}", ld_sx), ("sa", ctx.at(f"ffnfree{p}")))
            ffn_v = ctx.cnt["va"]
            aop(lambda s, p=p: s.activation(out=C_gg[p][:], in_=C_ffn[p][:],
                                            func=AF.Silu), ("va", ffn_v))
            gg_sa = ctx.cnt["sa"]
            ctx.mark(f"ffnfree{p}", "sa")
            vop(lambda v, p=p: v.tensor_mul(out=C_gg[p][:], in0=C_gg[p][:],
                                            in1=C_sv[p][:]),
                ("sa", gg_sa))
            osem = "so" if p == 0 else "s1"
            vop(lambda v, p=p: v.tensor_add(out=C_out[p][:], in0=C_gg[p][:],
                                            in1=C_x[p][:]),
                (osem, ctx.at(f"out_st_{j - 2}")))
            o_v = ctx.cnt["va"]
            ctx.mark(f"comb_{j}", "va")
            ctx.wait("pool", "va", o_v)
            ctx.inc(G.dma_start(out=d_out[j * 128:(j + 1) * 128, :],
                                in_=C_out[p][:]), osem, 16)
            ctx.mark(f"out_st_{j}", osem)
        for en in ("sp", "pe", "dve", "act", "pool"):
            ctx.wait(en, "so", NT * 8)
            ctx.wait(en, "s1", NT * 8)

    engine_map = {"sp": "sync", "pe": "tensor", "act": "scalar",
                  "dve": "vector", "pool": "gpsimd"}
    counts = []
    with nc.Block() as block:
        for name, attr in engine_map.items():
            def make_body(nm):
                def body(eng):
                    engines = {k: (_DryEng() if k != nm else eng) for k in engine_map}
                    ctx = Ctx(sems, engines, nm)
                    emit_all(ctx)
                    counts.append(dict(ctx.cnt))
                return body
            getattr(block, attr)(make_body(name))
    assert all(c == counts[0] for c in counts[1:]), "pass count divergence"
    return nc


_NC_CACHE = None


def kernel(**inputs):
    global _NC_CACHE
    x = np.ascontiguousarray(np.asarray(inputs["L0_attn_q"], np.float32).reshape(T, D))
    x_fm = np.ascontiguousarray(
        x.T.reshape(8, 128, T // 512, 512).transpose(2, 1, 0, 3)
        .reshape((T // 512) * 128, 8 * 512))
    in_common = dict(
        x_fm=x_fm,
        kv_k=np.ascontiguousarray(np.asarray(inputs["L0_attn_k"], np.float32)[0]),
        kv_v=np.ascontiguousarray(np.asarray(inputs["L0_attn_v"], np.float32)[0]),
        w_q=np.asarray(inputs["L0_attn_w_q"], np.float32),
        w_k=np.asarray(inputs["L0_attn_w_k"], np.float32),
        w_v=np.asarray(inputs["L0_attn_w_v"], np.float32),
        w_o=np.asarray(inputs["L0_attn_w_o"], np.float32),
        b_q=np.asarray(inputs["L0_attn_b_q"], np.float32).reshape(1, D),
        b_k=np.asarray(inputs["L0_attn_b_k"], np.float32).reshape(1, D),
        b_v=np.asarray(inputs["L0_attn_b_v"], np.float32).reshape(1, D),
        b_o=np.asarray(inputs["L0_attn_b_o"], np.float32).reshape(1, D),
        gate_w=np.asarray(inputs["L0_ffn_gate_w"], np.float32),
        sgw=np.asarray(inputs["L0_ffn_shared_gate_w"], np.float32),
        sguw=np.asarray(inputs["L0_ffn_shared_gate_up_w"], np.float32),
        sdw=np.asarray(inputs["L0_ffn_shared_down_w"], np.float32),
        egu=np.asarray(inputs["L0_ffn_expert_gate_up_w"], np.float32),
        edw=np.asarray(inputs["L0_ffn_expert_down_w"], np.float32),
    )
    xt = x.reshape(GT, 128, D)
    if _NC_CACHE is None:
        _NC_CACHE = build_nc()
    nc = _NC_CACHE
    in_maps = []
    for c in range(NCORE):
        im = dict(in_common)
        im["x_own"] = np.ascontiguousarray(xt[c::NCORE].reshape(TC, D))
        cm = np.zeros((128, 8), np.float32)
        cm[:, c] = 1.0
        im["cmask"] = cm
        in_maps.append(im)
    res = run_bass_kernel_spmd(nc, in_maps, core_ids=list(range(NCORE)))
    full = np.empty((GT, 128, D), np.float32)
    for c in range(NCORE):
        full[c::NCORE] = np.asarray(res.results[c]["out"]).reshape(NT, 128, D)
    return full.reshape(8, 4096, D)
